# version 01
# baseline (speedup 1.0000x reference)
import numpy as np
import ml_dtypes
from contextlib import ExitStack
import concourse.bass as bass
import concourse.mybir as mybir
from concourse.bass_utils import run_bass_kernel_spmd

F32 = mybir.dt.float32
BF16 = mybir.dt.bfloat16
AF = mybir.ActivationFunctionType
ALU = mybir.AluOpType
AX = mybir.AxisListType
BFNP = ml_dtypes.bfloat16

D = 1024
T = 4096
NCORES = 8
HID = 4096
NH, HD = 16, 64
NE = 8
CAP = 10240
MOE_EPS = 1.1920929e-07

ENGS = ("sync", "tensor", "scalar", "vector", "gpsimd")


class Sem:
    __slots__ = ("h", "v")

    def __init__(self, h):
        self.h = h
        self.v = 0


class K:

    def __init__(self, nc, active, eng):
        self.nc = nc
        self.active = active
        self.eng = eng

    def w(self, engname, sem, val):
        if val > 0 and self.active == engname:
            self.eng.wait_ge(sem.h, val)

    def dma(self, engname, dst, src, sem, **kw):
        sem.v += 16
        if self.active == engname:
            self.eng.dma_start(dst, src, **kw).then_inc(sem.h, 16)
        return sem.v

    def mm(self, out, lhsT, rhs, start, stop, sem=None, tp=None):
        if sem is not None:
            sem.v += 1
        if self.active == "tensor":
            kw = {}
            if tp is not None:
                kw["tile_position"] = tp
            i = self.eng.matmul(out, lhsT, rhs, start=start, stop=stop, **kw)
            if sem is not None:
                i.then_inc(sem.h, 1)

    def tr(self, out, in_, ident, sem=None):
        if sem is not None:
            sem.v += 1
        if self.active == "tensor":
            i = self.eng.transpose(out, in_, ident)
            if sem is not None:
                i.then_inc(sem.h, 1)

    def act(self, out, in_, func, bias=0.0, scale=1.0, accum=None, sem=None):
        if sem is not None:
            sem.v += 1
        if self.active == "scalar":
            kw = {}
            if accum is not None:
                kw["accum_out"] = accum
            i = self.eng.activation(out, in_, func, bias=bias, scale=scale, **kw)
            if sem is not None:
                i.then_inc(sem.h, 1)

    def vop(self, meth, *args, sem=None, **kw):
        if sem is not None:
            sem.v += 1
        if self.active == "vector":
            i = getattr(self.eng, meth)(*args, **kw)
            if sem is not None:
                i.then_inc(sem.h, 1)

    def gop(self, meth, *args, sem=None, inc=1, **kw):
        if sem is not None:
            sem.v += inc
        if self.active == "gpsimd":
            i = getattr(self.eng, meth)(*args, **kw)
            if sem is not None:
                i.then_inc(sem.h, inc)

    def barrier(self, sbar):
        sbar.v += len(ENGS)
        tgt = sbar.v
        if self.eng is not None:
            self.eng.nop().then_inc(sbar.h, 1)
            self.eng.wait_ge(sbar.h, tgt)


def ap3(dram, r0, nslab, c0, ncols):
    return dram[r0:r0 + nslab * 128, c0:c0 + ncols].rearrange(
        "(n p) c -> p n c", p=128)


def sb3(sbt, col0, nslab, ncols):
    return sbt[:, col0:col0 + nslab * ncols].rearrange(
        "p (n c) -> p n c", c=ncols)



def col128(v):
    return np.ascontiguousarray(v.reshape(-1, 128).T)


def prep_host(inputs):
    f = np.ascontiguousarray
    w = {}
    w["Wq"] = f(inputs["L0_attn_Wq"])
    w["Wk"] = f(inputs["L0_attn_Wk"])
    w["Wv"] = f(inputs["L0_attn_Wv"])
    w["bq8"] = col128(inputs["L0_attn_bq"])
    w["bk8"] = col128(inputs["L0_attn_bk"])
    w["bv"] = f(inputs["L0_attn_bv"].reshape(1, D))
    w["w1T"] = f(inputs["L0_ffn_w1_w"].T)
    w["w2T"] = f(inputs["L0_ffn_w2_w"].T)
    w["h1T"] = f(inputs["L0_ffn_h1_w"].T)
    w["h2T"] = f(inputs["L0_ffn_h2_w"].T)
    w["w3T"] = f(inputs["L0_ffn_w3_w"].T)
    w["s1c"] = col128(inputs["L0_ffn_w1_s"])
    w["s2c"] = col128(inputs["L0_ffn_w2_s"])
    w["sh1c"] = col128(inputs["L0_ffn_h1_s"])
    w["sh2c"] = col128(inputs["L0_ffn_h2_s"])
    w["s3c"] = col128(inputs["L0_ffn_w3_s"])
    w["Wqkv"] = f(inputs["L1_attn_Wqkv"])
    w["Wproj"] = f(inputs["L1_attn_Wproj"])
    w["bprojc"] = col128(inputs["L1_attn_bproj"])
    w["qs_row"] = f(inputs["L1_attn_q_scale"].reshape(1, HD))
    w["ks_row"] = f(inputs["L1_attn_k_scale"].reshape(1, HD))
    w["wg"] = f(inputs["L1_ffn_wg"])
    w["fc1T"] = f(np.transpose(inputs["L1_ffn_fc1_w"], (0, 2, 1)).reshape(NE * D, HID))
    w["fc2"] = f(inputs["L1_ffn_fc2_w"].reshape(NE * HID, D))
    w["fc1b"] = f(np.concatenate([col128(inputs["L1_ffn_fc1_b"][e]) for e in range(NE)], axis=0))
    w["fc2b"] = f(inputs["L1_ffn_fc2_b"])
    oh = np.zeros((NE, NE * 128), dtype=BFNP)
    for e in range(NE):
        oh[e, e * 128:(e + 1) * 128] = 1.0
    w["ohot"] = oh
    w["gammac"] = col128(inputs["L1_ffn_ln_gamma"])
    w["betac"] = col128(inputs["L1_ffn_ln_beta"])

    w["identb"] = np.eye(128, dtype=BFNP)
    w["identf"] = np.eye(128, dtype=np.float32)
    p = np.arange(128)[:, None]
    c = np.arange(384)[None, :]
    w["cmask"] = (p <= c - 128).astype(BFNP)
    w["ltri"] = np.triu(np.ones((128, 128), dtype=np.float32))
    w["eidxm8"] = np.broadcast_to(
        (np.arange(8, dtype=np.float32) - 8.0)[None, :], (128, 8)).copy()

    percore = []
    x = inputs["L0_attn_x"]
    for b in range(NCORES):
        cm = np.zeros((8, 1), dtype=np.float32)
        cm[:b] = 1.0
        percore.append({"xT": f(x[b].T.astype(np.float32)), "coremask": cm})
    return w, percore


WSPECS = [
    ("Wq", [D, D], F32), ("Wk", [D, D], F32), ("Wv", [D, D], F32),
    ("bq8", [128, 8], F32), ("bk8", [128, 8], F32), ("bv", [1, D], F32),
    ("w1T", [D, HID], F32), ("w2T", [D, HID], F32),
    ("h1T", [HID, HID], F32), ("h2T", [HID, HID], F32), ("w3T", [HID, D], F32),
    ("s1c", [128, 32], F32), ("s2c", [128, 32], F32), ("sh1c", [128, 32], F32),
    ("sh2c", [128, 32], F32), ("s3c", [128, 8], F32),
    ("Wqkv", [D, 3 * D], F32), ("Wproj", [D, D], F32), ("bprojc", [128, 8], F32),
    ("qs_row", [1, HD], F32), ("ks_row", [1, HD], F32),
    ("wg", [D, NE], F32), ("fc1T", [NE * D, HID], F32), ("fc2", [NE * HID, D], F32),
    ("fc1b", [NE * 128, 32], F32), ("fc2b", [NE, D], F32),
    ("gammac", [128, 8], F32), ("betac", [128, 8], F32),
    ("identf", [128, 128], F32), ("ltri", [128, 128], F32),
    ("eidxm8", [128, 8], F32), ("coremask", [8, 1], F32),
    ("identb", [128, 128], BF16), ("cmask", [128, 384], BF16),
    ("ohot", [NE, NE * 128], BF16),
]

BF_COLS = 88064
FA_COLS = 7168


class Prog:
    def __init__(self, stages, ext_in="xT", ext_out="outT", dbg_out=()):
        self.stages = stages
        self.dbg_out = set(dbg_out)
        nc = bass.Bass("TRN2", target_bir_lowering=False)
        self.nc = nc
        self.d = {}

        def dram(name, shape, dtype, kind):
            self.d[name] = nc.dram_tensor(name, shape, dtype, kind=kind).ap()

        dram("xT", [D, T], F32, "ExternalInput")
        dram("outT", [D, T], F32, "ExternalOutput")
        for nm, sh, dty in WSPECS:
            dram(nm, sh, dty, "ExternalInput")
        for nm, sh, dty in [
                ("x1", [D, T], F32), ("x2", [D, T], F32), ("x3", [D, T], F32),
                ("xnS", [D, T], BF16), ("xn3f", [D, T], F32),
                ("l0qt", [D, T], BF16), ("l0kt", [D, T], BF16), ("l0ou", [D, T], F32),
                ("hbA", [HID, T], BF16), ("hbB", [HID, T], BF16),
                ("l1qt", [128, 8 * T], BF16), ("l1kt", [128, 8 * T], BF16),
                ("l1ou", [128, 8 * T], BF16),
                ("ccin", [1, 128], F32), ("ccout", [8, 128], F32),
                ("ye", [NE * D, T], BF16), ("gtd", [8, T], BF16)]:
            dram(nm, sh, dty, "ExternalOutput" if nm in self.dbg_out else "Internal")

        self.stack = ExitStack()
        e = self.stack.enter_context
        self.BF = e(nc.sbuf_tensor("BF", [128, BF_COLS], BF16))
        self.FA = e(nc.sbuf_tensor("FA", [128, FA_COLS], F32))
        self.identb = e(nc.sbuf_tensor("identb_sb", [128, 128], BF16))
        self.identf = e(nc.sbuf_tensor("identf_sb", [128, 128], F32))
        self.onesf = e(nc.sbuf_tensor("onesf_sb", [128, 128], F32))
        self.cmask = e(nc.sbuf_tensor("cmask_sb", [128, 384], BF16))
        self.ltri = e(nc.sbuf_tensor("ltri_sb", [128, 128], F32))
        self.eidxm8 = e(nc.sbuf_tensor("eidxm8_sb", [128, 8], F32))
        self.smallf = e(nc.sbuf_tensor("smallf_sb", [128, 8], F32))
        self.ohot_sb = e(nc.sbuf_tensor("ohot_sb", [8, NE * 128], BF16))
        self.fc2b_sb = e(nc.sbuf_tensor("fc2b_sb", [8, D], BF16))
        self.ps = [e(nc.psum_tensor(f"ps{j}", [128, 512], F32)) for j in range(6)]
        self.pb = [e(nc.psum_tensor(f"pb{j}", [128, 1024], BF16)) for j in range(2)]
        self.sems = {}
        for pos in range(len(stages) + 1):
            for nm in ("LD", "WT", "PE", "ACT", "DVE", "ST", "GP"):
                key = f"{nm}{pos}"
                self.sems[key] = Sem(e(nc.semaphore(key)))
        self.sems["BAR"] = Sem(e(nc.semaphore("BAR")))

        with nc.Block() as blk:
            for name, meth in (("sync", blk.sync), ("tensor", blk.tensor),
                               ("scalar", blk.scalar), ("vector", blk.vector),
                               ("gpsimd", blk.gpsimd)):
                def body(eng, name=name):
                    for s in self.sems.values():
                        s.v = 0
                    self.walk(K(nc, name, eng))
                meth(body)
        self.stack.close()

    def walk(self, k):
        sm = self.sems
        pre = sm["LD0"]
        k.dma("gpsimd", self.identb[:, :], self.d["identb"][:, :], pre)
        k.dma("gpsimd", self.identf[:, :], self.d["identf"][:, :], pre)
        k.dma("gpsimd", self.cmask[:, :], self.d["cmask"][:, :], pre)
        k.dma("gpsimd", self.ltri[:, :], self.d["ltri"][:, :], pre)
        k.dma("gpsimd", self.eidxm8[:, :], self.d["eidxm8"][:, :], pre)
        k.gop("memset", self.onesf[:, :], 1.0, sem=sm["GP0"])
        k.gop("memset", self.smallf[:, 0:1], 1e-5, sem=sm["GP0"])
        for e in ("vector", "tensor", "scalar"):
            k.w(e, pre, pre.v)
            k.w(e, sm["GP0"], sm["GP0"].v)
        k.barrier(sm["BAR"])
        for idx, st in enumerate(self.stages, start=1):
            s = {nm: sm[f"{nm}{idx}"] for nm in ("LD", "WT", "PE", "ACT", "DVE", "ST", "GP")}
            getattr(self, "st_" + st[0])(k, s, *st[1:])
            k.barrier(sm["BAR"])

    def fence(self, k, s):
        k.w("gpsimd", s["ST"], s["ST"].v)
        k.w("gpsimd", s["WT"], s["WT"].v)
        k.w("sync", s["LD"], s["LD"].v)
        k.barrier(self.sems["BAR"])

    def st_ln(self, k, s, src, dstb, dstf=None):
        CH = 128
        NCH = T // CH
        FA, BF = self.FA, self.BF
        src_d, dstb_d = self.d[src], self.d[dstb]
        dstf_d = self.d[dstf] if dstf else None
        XB = [0, 1024]
        SQ = [2048, 2176]
        TMP = 2304
        FST = [2432, 3456]
        ROWS = 4480
        ld, pe, acts, dve, st = s["LD"], s["PE"], s["ACT"], s["DVE"], s["ST"]

        mmsq = {}
        stats_done = [0] * NCH
        bcast_done = [0] * NCH
        apply_done = [0] * NCH
        store_done = [0] * NCH
        for c in range(NCH):
            par = c % 2
            mu = FA[0:1, ROWS + par * 512: ROWS + par * 512 + CH]
            v1 = FA[0:1, ROWS + par * 512 + 128: ROWS + par * 512 + 128 + CH]
            rs = FA[0:1, ROWS + par * 512 + 256: ROWS + par * 512 + 256 + CH]
            sd = FA[0:1, ROWS + par * 512 + 384: ROWS + par * 512 + 384 + CH]
            if c >= 2:
                k.w("sync", dve, apply_done[c - 2])
            k.dma("sync", sb3(FA, XB[par], 8, CH), ap3(src_d, 0, 8, c * CH, CH), ld)
            ldv = ld.v
            bsum, bsq = self.ps[par], self.ps[2 + par]
            bmu, brs = self.ps[4], self.ps[5]
            for sl in range(8):
                xsl = FA[:, XB[par] + sl * CH: XB[par] + (sl + 1) * CH]
                sqb = FA[:, SQ[sl % 2]: SQ[sl % 2] + CH]
                if sl == 0:
                    k.w("scalar", ld, ldv)
                    if c >= 1:
                        k.w("scalar", pe, mmsq[(c - 1, 7)])
                elif sl in (1,) and c >= 1:
                    k.w("scalar", pe, mmsq[(c - 1, 7)])
                elif sl >= 2:
                    k.w("scalar", pe, mmsq[(c, sl - 2)])
                k.act(sqb, xsl, AF.Square, sem=acts)
                sqv = acts.v
                if sl == 0:
                    k.w("tensor", ld, ldv)
                    if c >= 2:
                        k.w("tensor", dve, stats_done[c - 2])
                k.mm(bsum[0:1, 0:CH], self.onesf[:, 0:1], xsl,
                     start=(sl == 0), stop=(sl == 7))
                k.w("tensor", acts, sqv)
                k.mm(bsq[0:1, 0:CH], self.onesf[:, 0:1], sqb,
                     start=(sl == 0), stop=(sl == 7), sem=pe)
                mmsq[(c, sl)] = pe.v
            k.w("vector", pe, mmsq[(c, 7)])
            if c >= 2:
                k.w("vector", pe, bcast_done[c - 2])
            k.vop("tensor_scalar_mul", mu, bsum[0:1, 0:CH], 1.0 / 1024.0, sem=dve)
            k.w("vector", dve, dve.v)
            k.vop("tensor_mul", v1, mu, mu, sem=dve)
            k.w("vector", dve, dve.v)
            k.vop("scalar_tensor_tensor", v1, bsq[0:1, 0:CH], 1.0 / 1024.0, v1,
                  op0=ALU.mult, op1=ALU.subtract, sem=dve)
            k.w("scalar", dve, dve.v)
            k.act(sd, v1, AF.Sqrt, bias=self.smallf[0:1, 0:1], sem=acts)
            k.w("vector", acts, acts.v)
            k.vop("reciprocal", rs, sd, sem=dve)
            stats_done[c] = dve.v
            if c >= 1:
                k.w("tensor", dve, apply_done[c - 1])
            k.w("tensor", dve, stats_done[c])
            k.mm(bmu[:, 0:CH], self.onesf[0:1, :], mu, start=True, stop=True)
            k.mm(brs[:, 0:CH], self.onesf[0:1, :], rs, start=True, stop=True, sem=pe)
            bcast_done[c] = pe.v
            k.w("vector", pe, bcast_done[c])
            if c >= 2:
                k.w("vector", st, store_done[c - 2])
            for sl in range(8):
                xsl = FA[:, XB[par] + sl * CH: XB[par] + (sl + 1) * CH]
                tm = FA[:, TMP: TMP + CH]
                ob = BF[:, par * 1024 + sl * CH: par * 1024 + (sl + 1) * CH]
                k.vop("tensor_sub", tm, xsl, bmu[:, 0:CH])
                k.vop("tensor_mul", ob, tm, brs[:, 0:CH],
                      sem=(dve if (sl == 7 and not dstf) else None))
                if dstf:
                    fo = FA[:, FST[par] + sl * CH: FST[par] + (sl + 1) * CH]
                    k.vop("tensor_mul", fo, tm, brs[:, 0:CH],
                          sem=(dve if sl == 7 else None))
            apply_done[c] = dve.v
            k.w("gpsimd", dve, apply_done[c])
            k.dma("gpsimd", ap3(dstb_d, 0, 8, c * CH, CH), sb3(BF, par * 1024, 8, CH), st)
            if dstf:
                k.dma("gpsimd", ap3(dstf_d, 0, 8, c * CH, CH), sb3(FA, FST[par], 8, CH), st)
            store_done[c] = st.v
        k.w("gpsimd", st, store_done[NCH - 1])

    def st_l0a(self, k, s, src, dst):
        FA, BF = self.FA, self.BF
        dd = self.d
        ld, wt, pe, acts, dve, st = s["LD"], s["WT"], s["PE"], s["ACT"], s["DVE"], s["ST"]
        WQ, WK, WV = 0, 8192, 16384
        XN = [24576, 28672]
        VT = 32768
        QS = [65536, 66048]
        KS = [66560, 67072]
        BQ8, BK8 = 0, 8
        BVB = 16
        k.dma("gpsimd", sb3(BF, WQ, 8, 1024), ap3(dd["Wq"], 0, 8, 0, 1024), wt)
        k.dma("gpsimd", sb3(BF, WK, 8, 1024), ap3(dd["Wk"], 0, 8, 0, 1024), wt)
        k.dma("gpsimd", sb3(BF, WV, 8, 1024), ap3(dd["Wv"], 0, 8, 0, 1024), wt)
        k.dma("gpsimd", FA[:, BQ8:BQ8 + 8], dd["bq8"][:, :], wt)
        k.dma("gpsimd", FA[:, BK8:BK8 + 8], dd["bk8"][:, :], wt)
        k.dma("gpsimd", FA[0:1, BVB:BVB + D], dd["bv"][:, :], wt)
        wt_all = wt.v
        k.w("tensor", wt, wt_all)
        k.mm(self.ps[4][:, :], self.onesf[0:1, :], FA[0:1, BVB:BVB + 512],
             start=True, stop=True)
        k.mm(self.ps[5][:, :], self.onesf[0:1, :], FA[0:1, BVB + 512:BVB + 1024],
             start=True, stop=True, sem=pe)
        k.w("vector", pe, pe.v)
        k.vop("tensor_copy", FA[:, BVB:BVB + 512], self.ps[4][:, :])
        k.vop("tensor_copy", FA[:, BVB + 512:BVB + 1024], self.ps[5][:, :], sem=dve)
        bvb_done = dve.v

        q_evac = {}
        v_evac = {}
        qk_store = {}
        xn_read = {}
        for c in range(8):
            par = c % 2
            if c >= 2:
                k.w("sync", pe, xn_read[c - 2])
            k.dma("sync", sb3(BF, XN[par], 8, 512), ap3(dd["xnS"], 0, 8, c * 512, 512), ld)
            ldv = ld.v
            for mi in range(16):
                which = "q" if mi < 8 else "k"
                m = mi % 8
                woff = WQ if which == "q" else WK
                boff = BQ8 if which == "q" else BK8
                qdram = dd["l0qt"] if which == "q" else dd["l0kt"]
                stg = QS if which == "q" else KS
                g = c * 16 + mi
                bank = self.ps[g % 2]
                for sl in range(8):
                    if sl == 0:
                        k.w("tensor", ld, ldv)
                        if g == 0:
                            k.w("tensor", wt, wt_all)
                        if g >= 2:
                            k.w("tensor", dve, q_evac[g - 2])
                    k.mm(bank[:, :], BF[:, woff + sl * 1024 + m * 128: woff + sl * 1024 + (m + 1) * 128],
                         BF[:, XN[par] + sl * 512: XN[par] + (sl + 1) * 512],
                         start=(sl == 0), stop=(sl == 7), sem=(pe if sl == 7 else None))
                mmv = pe.v
                k.w("vector", pe, mmv)
                if g >= 2:
                    k.w("vector", st, qk_store[g - 2])
                k.vop("tensor_scalar_add", BF[:, stg[m % 2]: stg[m % 2] + 512],
                      bank[:, :], FA[:, boff + m: boff + m + 1], sem=dve)
                q_evac[g] = dve.v
                k.w("gpsimd", dve, dve.v)
                k.dma("gpsimd", qdram[m * 128:(m + 1) * 128, c * 512:(c + 1) * 512],
                      BF[:, stg[m % 2]: stg[m % 2] + 512], st)
                qk_store[g] = st.v
            for sub in range(4):
                tile = c * 4 + sub
                for half in range(2):
                    gv = c * 8 + sub * 2 + half
                    bank = self.ps[2 + gv % 2]
                    for sl in range(8):
                        if sl == 0:
                            k.w("tensor", ld, ldv)
                            if gv >= 2:
                                k.w("tensor", dve, v_evac[gv - 2])
                        k.mm(bank[:, :],
                             BF[:, XN[par] + sl * 512 + sub * 128: XN[par] + sl * 512 + (sub + 1) * 128],
                             BF[:, WV + sl * 1024 + half * 512: WV + sl * 1024 + (half + 1) * 512],
                             start=(sl == 0), stop=(sl == 7), sem=(pe if sl == 7 else None))
                    mmv = pe.v
                    k.w("vector", pe, mmv)
                    if gv == 0:
                        k.w("vector", dve, bvb_done)
                    k.vop("tensor_add", BF[:, VT + tile * 1024 + half * 512: VT + tile * 1024 + (half + 1) * 512],
                          bank[:, :], FA[:, BVB + half * 512: BVB + (half + 1) * 512], sem=dve)
                    v_evac[gv] = dve.v
            xn_read[c] = pe.v
        p0_pe = pe.v
        p0_dve = dve.v
        p0_st = st.v

        KT = 0
        EB = 65536
        QB = 74240
        ZC, OUS, ZCOL, RZC = 1152, 1760, 2304, 2320
        k.w("gpsimd", st, p0_st)
        k.w("gpsimd", dve, p0_dve)
        k.dma("gpsimd", sb3(BF, KT, 8, 4096), ap3(dd["l0kt"], 0, 8, 0, 4096), wt)
        kt_done = wt.v
        k.vop("memset", FA[:, ZC:ZC + 544], 0.0, sem=dve)
        zc_zero = dve.v
        zci = 0
        av_done = [0] * 16
        exp_done = [0] * 16
        s_done = [0] * 16
        ou_stored = {}
        ou_evac = {}
        for i in range(16):
            nj = 2 * i + 2
            if i >= 1:
                k.w("sync", pe, s_done[i - 1])
            k.dma("sync", sb3(BF, QB, 8, 256), ap3(dd["l0qt"], 0, 8, i * 256, 256), ld)
            qldv = ld.v
            if i >= 1:
                k.w("scalar", pe, av_done[i - 1])
            for j in range(nj):
                bank = self.ps[j % 2]
                for sl in range(8):
                    if sl == 0:
                        k.w("tensor", ld, qldv)
                        if i == 0 and j == 0:
                            k.w("tensor", wt, kt_done)
                        if j >= 2:
                            k.w("tensor", acts, exp_j[j - 2])
                        elif i >= 1:
                            k.w("tensor", acts, exp_done[i - 1])
                    k.mm(bank[:, 0:256],
                         BF[:, KT + sl * 4096 + j * 128: KT + sl * 4096 + (j + 1) * 128],
                         BF[:, QB + sl * 256: QB + (sl + 1) * 256],
                         start=(sl == 0), stop=(sl == 7), sem=(pe if sl == 7 else None))
                if j == 0:
                    exp_j = {}
                smm = pe.v
                k.w("scalar", pe, smm)
                diag = (j >= 2 * i)
                etile = BF[:, EB + j * 256: EB + (j + 1) * 256]
                if not diag:
                    if zci == 0:
                        k.w("scalar", dve, zc_zero)
                    k.act(etile, bank[:, 0:256], AF.Exp, scale=1.0 / 32.0,
                          accum=FA[:, ZC + zci: ZC + zci + 1], sem=acts)
                else:
                    k.act(etile, bank[:, 0:256], AF.Exp, scale=1.0 / 32.0, sem=acts)
                    k.w("vector", acts, acts.v)
                    moff = 128 if j == 2 * i else 0
                    k.vop("tensor_mul", etile, etile, self.cmask[:, moff:moff + 256])
                    k.w("vector", dve, zc_zero)
                    k.vop("reduce_sum", FA[:, ZC + zci: ZC + zci + 1], etile,
                          axis=AX.X, sem=dve)
                exp_j[j] = acts.v
                zci += 1
            s_done[i] = pe.v
            exp_done[i] = acts.v
            mask_done_i = dve.v
            for m in range(8):
                bank = self.ps[2 + m % 2]
                for j in range(nj):
                    if j == 0:
                        k.w("tensor", acts, exp_done[i])
                        k.w("tensor", dve, mask_done_i)
                        if m >= 2:
                            k.w("tensor", dve, ou_evac[(i, m - 2)])
                        elif i >= 1:
                            k.w("tensor", dve, ou_evac[(i - 1, m + 6)])
                    k.mm(bank[:, 0:256],
                         BF[:, VT + j * 1024 + m * 128: VT + j * 1024 + (m + 1) * 128],
                         BF[:, EB + j * 256: EB + (j + 1) * 256],
                         start=(j == 0), stop=(j == nj - 1), sem=(pe if j == nj - 1 else None))
                avv = pe.v
                k.w("vector", pe, avv)
                idx = i * 8 + m
                if idx >= 2:
                    k.w("vector", st, ou_stored.get(idx - 2, 0))
                ob = FA[:, OUS + (m % 2) * 256: OUS + (m % 2) * 256 + 256]
                k.vop("tensor_copy", ob, bank[:, 0:256], sem=dve)
                ou_evac[(i, m)] = dve.v
                k.w("gpsimd", dve, dve.v)
                k.dma("gpsimd", dd["l0ou"][m * 128:(m + 1) * 128, i * 256:(i + 1) * 256], ob, st)
                ou_stored[idx] = st.v
            av_done[i] = pe.v
        k.w("vector", acts, exp_done[15])
        k.vop("reduce_sum", FA[:, ZCOL:ZCOL + 1], FA[:, ZC:ZC + 544], axis=AX.X, sem=dve)
        k.w("tensor", dve, dve.v)
        k.tr(self.ps[0][0:1, 0:128], FA[:, ZCOL:ZCOL + 1], self.identf[:, :], sem=pe)
        k.w("vector", pe, pe.v)
        k.vop("reduce_sum", FA[0:1, ZCOL + 2:ZCOL + 3], self.ps[0][0:1, 0:128],
              axis=AX.X, sem=dve)
        k.w("vector", dve, dve.v)
        k.vop("reciprocal", FA[0:1, ZCOL + 4:ZCOL + 5], FA[0:1, ZCOL + 2:ZCOL + 3], sem=dve)
        k.w("tensor", dve, dve.v)
        k.mm(self.ps[1][:, 0:1], self.onesf[0:1, :], FA[0:1, ZCOL + 4:ZCOL + 5],
             start=True, stop=True, sem=pe)
        k.w("vector", pe, pe.v)
        k.vop("tensor_copy", FA[:, RZC:RZC + 1], self.ps[1][:, 0:1], sem=dve)
        rz_done = dve.v

        XC, OC, OO = 2560, 3584, 4608
        res_done = {}
        out_done = {}
        k.w("sync", st, ou_stored[127])
        for sl in range(8):
            for h in range(8):
                idx = sl * 8 + h
                par = idx % 2
                xc = FA[:, XC + par * 512: XC + par * 512 + 512]
                oc = FA[:, OC + par * 512: OC + par * 512 + 512]
                oo = FA[:, OO + par * 512: OO + par * 512 + 512]
                if idx >= 2:
                    k.w("sync", dve, res_done[idx - 2])
                k.dma("sync", xc, dd[src][sl * 128:(sl + 1) * 128, h * 512:(h + 1) * 512], ld)
                k.dma("sync", oc, dd["l0ou"][sl * 128:(sl + 1) * 128, h * 512:(h + 1) * 512], ld)
                ldv = ld.v
                k.w("vector", ld, ldv)
                if idx == 0:
                    k.w("vector", dve, rz_done)
                if idx >= 2:
                    k.w("vector", st, out_done[idx - 2])
                k.vop("scalar_tensor_tensor", oo, oc, FA[:, RZC:RZC + 1], xc,
                      op0=ALU.mult, op1=ALU.add, sem=dve)
                res_done[idx] = dve.v
                k.w("gpsimd", dve, dve.v)
                k.dma("gpsimd", dd[dst][sl * 128:(sl + 1) * 128, h * 512:(h + 1) * 512], oo, st)
                out_done[idx] = st.v
        k.w("gpsimd", st, st.v)

    def st_l0f(self, k, s, src, dst):
        FA, BF = self.FA, self.BF
        dd = self.d
        ld, wt, pe, acts, dve, st = s["LD"], s["WT"], s["PE"], s["ACT"], s["DVE"], s["ST"]

        W1, W2 = 0, 32768
        XNC = [65536, 69632]
        HST = [73728, 74240]
        TMPA = [0, 512]
        SC = 1024
        for sl in range(4):
            k.dma("gpsimd", sb3(BF, W1 + sl * 8192, 2, 4096),
                  ap3(dd["w1T"], sl * 256, 2, 0, 4096), wt)
        for sl in range(4):
            k.dma("gpsimd", sb3(BF, W2 + sl * 8192, 2, 4096),
                  ap3(dd["w2T"], sl * 256, 2, 0, 4096), wt)
        k.dma("gpsimd", FA[:, SC:SC + 32], dd["s1c"][:, :], wt)
        k.dma("gpsimd", FA[:, SC + 32:SC + 64], dd["s2c"][:, :], wt)
        wt_a = wt.v
        evac1 = {}
        evac2 = {}
        hstore = {}
        xn_read = {}
        for c in range(8):
            par = c % 2
            if c >= 2:
                k.w("sync", pe, xn_read[c - 2])
            k.dma("sync", sb3(BF, XNC[par], 8, 512), ap3(dd["xnS"], 0, 8, c * 512, 512), ld)
            ldv = ld.v
            for oc in range(32):
                g = c * 32 + oc
                b1, b2 = self.ps[g % 2], self.ps[2 + g % 2]
                for sl in range(8):
                    if sl == 0:
                        k.w("tensor", ld, ldv)
                        if g == 0:
                            k.w("tensor", wt, wt_a)
                        if g >= 2:
                            k.w("tensor", acts, evac1[g - 2])
                            k.w("tensor", dve, evac2[g - 2])
                    k.mm(b1[:, :], BF[:, W1 + sl * 4096 + oc * 128: W1 + sl * 4096 + (oc + 1) * 128],
                         BF[:, XNC[par] + sl * 512: XNC[par] + (sl + 1) * 512],
                         start=(sl == 0), stop=(sl == 7))
                    k.mm(b2[:, :], BF[:, W2 + sl * 4096 + oc * 128: W2 + sl * 4096 + (oc + 1) * 128],
                         BF[:, XNC[par] + sl * 512: XNC[par] + (sl + 1) * 512],
                         start=(sl == 0), stop=(sl == 7), sem=(pe if sl == 7 else None))
                mmv = pe.v
                k.w("scalar", pe, mmv)
                if g >= 2:
                    k.w("scalar", dve, evac2[g - 2])
                tmp = FA[:, TMPA[g % 2]: TMPA[g % 2] + 512]
                k.act(tmp, b1[:, :], AF.Silu, scale=FA[:, SC + oc: SC + oc + 1], sem=acts)
                evac1[g] = acts.v
                k.w("vector", acts, acts.v)
                if g >= 2:
                    k.w("vector", st, hstore[g - 2])
                hs = BF[:, HST[g % 2]: HST[g % 2] + 512]
                k.vop("scalar_tensor_tensor", hs, b2[:, :], FA[:, SC + 32 + oc: SC + 33 + oc],
                      tmp, op0=ALU.mult, op1=ALU.mult, sem=dve)
                evac2[g] = dve.v
                k.w("gpsimd", dve, dve.v)
                k.dma("gpsimd", dd["hbA"][oc * 128:(oc + 1) * 128, c * 512:(c + 1) * 512], hs, st)
                hstore[g] = st.v
            xn_read[c] = pe.v
        self.fence(k, s)

        for (wname, scoff, hin, hout) in (("h1T", 0, "hbA", "hbB"), ("h2T", 32, "hbB", "hbA")):
            WQ = 0
            HC = [32768, 49152]
            GST = [65536, 66048]
            k.dma("gpsimd", FA[:, SC + 64 + scoff: SC + 96 + scoff],
                  dd["sh1c" if wname == "h1T" else "sh2c"][:, :], wt)
            evg = {}
            gst_d = {}
            hread = {}
            for qt in range(4):
                if qt >= 1:
                    k.w("gpsimd", pe, hread[(qt - 1, 7)])
                for sl in range(4):
                    k.dma("gpsimd", sb3(BF, WQ + sl * 8192, 8, 1024),
                          ap3(dd[wname], sl * 1024, 8, qt * 1024, 1024), wt)
                wt_q = wt.v
                for c in range(8):
                    par = c % 2
                    if (qt * 8 + c) >= 2:
                        pc = qt * 8 + c - 2
                        k.w("sync", pe, hread[(pc // 8, pc % 8)])
                    for half in range(4):
                        k.dma("sync", sb3(BF, HC[par] + half * 4096, 8, 512),
                              ap3(dd[hin], half * 1024, 8, c * 512, 512), ld)
                    ldv = ld.v
                    for oc in range(8):
                        g = (qt * 8 + c) * 8 + oc
                        bank = self.ps[g % 2]
                        for sl in range(32):
                            if sl == 0:
                                k.w("tensor", ld, ldv)
                                k.w("tensor", wt, wt_q)
                                if g >= 2:
                                    k.w("tensor", acts, evg[g - 2])
                            k.mm(bank[:, :],
                                 BF[:, WQ + sl * 1024 + oc * 128: WQ + sl * 1024 + (oc + 1) * 128],
                                 BF[:, HC[par] + sl * 512: HC[par] + (sl + 1) * 512],
                                 start=(sl == 0), stop=(sl == 31), sem=(pe if sl == 31 else None))
                        mmv = pe.v
                        k.w("scalar", pe, mmv)
                        if g >= 2:
                            k.w("scalar", st, gst_d[g - 2])
                        gs = BF[:, GST[g % 2]: GST[g % 2] + 512]
                        k.act(gs, bank[:, :], AF.Gelu,
                              scale=FA[:, SC + 64 + scoff + qt * 8 + oc: SC + 65 + scoff + qt * 8 + oc],
                              sem=acts)
                        evg[g] = acts.v
                        k.w("gpsimd", acts, acts.v)
                        k.dma("gpsimd",
                              dd[hout][qt * 1024 + oc * 128: qt * 1024 + (oc + 1) * 128,
                                       c * 512:(c + 1) * 512], gs, st)
                        gst_d[g] = st.v
                    hread[(qt, c)] = pe.v
            self.fence(k, s)

        W3 = 0
        HC = [32768, 40960]
        XC = [0, 2048]
        OST = [4352, 4608]
        S3 = 4864
        for sl in range(4):
            k.dma("gpsimd", sb3(BF, W3 + sl * 8192, 8, 1024),
                  ap3(dd["w3T"], sl * 1024, 8, 0, 1024), wt)
        k.dma("gpsimd", FA[:, S3:S3 + 8], dd["s3c"][:, :], wt)
        wt_d = wt.v
        evd = {}
        ost_d = {}
        hread = {}
        for c in range(16):
            par = c % 2
            if c >= 2:
                k.w("sync", pe, hread[c - 2])
                k.w("sync", dve, evd[(c - 2) * 8 + 7])
            for half in range(4):
                k.dma("sync", sb3(BF, HC[par] + half * 2048, 8, 256),
                      ap3(dd["hbA"], half * 1024, 8, c * 256, 256), ld)
            k.dma("sync", sb3(FA, XC[par], 8, 256), ap3(dd[src], 0, 8, c * 256, 256), ld)
            ldv = ld.v
            for m in range(8):
                g = c * 8 + m
                bank = self.ps[g % 2]
                for sl in range(32):
                    if sl == 0:
                        k.w("tensor", ld, ldv)
                        if g == 0:
                            k.w("tensor", wt, wt_d)
                        if g >= 2:
                            k.w("tensor", dve, evd[g - 2])
                    k.mm(bank[:, 0:256],
                         BF[:, W3 + sl * 1024 + m * 128: W3 + sl * 1024 + (m + 1) * 128],
                         BF[:, HC[par] + sl * 256: HC[par] + (sl + 1) * 256],
                         start=(sl == 0), stop=(sl == 31), sem=(pe if sl == 31 else None))
                mmv = pe.v
                k.w("vector", pe, mmv)
                k.w("vector", ld, ldv)
                if g >= 2:
                    k.w("vector", st, ost_d[g - 2])
                os_ = FA[:, OST[g % 2]: OST[g % 2] + 256]
                k.vop("scalar_tensor_tensor", os_, bank[:, 0:256], FA[:, S3 + m: S3 + m + 1],
                      FA[:, XC[par] + m * 256: XC[par] + (m + 1) * 256],
                      op0=ALU.mult, op1=ALU.add, sem=dve)
                evd[g] = dve.v
                k.w("gpsimd", dve, dve.v)
                k.dma("gpsimd", dd[dst][m * 128:(m + 1) * 128, c * 256:(c + 1) * 256], os_, st)
                ost_d[g] = st.v
            hread[c] = pe.v
        k.w("gpsimd", st, st.v)

    def st_l1a(self, k, s, src, dst):
        FA, BF = self.FA, self.BF
        dd = self.d
        ld, wt, pe, acts, dve, st = s["LD"], s["WT"], s["PE"], s["ACT"], s["DVE"], s["ST"]

        WQKV = 0
        XNC = [24576, 28672]
        V2 = 32768
        QST = [66560, 70656]
        KST = [74752, 78848]
        QN = [82944, 83008]
        QSB, KSB = 0, 64
        SSC, DEN = 128, 136
        JNK = 160
        k.gop("memset", BF[:, V2:V2 + 33280], 1.0, sem=s["GP"])
        v2_ones = s["GP"].v
        for sl in range(4):
            k.dma("gpsimd", sb3(BF, WQKV + sl * 6144, 2, 3072),
                  ap3(dd["Wqkv"], sl * 256, 2, 0, 3072), wt)
        k.dma("gpsimd", FA[0:1, QSB:QSB + 64], dd["qs_row"][:, :], wt)
        k.dma("gpsimd", FA[0:1, KSB:KSB + 64], dd["ks_row"][:, :], wt)
        wt_a = wt.v
        k.w("tensor", wt, wt_a)
        k.mm(self.ps[4][:, 0:64], self.onesf[0:1, :], FA[0:1, QSB:QSB + 64],
             start=True, stop=True)
        k.mm(self.ps[5][:, 0:64], self.onesf[0:1, :], FA[0:1, KSB:KSB + 64],
             start=True, stop=True, sem=pe)
        k.w("vector", pe, pe.v)
        k.vop("tensor_copy", FA[:, QSB:QSB + 64], self.ps[4][:, 0:64])
        k.vop("tensor_copy", FA[:, KSB:KSB + 64], self.ps[5][:, 0:64], sem=dve)
        scb_done = dve.v

        ev = {}
        tr_done = {}
        qn_done = {}
        cp_done = {}
        xn_read = {}
        stq = {}
        last_tr = {}
        last_pb = {}
        for c in range(8):
            par = c % 2
            if c >= 2:
                k.w("sync", pe, xn_read[c - 2])
                k.w("vector", st, stq[c - 2])
            k.dma("sync", sb3(BF, XNC[par], 8, 512), ap3(dd["xnS"], 0, 8, c * 512, 512), ld)
            ldv = ld.v
            for sub in range(4):
                tile = c * 4 + sub
                for n in range(6):
                    g = (c * 4 + sub) * 6 + n
                    bank = self.ps[g % 4]
                    for sl in range(8):
                        if sl == 0:
                            k.w("tensor", ld, ldv)
                            if g == 0:
                                k.w("tensor", wt, wt_a)
                            if g >= 4:
                                k.w("tensor", dve, ev.get(g - 4, 0))
                                k.w("tensor", acts, ev.get(("a", g - 4), 0))
                        k.mm(bank[:, :],
                             BF[:, XNC[par] + sl * 512 + sub * 128: XNC[par] + sl * 512 + (sub + 1) * 128],
                             BF[:, WQKV + sl * 3072 + n * 512: WQKV + sl * 3072 + (n + 1) * 512],
                             start=(sl == 0), stop=(sl == 7), sem=(pe if sl == 7 else None))
                    mmv = pe.v
                    if n >= 4:
                        k.w("vector", pe, mmv)
                        if g == 0:
                            pass
                        k.w("vector", s["GP"], v2_ones)
                        for h8 in range(8):
                            h = (n - 4) * 8 + h8
                            slot = tile * 16 + h
                            k.vop("tensor_copy", BF[:, V2 + slot * 65: V2 + slot * 65 + 64],
                                  bank[:, h8 * 64:(h8 + 1) * 64],
                                  sem=(dve if h8 == 7 else None))
                        ev[g] = dve.v
                        ev[("a", g)] = ev.get(("a", g - 1), 0)
                        continue
                    isq = n < 2
                    scoff = QSB if isq else KSB
                    sstg = QST[par] if isq else KST[par]
                    k.w("scalar", pe, mmv)
                    for h8 in range(8):
                        k.act(FA[:, JNK:JNK + 64], bank[:, h8 * 64:(h8 + 1) * 64],
                              AF.Square, accum=FA[:, SSC + h8:SSC + h8 + 1],
                              sem=(acts if h8 == 7 else None))
                    k.w("scalar", acts, acts.v)
                    k.act(FA[:, SSC:SSC + 8], FA[:, SSC:SSC + 8], AF.Sqrt, sem=acts)
                    sq_done = acts.v
                    ev[("a", g)] = acts.v
                    k.w("vector", acts, sq_done)
                    k.vop("tensor_scalar", FA[:, DEN:DEN + 8], FA[:, SSC:SSC + 8],
                          0.125, 1e-8, op0=ALU.mult, op1=ALU.add, sem=dve)
                    k.w("vector", dve, dve.v)
                    k.vop("reciprocal", FA[:, DEN:DEN + 8], FA[:, DEN:DEN + 8], sem=dve)
                    k.w("vector", dve, dve.v)
                    if c == 0 and sub == 0 and n == 0:
                        k.w("vector", dve, scb_done)
                    for h8 in range(8):
                        h = (n % 2) * 8 + h8
                        qb = QN[h8 % 2]
                        if (h8 % 2) in last_tr:
                            k.w("vector", pe, last_tr[h8 % 2])
                        k.vop("scalar_tensor_tensor", BF[:, qb:qb + 64],
                              bank[:, h8 * 64:(h8 + 1) * 64], FA[:, DEN + h8:DEN + h8 + 1],
                              FA[:, scoff:scoff + 64], op0=ALU.mult, op1=ALU.mult, sem=dve)
                        qn_done[(g, h8)] = dve.v
                        k.w("tensor", dve, qn_done[(g, h8)])
                        if (h8 % 2) in last_pb:
                            k.w("tensor", dve, last_pb[h8 % 2])
                        k.tr(self.pb[h8 % 2][0:64, 0:128], BF[:, qb:qb + 64],
                             self.identb[:, :], sem=pe)
                        last_tr[h8 % 2] = pe.v
                        k.w("vector", pe, pe.v)
                        p_ = h // 2
                        r0 = (h % 2) * 64
                        k.vop("tensor_copy",
                              BF[r0:r0 + 64, sstg + p_ * 512 + sub * 128: sstg + p_ * 512 + (sub + 1) * 128],
                              self.pb[h8 % 2][0:64, 0:128], sem=dve)
                        last_pb[h8 % 2] = dve.v
                    ev[g] = dve.v
            xn_read[c] = pe.v
            k.w("gpsimd", dve, ev[(c * 4 + 3) * 6 + 3])
            for p_ in range(8):
                k.dma("gpsimd", dd["l1qt"][:, p_ * 4096 + c * 512: p_ * 4096 + (c + 1) * 512],
                      BF[:, QST[par] + p_ * 512: QST[par] + (p_ + 1) * 512], st)
                k.dma("gpsimd", dd["l1kt"][:, p_ * 4096 + c * 512: p_ * 4096 + (c + 1) * 512],
                      BF[:, KST[par] + p_ * 512: KST[par] + (p_ + 1) * 512], st)
            stq[c] = st.v
            if c >= 1:
                pass
        self.fence(k, s)

        KT = 0
        EA = [66048, 67072]
        EB = [68096, 69120]
        QB = 70144
        OUS = 74240
        ZR = 200
        ZBS = 1224
        k.dma("sync", BF[:, KT:KT + 32768], dd["l1kt"][:, :], ld)
        kt_done = ld.v
        sblk = {}
        expd = {}
        oud = {}
        oust = {}
        for i in range(8):
            if i >= 1:
                k.w("sync", pe, sblk[i - 1])
                k.w("vector", st, oust[i - 1])
            qsrc = dd["l1qt"].rearrange("p (n c) -> p n c", c=4096)[:, :, i * 512:(i + 1) * 512]
            k.dma("sync", sb3(BF, QB, 8, 512), qsrc, ld)
            ldq = ld.v
            for p_ in range(8):
                for j in range(32):
                    psA = self.ps[2 * (j % 2)]
                    psB = self.ps[1 + 2 * (j % 2)]
                    if j == 0:
                        k.w("tensor", ld, ldq)
                        if i == 0 and p_ == 0:
                            k.w("tensor", ld, kt_done)
                        k.w("tensor", dve, oud.get((i, p_ - 1) if p_ >= 1 else (i - 1, 7), 0))
                    if j >= 2:
                        k.w("tensor", acts, expd[(i, p_, j - 2)])
                    elif (i, p_) != (0, 0):
                        pp = (i, p_ - 1) if p_ >= 1 else (i - 1, 7)
                        k.w("tensor", acts, expd[(pp[0], pp[1], 31)])
                    k.mm(psA[:, :], BF[0:64, KT + p_ * 4096 + j * 128: KT + p_ * 4096 + (j + 1) * 128],
                         BF[0:64, QB + p_ * 512: QB + (p_ + 1) * 512], start=True, stop=True)
                    k.mm(psB[:, :], BF[64:128, KT + p_ * 4096 + j * 128: KT + p_ * 4096 + (j + 1) * 128],
                         BF[64:128, QB + p_ * 512: QB + (p_ + 1) * 512], start=True, stop=True,
                         sem=pe, tp=(64, 0))
                    smm = pe.v
                    k.w("scalar", pe, smm)
                    ea = BF[:, EA[j % 2]:EA[j % 2] + 512]
                    eb = BF[:, EB[j % 2]:EB[j % 2] + 512]
                    if j >= 2:
                        k.w("scalar", pe, avd[(j - 2, 1)])
                    k.act(ea, psA[:, :], AF.Exp, scale=0.125, sem=acts)
                    k.act(eb, psB[:, :], AF.Exp, scale=0.125, sem=acts)
                    expd[(i, p_, j)] = acts.v
                    if j == 0:
                        avd = {}
                        k.w("tensor", dve, oud.get((i, p_ - 1) if p_ >= 1 else (i - 1, 7), 0))
                    k.w("tensor", acts, expd[(i, p_, j)])
                    k.mm(self.ps[4][0:65, :], BF[:, V2 + (j * 16 + 2 * p_) * 65: V2 + (j * 16 + 2 * p_) * 65 + 65],
                         ea, start=(j == 0), stop=(j == 31))
                    k.mm(self.ps[5][0:65, :], BF[:, V2 + (j * 16 + 2 * p_ + 1) * 65: V2 + (j * 16 + 2 * p_ + 1) * 65 + 65],
                         eb, start=(j == 0), stop=(j == 31), sem=pe)
                    avd[(j, 1)] = pe.v
                av_last = pe.v
                k.w("vector", pe, av_last)
                zra = FA[0:1, ZR:ZR + 512]
                zrb = FA[0:1, ZR + 512:ZR + 1024]
                k.vop("reciprocal", zra, self.ps[4][64:65, 0:512])
                k.vop("reciprocal", zrb, self.ps[5][64:65, 0:512], sem=dve)
                k.w("vector", dve, dve.v)
                zb_done_prev = 0
                k.w("tensor", dve, dve.v)
                k.w("tensor", acts, expd[(i, p_, 31)])
                k.mm(self.ps[0][0:64, :], self.onesf[0:1, 0:64], zra, start=True, stop=True)
                k.mm(self.ps[1][0:64, :], self.onesf[0:1, 0:64], zrb, start=True, stop=True, sem=pe)
                zbv = pe.v
                k.w("vector", pe, zbv)
                k.vop("tensor_copy", FA[:, ZBS:ZBS + 512][0:64, :], self.ps[0][0:64, :])
                k.vop("tensor_copy", FA[:, ZBS + 512:ZBS + 1024][0:64, :], self.ps[1][0:64, :], sem=dve)
                k.w("vector", dve, dve.v)
                k.vop("tensor_mul", BF[0:64, OUS + p_ * 512: OUS + (p_ + 1) * 512],
                      self.ps[4][0:64, 0:512], FA[0:64, ZBS:ZBS + 512])
                k.vop("tensor_mul", BF[64:128, OUS + p_ * 512: OUS + (p_ + 1) * 512],
                      self.ps[5][0:64, 0:512], FA[0:64, ZBS + 512:ZBS + 1024], sem=dve)
                oud[(i, p_)] = dve.v
            sblk[i] = pe.v
            k.w("gpsimd", dve, oud[(i, 7)])
            if i >= 1:
                pass
            for p_ in range(8):
                k.dma("gpsimd", dd["l1ou"][:, p_ * 4096 + i * 512: p_ * 4096 + (i + 1) * 512],
                      BF[:, OUS + p_ * 512: OUS + (p_ + 1) * 512], st)
            oust[i] = st.v
        self.fence(k, s)

        WP = 0
        OUC = [8192, 10240]
        XC = [0, 2048]
        OST = [4352, 4608]
        BPJ = 4864
        for sl in range(4):
            k.dma("gpsimd", sb3(BF, WP + sl * 2048, 2, 1024),
                  ap3(dd["Wproj"], sl * 256, 2, 0, 1024), wt)
        k.dma("gpsimd", FA[:, BPJ:BPJ + 8], dd["bprojc"][:, :], wt)
        wt_p = wt.v
        evp = {}
        ostd = {}
        ourd = {}
        for c in range(16):
            par = c % 2
            if c >= 2:
                k.w("sync", pe, ourd[c - 2])
                k.w("sync", dve, evp[(c - 2) * 8 + 7])
            for p_ in range(8):
                k.dma("sync", BF[:, OUC[par] + p_ * 256: OUC[par] + (p_ + 1) * 256],
                      dd["l1ou"][:, p_ * 4096 + c * 256: p_ * 4096 + (c + 1) * 256], ld)
            k.dma("sync", sb3(FA, XC[par], 8, 256), ap3(dd[src], 0, 8, c * 256, 256), ld)
            ldv = ld.v
            for m in range(8):
                g = c * 8 + m
                bank = self.ps[g % 2]
                for sl in range(8):
                    if sl == 0:
                        k.w("tensor", ld, ldv)
                        if g == 0:
                            k.w("tensor", wt, wt_p)
                        if g >= 2:
                            k.w("tensor", dve, evp[g - 2])
                    k.mm(bank[:, 0:256],
                         BF[:, WP + sl * 1024 + m * 128: WP + sl * 1024 + (m + 1) * 128],
                         BF[:, OUC[par] + sl * 256: OUC[par] + (sl + 1) * 256],
                         start=(sl == 0), stop=(sl == 7), sem=(pe if sl == 7 else None))
                mmv = pe.v
                k.w("vector", pe, mmv)
                k.w("vector", ld, ldv)
                if g >= 2:
                    k.w("vector", st, ostd[g - 2])
                os_ = FA[:, OST[g % 2]: OST[g % 2] + 256]
                k.vop("scalar_tensor_tensor", os_, bank[:, 0:256], FA[:, BPJ + m: BPJ + m + 1],
                      FA[:, XC[par] + m * 256: XC[par] + (m + 1) * 256],
                      op0=ALU.add, op1=ALU.add, sem=dve)
                evp[g] = dve.v
                k.w("gpsimd", dve, dve.v)
                k.dma("gpsimd", dd[dst][m * 128:(m + 1) * 128, c * 256:(c + 1) * 256], os_, st)
                ostd[g] = st.v
            ourd[c] = pe.v
        k.w("gpsimd", st, st.v)

    def st_moe(self, k, s, src, dst):
        FA, BF = self.FA, self.BF
        dd = self.d
        ld, wt, pe, acts, dve, st = s["LD"], s["WT"], s["PE"], s["ACT"], s["DVE"], s["ST"]

        def vsync():
            k.w("vector", dve, dve.v)

        GTC = 81920

        XNT = [0, 1024]
        WGC = 2048
        SEL1, SEL2 = 2112, 2368
        G1C, G2C = 2624, 2656
        RW = 2688
        NM, SE, M1, I1, M2, RD = RW, RW + 1, RW + 2, RW + 3, RW + 4, RW + 5
        PRB, T1, CAND = RW + 16, RW + 24, RW + 32
        SELT = RW + 40
        CC = RW + 64
        CCO = CC + 128
        OF1, OF2 = CCO + 140, CCO + 150
        OF1B, OF2B = CCO + 160, CCO + 168
        B1R, B2R = CCO + 180, CCO + 188
        RNK, KP = CCO + 196, CCO + 198
        k.dma("gpsimd", sb3(FA, WGC, 8, 8), ap3(dd["wg"], 0, 8, 0, 8), wt)
        k.dma("gpsimd", self.ohot_sb[:, :], dd["ohot"][:, :], wt)
        k.dma("gpsimd", self.fc2b_sb[:, :], dd["fc2b"][:, :], wt)
        wt_r = wt.v
        ev_t = {}
        for t in range(32):
            par = t % 2
            if t >= 2:
                k.w("sync", dve, ev_t[t - 2])
            k.dma("sync", sb3(FA, XNT[par], 8, 128), ap3(dd["xn3f"], 0, 8, t * 128, 128), ld)
            ldv = ld.v
            bank = self.ps[par]
            for sl in range(8):
                if sl == 0:
                    k.w("tensor", ld, ldv)
                    k.w("tensor", wt, wt_r)
                    if t >= 2:
                        k.w("tensor", dve, ev_t[t - 2])
                k.mm(bank[:, 0:8], FA[:, XNT[par] + sl * 128: XNT[par] + (sl + 1) * 128],
                     FA[:, WGC + sl * 8: WGC + (sl + 1) * 8],
                     start=(sl == 0), stop=(sl == 7), sem=(pe if sl == 7 else None))
            k.w("vector", pe, pe.v)
            nm = FA[:, NM:NM + 1]
            k.vop("reduce_max", nm, bank[:, 0:8], axis=AX.X, negate=True, sem=dve)
            k.w("scalar", dve, dve.v)
            k.act(FA[:, PRB:PRB + 8], bank[:, 0:8], AF.Exp, bias=nm,
                  accum=FA[:, SE:SE + 1], sem=acts)
            k.w("vector", acts, acts.v)
            k.vop("reciprocal", FA[:, SE:SE + 1], FA[:, SE:SE + 1], sem=dve)
            vsync()
            k.vop("tensor_scalar_mul", FA[:, PRB:PRB + 8], FA[:, PRB:PRB + 8],
                  FA[:, SE:SE + 1], sem=dve)
            vsync()
            sel1 = FA[:, SEL1 + t * 8: SEL1 + (t + 1) * 8]
            sel2 = FA[:, SEL2 + t * 8: SEL2 + (t + 1) * 8]
            for (selx, mx, gcol) in ((sel1, M1, G1C), (sel2, M2, G2C)):
                k.vop("reduce_max", FA[:, mx:mx + 1], FA[:, PRB:PRB + 8], axis=AX.X, sem=dve)
                vsync()
                k.vop("tensor_scalar", FA[:, T1:T1 + 8], FA[:, PRB:PRB + 8],
                      FA[:, mx:mx + 1], None, op0=ALU.is_equal, sem=dve)
                vsync()
                k.vop("tensor_mul", FA[:, CAND:CAND + 8], FA[:, T1:T1 + 8],
                      self.eidxm8[:, :], sem=dve)
                vsync()
                k.vop("tensor_reduce", FA[:, I1:I1 + 1], FA[:, CAND:CAND + 8],
                      axis=AX.X, op=ALU.min, sem=dve)
                vsync()
                k.vop("tensor_scalar", selx, self.eidxm8[:, :], FA[:, I1:I1 + 1],
                      None, op0=ALU.is_equal, sem=dve)
                vsync()
                if mx == M1:
                    k.vop("tensor_scalar", FA[:, SELT:SELT + 8], selx, -1.0, 1.0,
                          op0=ALU.mult, op1=ALU.add, sem=dve)
                    vsync()
                    k.vop("tensor_mul", FA[:, PRB:PRB + 8], FA[:, PRB:PRB + 8],
                          FA[:, SELT:SELT + 8], sem=dve)
                    vsync()
            k.vop("tensor_add", FA[:, RD:RD + 1], FA[:, M1:M1 + 1], FA[:, M2:M2 + 1], sem=dve)
            vsync()
            k.vop("tensor_scalar_max", FA[:, RD:RD + 1], FA[:, RD:RD + 1], MOE_EPS, sem=dve)
            vsync()
            k.vop("reciprocal", FA[:, RD:RD + 1], FA[:, RD:RD + 1], sem=dve)
            vsync()
            k.vop("tensor_mul", FA[:, G1C + t:G1C + t + 1], FA[:, M1:M1 + 1],
                  FA[:, RD:RD + 1], sem=dve)
            k.vop("tensor_mul", FA[:, G2C + t:G2C + t + 1], FA[:, M2:M2 + 1],
                  FA[:, RD:RD + 1], sem=dve)
            ev_t[t] = dve.v
            k.w("tensor", dve, ev_t[t])
            k.mm(self.ps[4][0:1, 0:8], self.onesf[:, 0:1], sel1,
                 start=(t == 0), stop=(t == 31))
            k.mm(self.ps[5][0:1, 0:8], self.onesf[:, 0:1], sel2,
                 start=(t == 0), stop=(t == 31), sem=(pe if t == 31 else None))
        cnt_pe = pe.v
        k.vop("memset", FA[0:1, CC:CC + 128], 0.0, sem=dve)
        vsync()
        k.w("vector", pe, cnt_pe)
        k.vop("tensor_copy", FA[0:1, CC:CC + 8], self.ps[4][0:1, 0:8])
        k.vop("tensor_copy", FA[0:1, CC + 8:CC + 16], self.ps[5][0:1, 0:8], sem=dve)
        k.w("gpsimd", dve, dve.v)
        k.dma("gpsimd", dd["ccin"][:, :], FA[0:1, CC:CC + 128], s["GP"])
        k.w("gpsimd", s["GP"], s["GP"].v)
        k.gop("collective_compute", "AllGather", ALU.bypass,
              replica_groups=[[0, 1, 2, 3, 4, 5, 6, 7]],
              ins=[dd["ccin"].tensor.ap().opt()],
              outs=[dd["ccout"].tensor.ap().opt()], sem=s["GP"], inc=1)
        k.w("gpsimd", s["GP"], s["GP"].v)
        k.dma("gpsimd", FA[0:8, CCO:CCO + 128], dd["ccout"][:, :], s["GP"])
        k.dma("gpsimd", FA[0:8, CCO + 128:CCO + 129], dd["coremask"][:, :], s["GP"])
        cm_done = s["GP"].v
        k.w("tensor", s["GP"], cm_done)
        k.mm(self.ps[2][0:1, 0:16], FA[0:8, CCO + 128:CCO + 129], FA[0:8, CCO:CCO + 16],
             start=True, stop=True)
        k.mm(self.ps[3][0:1, 0:16], self.onesf[0:8, 0:1], FA[0:8, CCO:CCO + 16],
             start=True, stop=True, sem=pe)
        pfx_pe = pe.v
        k.w("vector", pe, pfx_pe)
        k.vop("tensor_scalar_add", FA[0:1, OF1:OF1 + 8], self.ps[2][0:1, 0:8], -1.0)
        k.vop("tensor_copy", FA[0:1, OF2:OF2 + 8], self.ps[3][0:1, 0:8], sem=dve)
        vsync()
        k.vop("scalar_tensor_tensor", FA[0:1, OF2:OF2 + 8], self.ps[2][0:1, 8:16], -1.0,
              FA[0:1, OF2:OF2 + 8], op0=ALU.add, op1=ALU.add, sem=dve)
        vsync()
        k.w("tensor", dve, dve.v)
        k.mm(self.ps[2][:, 32:40], self.onesf[0:1, :], FA[0:1, OF1:OF1 + 8],
             start=True, stop=True)
        k.mm(self.ps[3][:, 32:40], self.onesf[0:1, :], FA[0:1, OF2:OF2 + 8],
             start=True, stop=True, sem=pe)
        k.w("vector", pe, pe.v)
        k.vop("tensor_copy", FA[:, OF1B:OF1B + 8], self.ps[2][:, 32:40])
        k.vop("tensor_copy", FA[:, OF2B:OF2B + 8], self.ps[3][:, 32:40], sem=dve)
        vsync()
        k.vop("memset", FA[0:1, B1R:B1R + 8], 0.0)
        k.vop("memset", FA[0:1, B2R:B2R + 8], 0.0, sem=dve)
        vsync()
        gt_done = {}
        base_upd = {0: 0, 1: 0}
        for t in range(32):
            gtile = FA[:, CAND:CAND + 8]
            for (j, SEL, BR, OFB) in ((0, SEL1, B1R, OF1B), (1, SEL2, B2R, OF2B)):
                selx = FA[:, SEL + t * 8: SEL + (t + 1) * 8]
                gcol = (G1C if j == 0 else G2C) + t
                bank = self.ps[j]
                bsum = self.ps[2 + j]
                k.w("tensor", dve, dve.v)
                k.mm(bank[:, 0:8], self.ltri[:, :], selx, start=True, stop=False)
                k.mm(bank[:, 0:8], self.onesf[0:1, :], FA[0:1, BR:BR + 8],
                     start=False, stop=True)
                k.mm(bsum[0:1, 0:8], self.onesf[:, 0:1], selx, start=True, stop=True,
                     sem=pe)
                cum_pe = pe.v
                k.w("vector", pe, cum_pe)
                k.vop("tensor_add", FA[0:1, BR:BR + 8], FA[0:1, BR:BR + 8],
                      bsum[0:1, 0:8], sem=dve)
                base_upd[j] = dve.v
                vsync()
                k.vop("tensor_add", FA[:, T1:T1 + 8], bank[:, 0:8], FA[:, OFB:OFB + 8], sem=dve)
                vsync()
                k.vop("tensor_mul", FA[:, T1:T1 + 8], FA[:, T1:T1 + 8], selx, sem=dve)
                vsync()
                k.vop("reduce_sum", FA[:, RNK:RNK + 1], FA[:, T1:T1 + 8], axis=AX.X, sem=dve)
                vsync()
                k.vop("tensor_single_scalar", FA[:, KP:KP + 1], FA[:, RNK:RNK + 1],
                      float(CAP), op=ALU.is_lt, sem=dve)
                vsync()
                k.vop("tensor_mul", FA[:, KP:KP + 1], FA[:, KP:KP + 1],
                      FA[:, gcol:gcol + 1], sem=dve)
                vsync()
                if j == 0:
                    k.vop("tensor_scalar_mul", gtile, selx, FA[:, KP:KP + 1], sem=dve)
                else:
                    k.vop("scalar_tensor_tensor", gtile, selx, FA[:, KP:KP + 1],
                          gtile, op0=ALU.mult, op1=ALU.add, sem=dve)
                vsync()
            k.w("tensor", dve, dve.v)
            if t >= 2:
                k.w("tensor", dve, gt_done[t - 2])
            k.tr(self.ps[4 + t % 2][0:8, 0:128], gtile, self.identf[:, :], sem=pe)
            k.w("vector", pe, pe.v)
            k.vop("tensor_copy", BF[0:8, GTC + t * 128: GTC + (t + 1) * 128],
                  self.ps[4 + t % 2][0:8, 0:128], sem=dve)
            gt_done[t] = dve.v
        self.fence(k, s)

        WA, WB = 0, 32768
        XNC = [65536, 69632]
        HST = [73728, 74240]
        HC = [65536, 73728]
        YST = [86016, 86528]
        GB = [0, 512]
        TMP = [1024, 1536]
        F1B = 2048
        wa_done = [0] * NE
        wb_done = [0] * NE
        pha_done = [0] * NE
        phb_done = [0] * NE
        hstore = {}
        ystore = {}
        for e in range(NE):
            if e >= 1:
                k.w("gpsimd", pe, pha_done[e - 1])
            for q in range(4):
                k.dma("gpsimd", sb3(BF, WA + q * 8192, 2, 4096),
                      ap3(dd["fc1T"], e * 1024 + q * 256, 2, 0, 4096), wt)
            k.dma("gpsimd", FA[:, F1B:F1B + 32],
                  dd["fc1b"][e * 128:(e + 1) * 128, :], wt)
            wa_done[e] = wt.v
            if e >= 1:
                k.w("gpsimd", pe, phb_done[e - 1])
            for q in range(4):
                k.dma("gpsimd", sb3(BF, WB + q * 8192, 8, 1024),
                      ap3(dd["fc2"], e * 4096 + q * 1024, 8, 0, 1024), wt)
            wb_done[e] = wt.v
            ga_done = {}
            xn_read = {}
            for c in range(8):
                par = c % 2
                g0 = e * 8 + c
                if c >= 2:
                    k.w("sync", pe, xn_read[c - 2])
                k.dma("sync", sb3(BF, XNC[par], 8, 512), ap3(dd["xnS"], 0, 8, c * 512, 512), ld)
                ldv = ld.v
                k.w("tensor", wt, wa_done[e])
                if c >= 2:
                    k.w("tensor", dve, ga_done[("gb", c - 2)])
                k.mm(self.ps[4 + par][:, :], self.ohot_sb[0:8, e * 128:(e + 1) * 128],
                     BF[0:8, GTC + c * 512: GTC + (c + 1) * 512], start=True, stop=True, sem=pe)
                gbv = pe.v
                k.w("vector", pe, gbv)
                gb = FA[:, GB[par]:GB[par] + 512]
                k.vop("tensor_copy", gb, self.ps[4 + par][:, :], sem=dve)
                gb_done = dve.v
                ga_done[("gb", c)] = gb_done
                for oc in range(32):
                    g = c * 32 + oc
                    bank = self.ps[g % 2]
                    for sl in range(8):
                        if sl == 0:
                            k.w("tensor", ld, ldv)
                            if g >= 2:
                                k.w("tensor", dve, ga_done[g - 2])
                        k.mm(bank[:, :],
                             BF[:, WA + sl * 4096 + oc * 128: WA + sl * 4096 + (oc + 1) * 128],
                             BF[:, XNC[par] + sl * 512: XNC[par] + (sl + 1) * 512],
                             start=(sl == 0), stop=(sl == 7), sem=(pe if sl == 7 else None))
                    mmv = pe.v
                    k.w("scalar", pe, mmv)
                    if g >= 2:
                        k.w("scalar", dve, ga_done[g - 2])
                    tmp = FA[:, TMP[g % 2]: TMP[g % 2] + 512]
                    k.act(tmp, bank[:, :], AF.Silu, bias=FA[:, F1B + oc:F1B + oc + 1], sem=acts)
                    k.w("vector", acts, acts.v)
                    k.w("vector", dve, gb_done)
                    gk = e * 256 + g
                    if gk >= 2:
                        k.w("vector", st, hstore.get(gk - 2, 0))
                    hs = BF[:, HST[g % 2]: HST[g % 2] + 512]
                    k.vop("tensor_mul", hs, tmp, gb, sem=dve)
                    ga_done[g] = dve.v
                    k.w("gpsimd", dve, dve.v)
                    k.dma("gpsimd", dd["hbA"][oc * 128:(oc + 1) * 128, c * 512:(c + 1) * 512],
                          hs, st)
                    hstore[gk] = st.v
                xn_read[c] = pe.v
            pha_done[e] = pe.v
            self.fence(k, s)
            yb_done = {}
            hread = {}
            for c in range(8):
                if c >= 1:
                    k.w("sync", pe, hread[c - 1])
                for half in range(2):
                    for q in range(2):
                        k.dma("sync", sb3(BF, HC[half] + q * 4096, 8, 512),
                              ap3(dd["hbA"], half * 2048 + q * 1024, 8, c * 512, 512), ld)
                ldv = ld.v
                for m in range(8):
                    g = c * 8 + m
                    bank = self.ps[g % 2]
                    for sl in range(32):
                        if sl == 0:
                            k.w("tensor", ld, ldv)
                            k.w("tensor", wt, wb_done[e])
                            if g >= 2:
                                k.w("tensor", dve, yb_done[g - 2])
                        half, sl16 = sl // 16, sl % 16
                        k.mm(bank[:, :],
                             BF[:, WB + sl * 1024 + m * 128: WB + sl * 1024 + (m + 1) * 128],
                             BF[:, HC[half] + sl16 * 512: HC[half] + (sl16 + 1) * 512],
                             start=(sl == 0), stop=(sl == 31), sem=(pe if sl == 31 else None))
                    mmv = pe.v
                    k.w("vector", pe, mmv)
                    gk = e * 64 + g
                    if gk >= 2:
                        k.w("vector", st, ystore.get(gk - 2, 0))
                    ys = BF[:, YST[g % 2]: YST[g % 2] + 512]
                    k.vop("tensor_copy", ys, bank[:, :], sem=dve)
                    yb_done[g] = dve.v
                    k.w("gpsimd", dve, dve.v)
                    k.dma("gpsimd",
                          dd["ye"][e * 1024 + m * 128: e * 1024 + (m + 1) * 128,
                                   c * 512:(c + 1) * 512], ys, st)
                    ystore[gk] = st.v
                hread[c] = pe.v
            phb_done[e] = pe.v
            self.fence(k, s)

        CH = 256
        YEC = [0, 2048]
        RES, ACC, XNF = 0, 2048, 3072
        X3C, OSG, SQT, RWS = 3328, 3840, 4352, 4920
        k.dma("gpsimd", FA[:, RWS + 800:RWS + 808], dd["gammac"][:, :], wt)
        k.dma("gpsimd", FA[:, RWS + 808:RWS + 816], dd["betac"][:, :], wt)
        wt_f = wt.v
        GAM = RWS + 800
        res_done = {}
        outst = {}
        sqd = {}
        for c in range(16):
            par = c % 2
            for m in range(8):
                g = c * 8 + m
                if g >= 2:
                    k.w("sync", dve, res_done.get(("acc", g - 2), 0))
                for e in range(NE):
                    k.dma("sync", BF[:, YEC[g % 2] + e * 256: YEC[g % 2] + (e + 1) * 256],
                          dd["ye"][e * 1024 + m * 128: e * 1024 + (m + 1) * 128,
                                   c * 256:(c + 1) * 256], ld)
                k.dma("sync", FA[:, XNF:XNF + 256],
                      dd["xn3f"][m * 128:(m + 1) * 128, c * 256:(c + 1) * 256], ld)
                ldv = ld.v
                if g >= 2:
                    k.w("tensor", dve, res_done.get(("bias", g - 2), 0))
                    if m < 2 and c >= 1:
                        k.w("tensor", dve, res_done.get(("x3", c * 8 - 1), 0))
                k.mm(self.ps[4 + g % 2][:, 0:256],
                     self.fc2b_sb[0:8, m * 128:(m + 1) * 128],
                     BF[0:8, GTC + c * 256: GTC + (c + 1) * 256], start=True, stop=True, sem=pe)
                biasv = pe.v
                k.w("vector", ld, ldv)
                ye0 = YEC[g % 2]
                a_ = FA[:, ACC:ACC + 256]
                b_ = FA[:, ACC + 256:ACC + 512]
                c_ = FA[:, ACC + 512:ACC + 768]
                d_ = FA[:, ACC + 768:ACC + 1024]
                k.vop("tensor_add", a_, BF[:, ye0:ye0 + 256], BF[:, ye0 + 256:ye0 + 512])
                k.vop("tensor_add", b_, BF[:, ye0 + 512:ye0 + 768], BF[:, ye0 + 768:ye0 + 1024])
                k.vop("tensor_add", c_, BF[:, ye0 + 1024:ye0 + 1280], BF[:, ye0 + 1280:ye0 + 1536])
                k.vop("tensor_add", d_, BF[:, ye0 + 1536:ye0 + 1792], BF[:, ye0 + 1792:ye0 + 2048])
                k.vop("tensor_add", a_, a_, b_)
                k.vop("tensor_add", c_, c_, d_)
                k.vop("tensor_add", a_, a_, c_)
                k.w("vector", pe, biasv)
                k.vop("tensor_add", a_, a_, self.ps[4 + g % 2][:, 0:256], sem=dve)
                res_done[("bias", g)] = dve.v
                rm = FA[:, RES + m * 256: RES + (m + 1) * 256]
                k.vop("tensor_add", rm, a_, FA[:, XNF:XNF + 256], sem=dve)
                res_done[("acc", g)] = dve.v
            res_v = dve.v
            bsum, bsq = self.ps[par], self.ps[2 + par]
            for m in range(8):
                if m == 0:
                    k.w("scalar", dve, res_v)
                sqb = FA[:, SQT + (m % 2) * 256: SQT + (m % 2) * 256 + 256]
                if (c, m - 2) in sqd:
                    k.w("scalar", pe, sqd[(c, m - 2)])
                elif c >= 1 and (c - 1, m + 6) in sqd:
                    k.w("scalar", pe, sqd[(c - 1, m + 6)])
                k.act(sqb, FA[:, RES + m * 256: RES + (m + 1) * 256], AF.Square, sem=acts)
                sqv = acts.v
                if m == 0:
                    k.w("tensor", dve, res_v)
                    if c >= 2:
                        k.w("tensor", dve, res_done[("stats", c - 2)])
                k.mm(bsum[0:1, 256:256 + CH], self.onesf[:, 0:1],
                     FA[:, RES + m * 256: RES + (m + 1) * 256], start=(m == 0), stop=(m == 7))
                k.w("tensor", acts, sqv)
                k.mm(bsq[0:1, 256:256 + CH], self.onesf[:, 0:1], sqb,
                     start=(m == 0), stop=(m == 7), sem=pe)
                sqd[(c, m)] = pe.v
            stat_pe = pe.v
            mu = FA[0:1, RWS:RWS + CH]
            v1 = FA[0:1, RWS + 256:RWS + 256 + CH]
            rs = FA[0:1, RWS + 512:RWS + 512 + CH]
            k.w("vector", pe, stat_pe)
            if c >= 2:
                k.w("vector", pe, res_done[("bcast", c - 2)])
            k.vop("tensor_scalar_mul", mu, bsum[0:1, 256:256 + CH], 1.0 / 1024.0, sem=dve)
            vsync()
            k.vop("tensor_mul", v1, mu, mu, sem=dve)
            vsync()
            k.vop("scalar_tensor_tensor", v1, bsq[0:1, 256:256 + CH], 1.0 / 1024.0, v1,
                  op0=ALU.mult, op1=ALU.subtract, sem=dve)
            k.w("scalar", dve, dve.v)
            k.act(v1, v1, AF.Sqrt, bias=self.smallf[0:1, 0:1], sem=acts)
            k.w("vector", acts, acts.v)
            k.vop("reciprocal", rs, v1, sem=dve)
            res_done[("stats", c)] = dve.v
            k.w("tensor", dve, res_done[("stats", c)])
            if c >= 2:
                k.w("tensor", dve, res_done[("x3", (c - 1) * 8 + 7)])
            k.mm(self.ps[4][:, 256:256 + CH], self.onesf[0:1, :], mu, start=True, stop=True)
            k.mm(self.ps[5][:, 256:256 + CH], self.onesf[0:1, :], rs, start=True, stop=True,
                 sem=pe)
            bc_pe = pe.v
            res_done[("bcast", c)] = bc_pe
            k.w("vector", pe, bc_pe)
            for m in range(8):
                g = c * 8 + m
                x3b = FA[:, X3C + (m % 2) * 256: X3C + (m % 2) * 256 + 256]
                if g >= 2:
                    k.w("sync", dve, res_done.get(("x3", g - 2), 0))
                k.dma("sync", x3b, dd[src][m * 128:(m + 1) * 128, c * 256:(c + 1) * 256], ld)
                ldx = ld.v
                rm = FA[:, RES + m * 256: RES + (m + 1) * 256]
                tm = FA[:, ACC:ACC + 256]
                t2 = FA[:, ACC + 256:ACC + 512]
                k.vop("tensor_sub", tm, rm, self.ps[4][:, 256:256 + CH])
                k.vop("tensor_mul", t2, tm, self.ps[5][:, 256:256 + CH])
                k.vop("tensor_scalar", tm, t2, FA[:, GAM + m:GAM + m + 1],
                      FA[:, GAM + 8 + m:GAM + 8 + m + 1], op0=ALU.mult, op1=ALU.add)
                k.w("vector", ld, ldx)
                k.w("vector", wt, wt_f)
                if g >= 2:
                    k.w("vector", st, outst.get(g - 2, 0))
                osg = FA[:, OSG + (m % 2) * 256: OSG + (m % 2) * 256 + 256]
                k.vop("tensor_add", osg, tm, x3b, sem=dve)
                res_done[("x3", g)] = dve.v
                k.w("gpsimd", dve, dve.v)
                k.dma("gpsimd", dd[dst][m * 128:(m + 1) * 128, c * 256:(c + 1) * 256], osg, st)
                outst[g] = st.v
        k.w("gpsimd", st, st.v)

    def build_in_maps(self, w, percore):
        maps = []
        for b in range(NCORES):
            m = {}
            for nm, sh, dty in WSPECS:
                if nm == "coremask":
                    m[nm] = percore[b]["coremask"]
                else:
                    m[nm] = w[nm]
            m["xT"] = percore[b]["xT"]
            maps.append(m)
        return maps


def run_stages(stages, inputs, out_name="outT", dbg_out=(), raw=False):
    w, percore = prep_host(inputs)
    p = Prog(stages, dbg_out=dbg_out)
    maps = p.build_in_maps(w, percore)
    res = run_bass_kernel_spmd(p.nc, maps, core_ids=list(range(NCORES)))
    if raw:
        return res
    return np.stack([np.asarray(r[out_name], dtype=np.float32).T for r in res.results])


def kernel(**inputs):
    stages = [
        ("ln", "xT", "xnS"),
        ("l0a", "xT", "x1"),
        ("ln", "x1", "xnS"),
        ("l0f", "x1", "x2"),
        ("ln", "x2", "xnS"),
        ("l1a", "x2", "x3"),
        ("ln", "x3", "xnS", "xn3f"),
        ("moe", "x3", "outT"),
    ]
    return run_stages(stages, inputs)


# version 03
# speedup vs baseline: 1.2539x; 1.2539x over previous
import numpy as np
import ml_dtypes
from contextlib import ExitStack
import concourse.bass as bass
import concourse.mybir as mybir
from concourse.bass_utils import run_bass_kernel_spmd

F32 = mybir.dt.float32
BF16 = mybir.dt.bfloat16
AF = mybir.ActivationFunctionType
ALU = mybir.AluOpType
AX = mybir.AxisListType
BFNP = ml_dtypes.bfloat16

D = 1024
T = 4096
NCORES = 8
HID = 4096
NH, HD = 16, 64
NE = 8
CAP = 10240
MOE_EPS = 1.1920929e-07

ENGS = ("sync", "tensor", "scalar", "vector", "gpsimd")


class Sem:
    __slots__ = ("h", "v")

    def __init__(self, h):
        self.h = h
        self.v = 0


class K:

    def __init__(self, nc, active, eng):
        self.nc = nc
        self.active = active
        self.eng = eng

    def w(self, engname, sem, val):
        if val > 0 and self.active == engname:
            self.eng.wait_ge(sem.h, val)

    def dma(self, engname, dst, src, sem, **kw):
        sem.v += 16
        if self.active == engname:
            self.eng.dma_start(dst, src, **kw).then_inc(sem.h, 16)
        return sem.v

    def mm(self, out, lhsT, rhs, start, stop, sem=None, tp=None):
        if sem is not None:
            sem.v += 1
        if self.active == "tensor":
            kw = {}
            if tp is not None:
                kw["tile_position"] = tp
            i = self.eng.matmul(out, lhsT, rhs, start=start, stop=stop, **kw)
            if sem is not None:
                i.then_inc(sem.h, 1)

    def tr(self, out, in_, ident, sem=None):
        if sem is not None:
            sem.v += 1
        if self.active == "tensor":
            i = self.eng.transpose(out, in_, ident)
            if sem is not None:
                i.then_inc(sem.h, 1)

    def act(self, out, in_, func, bias=0.0, scale=1.0, accum=None, sem=None):
        if sem is not None:
            sem.v += 1
        if self.active == "scalar":
            kw = {}
            if accum is not None:
                kw["accum_out"] = accum
            i = self.eng.activation(out, in_, func, bias=bias, scale=scale, **kw)
            if sem is not None:
                i.then_inc(sem.h, 1)

    def vop(self, meth, *args, sem=None, **kw):
        if sem is not None:
            sem.v += 1
        if self.active == "vector":
            i = getattr(self.eng, meth)(*args, **kw)
            if sem is not None:
                i.then_inc(sem.h, 1)

    def gop(self, meth, *args, sem=None, inc=1, **kw):
        if sem is not None:
            sem.v += inc
        if self.active == "gpsimd":
            i = getattr(self.eng, meth)(*args, **kw)
            if sem is not None:
                i.then_inc(sem.h, inc)

    def barrier(self, sbar):
        sbar.v += len(ENGS)
        tgt = sbar.v
        if self.eng is not None:
            self.eng.nop().then_inc(sbar.h, 1)
            self.eng.wait_ge(sbar.h, tgt)


def ap3(dram, r0, nslab, c0, ncols):
    return dram[r0:r0 + nslab * 128, c0:c0 + ncols].rearrange(
        "(n p) c -> p n c", p=128)


def sb3(sbt, col0, nslab, ncols):
    return sbt[:, col0:col0 + nslab * ncols].rearrange(
        "p (n c) -> p n c", c=ncols)



def col128(v):
    return np.ascontiguousarray(v.reshape(-1, 128).T)


def prep_host(inputs):
    f = np.ascontiguousarray
    w = {}
    w["Wq"] = f(inputs["L0_attn_Wq"])
    w["Wk"] = f(inputs["L0_attn_Wk"])
    w["Wv"] = f(inputs["L0_attn_Wv"])
    w["bq8"] = col128(inputs["L0_attn_bq"])
    w["bk8"] = col128(inputs["L0_attn_bk"])
    w["bv"] = f(inputs["L0_attn_bv"].reshape(1, D))
    w["w1T"] = f(inputs["L0_ffn_w1_w"].T)
    w["w2T"] = f(inputs["L0_ffn_w2_w"].T)
    w["h1T"] = f(inputs["L0_ffn_h1_w"].T)
    w["h2T"] = f(inputs["L0_ffn_h2_w"].T)
    w["w3T"] = f(inputs["L0_ffn_w3_w"].T)
    w["s1c"] = col128(inputs["L0_ffn_w1_s"])
    w["s2c"] = col128(inputs["L0_ffn_w2_s"])
    w["sh1c"] = col128(inputs["L0_ffn_h1_s"])
    w["sh2c"] = col128(inputs["L0_ffn_h2_s"])
    w["s3c"] = col128(inputs["L0_ffn_w3_s"])
    w["Wqkv"] = f(inputs["L1_attn_Wqkv"])
    w["Wproj"] = f(inputs["L1_attn_Wproj"])
    w["bprojc"] = col128(inputs["L1_attn_bproj"])
    w["qs_row"] = f(inputs["L1_attn_q_scale"].reshape(1, HD))
    w["ks_row"] = f(inputs["L1_attn_k_scale"].reshape(1, HD))
    w["wg"] = f(inputs["L1_ffn_wg"])
    w["fc1T"] = f(np.transpose(inputs["L1_ffn_fc1_w"], (0, 2, 1)).reshape(NE * D, HID))
    w["fc2"] = f(inputs["L1_ffn_fc2_w"].reshape(NE * HID, D))
    w["fc1b"] = f(np.concatenate([col128(inputs["L1_ffn_fc1_b"][e]) for e in range(NE)], axis=0))
    w["fc2b"] = f(inputs["L1_ffn_fc2_b"])
    oh = np.zeros((NE, NE * 128), dtype=BFNP)
    for e in range(NE):
        oh[e, e * 128:(e + 1) * 128] = 1.0
    w["ohot"] = oh
    w["gammac"] = col128(inputs["L1_ffn_ln_gamma"])
    w["betac"] = col128(inputs["L1_ffn_ln_beta"])

    w["identb"] = np.eye(128, dtype=BFNP)
    w["identf"] = np.eye(128, dtype=np.float32)
    p = np.arange(128)[:, None]
    c = np.arange(384)[None, :]
    w["cmask"] = (p <= c - 128).astype(BFNP)
    w["ltri"] = np.triu(np.ones((128, 128), dtype=np.float32))
    w["eidxm8"] = np.broadcast_to(
        (np.arange(8, dtype=np.float32) - 8.0)[None, :], (128, 8)).copy()

    percore = []
    x = inputs["L0_attn_x"]
    for b in range(NCORES):
        cm = np.zeros((8, 1), dtype=np.float32)
        cm[:b] = 1.0
        percore.append({"xT": f(x[b].T.astype(np.float32)), "coremask": cm})
    return w, percore


WSPECS = [
    ("Wq", [D, D], F32), ("Wk", [D, D], F32), ("Wv", [D, D], F32),
    ("bq8", [128, 8], F32), ("bk8", [128, 8], F32), ("bv", [1, D], F32),
    ("w1T", [D, HID], F32), ("w2T", [D, HID], F32),
    ("h1T", [HID, HID], F32), ("h2T", [HID, HID], F32), ("w3T", [HID, D], F32),
    ("s1c", [128, 32], F32), ("s2c", [128, 32], F32), ("sh1c", [128, 32], F32),
    ("sh2c", [128, 32], F32), ("s3c", [128, 8], F32),
    ("Wqkv", [D, 3 * D], F32), ("Wproj", [D, D], F32), ("bprojc", [128, 8], F32),
    ("qs_row", [1, HD], F32), ("ks_row", [1, HD], F32),
    ("wg", [D, NE], F32), ("fc1T", [NE * D, HID], F32), ("fc2", [NE * HID, D], F32),
    ("fc1b", [NE * 128, 32], F32), ("fc2b", [NE, D], F32),
    ("gammac", [128, 8], F32), ("betac", [128, 8], F32),
    ("identf", [128, 128], F32), ("ltri", [128, 128], F32),
    ("eidxm8", [128, 8], F32), ("coremask", [8, 1], F32),
    ("identb", [128, 128], BF16), ("cmask", [128, 384], BF16),
    ("ohot", [NE, NE * 128], BF16),
]

BF_COLS = 88064
FA_COLS = 7168


class Prog:
    def __init__(self, stages, ext_in="xT", ext_out="outT", dbg_out=()):
        self.stages = stages
        self.dbg_out = set(dbg_out)
        nc = bass.Bass("TRN2", target_bir_lowering=False)
        self.nc = nc
        self.d = {}

        def dram(name, shape, dtype, kind):
            self.d[name] = nc.dram_tensor(name, shape, dtype, kind=kind).ap()

        dram("xT", [D, T], F32, "ExternalInput")
        dram("outT", [D, T], F32, "ExternalOutput")
        for nm, sh, dty in WSPECS:
            dram(nm, sh, dty, "ExternalInput")
        for nm, sh, dty in [
                ("x1", [D, T], F32), ("x2", [D, T], F32), ("x3", [D, T], F32),
                ("xnS", [D, T], BF16), ("xn3f", [D, T], F32),
                ("l0qt", [D, T], BF16), ("l0kt", [D, T], BF16), ("l0ou", [D, T], F32),
                ("hbA", [HID, T], BF16), ("hbB", [HID, T], BF16),
                ("l1qt", [128, 8 * T], BF16), ("l1kt", [128, 8 * T], BF16),
                ("l1ou", [128, 8 * T], BF16),
                ("ccin", [1, 128], F32), ("ccout", [8, 128], F32),
                ("ye", [NE * 128, 8 * T], BF16), ("gtd", [8, T], BF16),
                ("resd", [D, T], F32)]:
            dram(nm, sh, dty, "ExternalOutput" if nm in self.dbg_out else "Internal")

        self.stack = ExitStack()
        e = self.stack.enter_context
        self.BF = e(nc.sbuf_tensor("BF", [128, BF_COLS], BF16))
        self.FA = e(nc.sbuf_tensor("FA", [128, FA_COLS], F32))
        self.identb = e(nc.sbuf_tensor("identb_sb", [128, 128], BF16))
        self.identf = e(nc.sbuf_tensor("identf_sb", [128, 128], F32))
        self.onesf = e(nc.sbuf_tensor("onesf_sb", [128, 128], F32))
        self.cmask = e(nc.sbuf_tensor("cmask_sb", [128, 384], BF16))
        self.ltri = e(nc.sbuf_tensor("ltri_sb", [128, 128], F32))
        self.eidxm8 = e(nc.sbuf_tensor("eidxm8_sb", [128, 8], F32))
        self.smallf = e(nc.sbuf_tensor("smallf_sb", [128, 8], F32))
        self.ohot_sb = e(nc.sbuf_tensor("ohot_sb", [8, NE * 128], BF16))
        self.fc2b_sb = e(nc.sbuf_tensor("fc2b_sb", [8, D], BF16))
        self.ps = [e(nc.psum_tensor(f"ps{j}", [128, 512], F32)) for j in range(6)]
        self.pb = [e(nc.psum_tensor(f"pb{j}", [128, 1024], BF16)) for j in range(2)]
        self.sems = {}
        for pos in range(len(stages) + 1):
            for nm in ("LD", "WT", "PE", "ACT", "DVE", "ST", "GP"):
                key = f"{nm}{pos}"
                self.sems[key] = Sem(e(nc.semaphore(key)))
        self.sems["BAR"] = Sem(e(nc.semaphore("BAR")))

        with nc.Block() as blk:
            for name, meth in (("sync", blk.sync), ("tensor", blk.tensor),
                               ("scalar", blk.scalar), ("vector", blk.vector),
                               ("gpsimd", blk.gpsimd)):
                def body(eng, name=name):
                    for s in self.sems.values():
                        s.v = 0
                    self.walk(K(nc, name, eng))
                meth(body)
        self.stack.close()

    def walk(self, k):
        sm = self.sems
        pre = sm["LD0"]
        k.dma("gpsimd", self.identb[:, :], self.d["identb"][:, :], pre)
        k.dma("gpsimd", self.identf[:, :], self.d["identf"][:, :], pre)
        k.dma("gpsimd", self.cmask[:, :], self.d["cmask"][:, :], pre)
        k.dma("gpsimd", self.ltri[:, :], self.d["ltri"][:, :], pre)
        k.dma("gpsimd", self.eidxm8[:, :], self.d["eidxm8"][:, :], pre)
        k.gop("memset", self.onesf[:, :], 1.0, sem=sm["GP0"])
        k.gop("memset", self.smallf[:, 0:1], 1e-5, sem=sm["GP0"])
        for e in ("vector", "tensor", "scalar"):
            k.w(e, pre, pre.v)
            k.w(e, sm["GP0"], sm["GP0"].v)
        k.barrier(sm["BAR"])
        for idx, st in enumerate(self.stages, start=1):
            s = {nm: sm[f"{nm}{idx}"] for nm in ("LD", "WT", "PE", "ACT", "DVE", "ST", "GP")}
            getattr(self, "st_" + st[0])(k, s, *st[1:])
            k.barrier(sm["BAR"])

    def fence(self, k, s):
        k.w("gpsimd", s["ST"], s["ST"].v)
        k.w("gpsimd", s["WT"], s["WT"].v)
        k.w("sync", s["LD"], s["LD"].v)
        k.barrier(self.sems["BAR"])

    def st_ln(self, k, s, src, dstb, dstf=None):
        CH = 128
        NCH = T // CH
        FA, BF = self.FA, self.BF
        src_d, dstb_d = self.d[src], self.d[dstb]
        dstf_d = self.d[dstf] if dstf else None
        XB = [0, 1024]
        SQ = [2048, 2176]
        TMP = 2304
        FST = [2432, 3456]
        ROWS = 4480
        ld, pe, acts, dve, st = s["LD"], s["PE"], s["ACT"], s["DVE"], s["ST"]

        mmsq = {}
        stats_done = [0] * NCH
        bcast_done = [0] * NCH
        apply_done = [0] * NCH
        store_done = [0] * NCH
        for c in range(NCH):
            par = c % 2
            mu = FA[0:1, ROWS + par * 512: ROWS + par * 512 + CH]
            v1 = FA[0:1, ROWS + par * 512 + 128: ROWS + par * 512 + 128 + CH]
            rs = FA[0:1, ROWS + par * 512 + 256: ROWS + par * 512 + 256 + CH]
            sd = FA[0:1, ROWS + par * 512 + 384: ROWS + par * 512 + 384 + CH]
            if c >= 2:
                k.w("sync", dve, apply_done[c - 2])
            k.dma("sync", sb3(FA, XB[par], 8, CH), ap3(src_d, 0, 8, c * CH, CH), ld)
            ldv = ld.v
            bsum, bsq = self.ps[par], self.ps[2 + par]
            bmu, brs = self.ps[4], self.ps[5]
            for sl in range(8):
                xsl = FA[:, XB[par] + sl * CH: XB[par] + (sl + 1) * CH]
                sqb = FA[:, SQ[sl % 2]: SQ[sl % 2] + CH]
                if sl == 0:
                    k.w("scalar", ld, ldv)
                    if c >= 1:
                        k.w("scalar", pe, mmsq[(c - 1, 7)])
                elif sl in (1,) and c >= 1:
                    k.w("scalar", pe, mmsq[(c - 1, 7)])
                elif sl >= 2:
                    k.w("scalar", pe, mmsq[(c, sl - 2)])
                k.act(sqb, xsl, AF.Square, sem=acts)
                sqv = acts.v
                if sl == 0:
                    k.w("tensor", ld, ldv)
                    if c >= 2:
                        k.w("tensor", dve, stats_done[c - 2])
                k.mm(bsum[0:1, 0:CH], self.onesf[:, 0:1], xsl,
                     start=(sl == 0), stop=(sl == 7))
                k.w("tensor", acts, sqv)
                k.mm(bsq[0:1, 0:CH], self.onesf[:, 0:1], sqb,
                     start=(sl == 0), stop=(sl == 7), sem=pe)
                mmsq[(c, sl)] = pe.v
            k.w("vector", pe, mmsq[(c, 7)])
            if c >= 2:
                k.w("vector", pe, bcast_done[c - 2])
            k.vop("tensor_scalar_mul", mu, bsum[0:1, 0:CH], 1.0 / 1024.0, sem=dve)
            k.w("vector", dve, dve.v)
            k.vop("tensor_mul", v1, mu, mu, sem=dve)
            k.w("vector", dve, dve.v)
            k.vop("scalar_tensor_tensor", v1, bsq[0:1, 0:CH], 1.0 / 1024.0, v1,
                  op0=ALU.mult, op1=ALU.subtract, sem=dve)
            k.w("scalar", dve, dve.v)
            k.act(sd, v1, AF.Sqrt, bias=self.smallf[0:1, 0:1], sem=acts)
            k.w("vector", acts, acts.v)
            k.vop("reciprocal", rs, sd, sem=dve)
            stats_done[c] = dve.v
            if c >= 1:
                k.w("tensor", dve, apply_done[c - 1])
            k.w("tensor", dve, stats_done[c])
            k.mm(bmu[:, 0:CH], self.onesf[0:1, :], mu, start=True, stop=True)
            k.mm(brs[:, 0:CH], self.onesf[0:1, :], rs, start=True, stop=True, sem=pe)
            bcast_done[c] = pe.v
            k.w("vector", pe, bcast_done[c])
            if c >= 2:
                k.w("vector", st, store_done[c - 2])
            for sl in range(8):
                xsl = FA[:, XB[par] + sl * CH: XB[par] + (sl + 1) * CH]
                tm = FA[:, TMP: TMP + CH]
                ob = BF[:, par * 1024 + sl * CH: par * 1024 + (sl + 1) * CH]
                k.vop("tensor_sub", tm, xsl, bmu[:, 0:CH])
                k.vop("tensor_mul", ob, tm, brs[:, 0:CH],
                      sem=(dve if (sl == 7 and not dstf) else None))
                if dstf:
                    fo = FA[:, FST[par] + sl * CH: FST[par] + (sl + 1) * CH]
                    k.vop("tensor_mul", fo, tm, brs[:, 0:CH],
                          sem=(dve if sl == 7 else None))
            apply_done[c] = dve.v
            k.w("gpsimd", dve, apply_done[c])
            k.dma("gpsimd", ap3(dstb_d, 0, 8, c * CH, CH), sb3(BF, par * 1024, 8, CH), st)
            if dstf:
                k.dma("gpsimd", ap3(dstf_d, 0, 8, c * CH, CH), sb3(FA, FST[par], 8, CH), st)
            store_done[c] = st.v
        k.w("gpsimd", st, store_done[NCH - 1])

    def st_l0a(self, k, s, src, dst):
        FA, BF = self.FA, self.BF
        dd = self.d
        ld, wt, pe, acts, dve, st = s["LD"], s["WT"], s["PE"], s["ACT"], s["DVE"], s["ST"]
        WQ, WK, WV = 0, 8192, 16384
        XN = [24576, 28672]
        VT = 32768
        QS = [65536, 66048]
        KS = [66560, 67072]
        BQ8, BK8 = 0, 8
        BVB = 16
        k.dma("gpsimd", sb3(BF, WQ, 8, 1024), ap3(dd["Wq"], 0, 8, 0, 1024), wt)
        k.dma("gpsimd", sb3(BF, WK, 8, 1024), ap3(dd["Wk"], 0, 8, 0, 1024), wt)
        k.dma("gpsimd", sb3(BF, WV, 8, 1024), ap3(dd["Wv"], 0, 8, 0, 1024), wt)
        k.dma("gpsimd", FA[:, BQ8:BQ8 + 8], dd["bq8"][:, :], wt)
        k.dma("gpsimd", FA[:, BK8:BK8 + 8], dd["bk8"][:, :], wt)
        k.dma("gpsimd", FA[0:1, BVB:BVB + D], dd["bv"][:, :], wt)
        wt_all = wt.v
        k.w("tensor", wt, wt_all)
        k.mm(self.ps[4][:, :], self.onesf[0:1, :], FA[0:1, BVB:BVB + 512],
             start=True, stop=True)
        k.mm(self.ps[5][:, :], self.onesf[0:1, :], FA[0:1, BVB + 512:BVB + 1024],
             start=True, stop=True, sem=pe)
        k.w("vector", pe, pe.v)
        k.vop("tensor_copy", FA[:, BVB:BVB + 512], self.ps[4][:, :])
        k.vop("tensor_copy", FA[:, BVB + 512:BVB + 1024], self.ps[5][:, :], sem=dve)
        bvb_done = dve.v

        q_evac = {}
        v_evac = {}
        qk_store = {}
        xn_read = {}
        for c in range(8):
            par = c % 2
            if c >= 2:
                k.w("sync", pe, xn_read[c - 2])
            k.dma("sync", sb3(BF, XN[par], 8, 512), ap3(dd["xnS"], 0, 8, c * 512, 512), ld)
            ldv = ld.v
            for mi in range(16):
                which = "q" if mi < 8 else "k"
                m = mi % 8
                woff = WQ if which == "q" else WK
                boff = BQ8 if which == "q" else BK8
                qdram = dd["l0qt"] if which == "q" else dd["l0kt"]
                stg = QS if which == "q" else KS
                g = c * 16 + mi
                bank = self.ps[g % 2]
                for sl in range(8):
                    if sl == 0:
                        k.w("tensor", ld, ldv)
                        if g == 0:
                            k.w("tensor", wt, wt_all)
                        if g >= 2:
                            k.w("tensor", dve, q_evac[g - 2])
                    k.mm(bank[:, :], BF[:, woff + sl * 1024 + m * 128: woff + sl * 1024 + (m + 1) * 128],
                         BF[:, XN[par] + sl * 512: XN[par] + (sl + 1) * 512],
                         start=(sl == 0), stop=(sl == 7), sem=(pe if sl == 7 else None))
                mmv = pe.v
                k.w("vector", pe, mmv)
                if g >= 2:
                    k.w("vector", st, qk_store[g - 2])
                k.vop("tensor_scalar_add", BF[:, stg[m % 2]: stg[m % 2] + 512],
                      bank[:, :], FA[:, boff + m: boff + m + 1], sem=dve)
                q_evac[g] = dve.v
                k.w("gpsimd", dve, dve.v)
                k.dma("gpsimd", qdram[m * 128:(m + 1) * 128, c * 512:(c + 1) * 512],
                      BF[:, stg[m % 2]: stg[m % 2] + 512], st)
                qk_store[g] = st.v
            for sub in range(4):
                tile = c * 4 + sub
                for half in range(2):
                    gv = c * 8 + sub * 2 + half
                    bank = self.ps[2 + gv % 2]
                    for sl in range(8):
                        if sl == 0:
                            k.w("tensor", ld, ldv)
                            if gv >= 2:
                                k.w("tensor", dve, v_evac[gv - 2])
                        k.mm(bank[:, :],
                             BF[:, XN[par] + sl * 512 + sub * 128: XN[par] + sl * 512 + (sub + 1) * 128],
                             BF[:, WV + sl * 1024 + half * 512: WV + sl * 1024 + (half + 1) * 512],
                             start=(sl == 0), stop=(sl == 7), sem=(pe if sl == 7 else None))
                    mmv = pe.v
                    k.w("vector", pe, mmv)
                    if gv == 0:
                        k.w("vector", dve, bvb_done)
                    k.vop("tensor_add", BF[:, VT + tile * 1024 + half * 512: VT + tile * 1024 + (half + 1) * 512],
                          bank[:, :], FA[:, BVB + half * 512: BVB + (half + 1) * 512], sem=dve)
                    v_evac[gv] = dve.v
            xn_read[c] = pe.v
        p0_pe = pe.v
        p0_dve = dve.v
        p0_st = st.v

        KT = 0
        EB = 65536
        QB = 74240
        ZC, OUS, ZCOL, RZC = 1152, 1760, 2304, 2320
        k.w("gpsimd", st, p0_st)
        k.w("gpsimd", dve, p0_dve)
        k.dma("gpsimd", sb3(BF, KT, 8, 4096), ap3(dd["l0kt"], 0, 8, 0, 4096), wt)
        kt_done = wt.v
        k.vop("memset", FA[:, ZC:ZC + 544], 0.0, sem=dve)
        zc_zero = dve.v
        zci = 0
        av_done = [0] * 16
        exp_done = [0] * 16
        s_done = [0] * 16
        ou_stored = {}
        ou_evac = {}
        for i in range(16):
            nj = 2 * i + 2
            if i >= 1:
                k.w("sync", pe, s_done[i - 1])
            k.dma("sync", sb3(BF, QB, 8, 256), ap3(dd["l0qt"], 0, 8, i * 256, 256), ld)
            qldv = ld.v
            if i >= 1:
                k.w("scalar", pe, av_done[i - 1])
            for j in range(nj):
                bank = self.ps[j % 2]
                for sl in range(8):
                    if sl == 0:
                        k.w("tensor", ld, qldv)
                        if i == 0 and j == 0:
                            k.w("tensor", wt, kt_done)
                        if j >= 2:
                            k.w("tensor", acts, exp_j[j - 2])
                        elif i >= 1:
                            k.w("tensor", acts, exp_done[i - 1])
                    k.mm(bank[:, 0:256],
                         BF[:, KT + sl * 4096 + j * 128: KT + sl * 4096 + (j + 1) * 128],
                         BF[:, QB + sl * 256: QB + (sl + 1) * 256],
                         start=(sl == 0), stop=(sl == 7), sem=(pe if sl == 7 else None))
                if j == 0:
                    exp_j = {}
                smm = pe.v
                k.w("scalar", pe, smm)
                diag = (j >= 2 * i)
                etile = BF[:, EB + j * 256: EB + (j + 1) * 256]
                if not diag:
                    if zci == 0:
                        k.w("scalar", dve, zc_zero)
                    k.act(etile, bank[:, 0:256], AF.Exp, scale=1.0 / 32.0,
                          accum=FA[:, ZC + zci: ZC + zci + 1], sem=acts)
                else:
                    k.act(etile, bank[:, 0:256], AF.Exp, scale=1.0 / 32.0, sem=acts)
                    k.w("vector", acts, acts.v)
                    moff = 128 if j == 2 * i else 0
                    k.vop("tensor_mul", etile, etile, self.cmask[:, moff:moff + 256])
                    k.w("vector", dve, zc_zero)
                    k.vop("reduce_sum", FA[:, ZC + zci: ZC + zci + 1], etile,
                          axis=AX.X, sem=dve)
                exp_j[j] = acts.v
                zci += 1
            s_done[i] = pe.v
            exp_done[i] = acts.v
            mask_done_i = dve.v
            for m in range(8):
                bank = self.ps[2 + m % 2]
                for j in range(nj):
                    if j == 0:
                        k.w("tensor", acts, exp_done[i])
                        k.w("tensor", dve, mask_done_i)
                        if m >= 2:
                            k.w("tensor", dve, ou_evac[(i, m - 2)])
                        elif i >= 1:
                            k.w("tensor", dve, ou_evac[(i - 1, m + 6)])
                    k.mm(bank[:, 0:256],
                         BF[:, VT + j * 1024 + m * 128: VT + j * 1024 + (m + 1) * 128],
                         BF[:, EB + j * 256: EB + (j + 1) * 256],
                         start=(j == 0), stop=(j == nj - 1), sem=(pe if j == nj - 1 else None))
                avv = pe.v
                k.w("vector", pe, avv)
                idx = i * 8 + m
                if idx >= 2:
                    k.w("vector", st, ou_stored.get(idx - 2, 0))
                ob = FA[:, OUS + (m % 2) * 256: OUS + (m % 2) * 256 + 256]
                k.vop("tensor_copy", ob, bank[:, 0:256], sem=dve)
                ou_evac[(i, m)] = dve.v
                k.w("gpsimd", dve, dve.v)
                k.dma("gpsimd", dd["l0ou"][m * 128:(m + 1) * 128, i * 256:(i + 1) * 256], ob, st)
                ou_stored[idx] = st.v
            av_done[i] = pe.v
        k.w("vector", acts, exp_done[15])
        k.vop("reduce_sum", FA[:, ZCOL:ZCOL + 1], FA[:, ZC:ZC + 544], axis=AX.X, sem=dve)
        k.w("tensor", dve, dve.v)
        k.tr(self.ps[0][0:1, 0:128], FA[:, ZCOL:ZCOL + 1], self.identf[:, :], sem=pe)
        k.w("vector", pe, pe.v)
        k.vop("reduce_sum", FA[0:1, ZCOL + 2:ZCOL + 3], self.ps[0][0:1, 0:128],
              axis=AX.X, sem=dve)
        k.w("vector", dve, dve.v)
        k.vop("reciprocal", FA[0:1, ZCOL + 4:ZCOL + 5], FA[0:1, ZCOL + 2:ZCOL + 3], sem=dve)
        k.w("tensor", dve, dve.v)
        k.mm(self.ps[1][:, 0:1], self.onesf[0:1, :], FA[0:1, ZCOL + 4:ZCOL + 5],
             start=True, stop=True, sem=pe)
        k.w("vector", pe, pe.v)
        k.vop("tensor_copy", FA[:, RZC:RZC + 1], self.ps[1][:, 0:1], sem=dve)
        rz_done = dve.v

        XC, OC, OO = 2560, 3584, 4608
        res_done = {}
        out_done = {}
        k.w("sync", st, ou_stored[127])
        for sl in range(8):
            for h in range(8):
                idx = sl * 8 + h
                par = idx % 2
                xc = FA[:, XC + par * 512: XC + par * 512 + 512]
                oc = FA[:, OC + par * 512: OC + par * 512 + 512]
                oo = FA[:, OO + par * 512: OO + par * 512 + 512]
                if idx >= 2:
                    k.w("sync", dve, res_done[idx - 2])
                k.dma("sync", xc, dd[src][sl * 128:(sl + 1) * 128, h * 512:(h + 1) * 512], ld)
                k.dma("sync", oc, dd["l0ou"][sl * 128:(sl + 1) * 128, h * 512:(h + 1) * 512], ld)
                ldv = ld.v
                k.w("vector", ld, ldv)
                if idx == 0:
                    k.w("vector", dve, rz_done)
                if idx >= 2:
                    k.w("vector", st, out_done[idx - 2])
                k.vop("scalar_tensor_tensor", oo, oc, FA[:, RZC:RZC + 1], xc,
                      op0=ALU.mult, op1=ALU.add, sem=dve)
                res_done[idx] = dve.v
                k.w("gpsimd", dve, dve.v)
                k.dma("gpsimd", dd[dst][sl * 128:(sl + 1) * 128, h * 512:(h + 1) * 512], oo, st)
                out_done[idx] = st.v
        k.w("gpsimd", st, st.v)

    def st_l0f(self, k, s, src, dst):
        FA, BF = self.FA, self.BF
        dd = self.d
        ld, wt, pe, acts, dve, st = s["LD"], s["WT"], s["PE"], s["ACT"], s["DVE"], s["ST"]

        W1, W2 = 0, 32768
        XNC = [65536, 69632]
        HST = [73728, 74240]
        TMPA = [0, 512]
        SC = 1024
        for sl in range(4):
            k.dma("gpsimd", sb3(BF, W1 + sl * 8192, 2, 4096),
                  ap3(dd["w1T"], sl * 256, 2, 0, 4096), wt)
        for sl in range(4):
            k.dma("gpsimd", sb3(BF, W2 + sl * 8192, 2, 4096),
                  ap3(dd["w2T"], sl * 256, 2, 0, 4096), wt)
        k.dma("gpsimd", FA[:, SC:SC + 32], dd["s1c"][:, :], wt)
        k.dma("gpsimd", FA[:, SC + 32:SC + 64], dd["s2c"][:, :], wt)
        wt_a = wt.v
        evac1 = {}
        evac2 = {}
        hstore = {}
        xn_read = {}
        for c in range(8):
            par = c % 2
            if c >= 2:
                k.w("sync", pe, xn_read[c - 2])
            k.dma("sync", sb3(BF, XNC[par], 8, 512), ap3(dd["xnS"], 0, 8, c * 512, 512), ld)
            ldv = ld.v
            for oc in range(32):
                g = c * 32 + oc
                b1, b2 = self.ps[g % 2], self.ps[2 + g % 2]
                for sl in range(8):
                    if sl == 0:
                        k.w("tensor", ld, ldv)
                        if g == 0:
                            k.w("tensor", wt, wt_a)
                        if g >= 2:
                            k.w("tensor", acts, evac1[g - 2])
                            k.w("tensor", dve, evac2[g - 2])
                    k.mm(b1[:, :], BF[:, W1 + sl * 4096 + oc * 128: W1 + sl * 4096 + (oc + 1) * 128],
                         BF[:, XNC[par] + sl * 512: XNC[par] + (sl + 1) * 512],
                         start=(sl == 0), stop=(sl == 7))
                    k.mm(b2[:, :], BF[:, W2 + sl * 4096 + oc * 128: W2 + sl * 4096 + (oc + 1) * 128],
                         BF[:, XNC[par] + sl * 512: XNC[par] + (sl + 1) * 512],
                         start=(sl == 0), stop=(sl == 7), sem=(pe if sl == 7 else None))
                mmv = pe.v
                k.w("scalar", pe, mmv)
                if g >= 2:
                    k.w("scalar", dve, evac2[g - 2])
                tmp = FA[:, TMPA[g % 2]: TMPA[g % 2] + 512]
                k.act(tmp, b1[:, :], AF.Silu, scale=FA[:, SC + oc: SC + oc + 1], sem=acts)
                evac1[g] = acts.v
                k.w("vector", acts, acts.v)
                if g >= 2:
                    k.w("vector", st, hstore[g - 2])
                hs = BF[:, HST[g % 2]: HST[g % 2] + 512]
                k.vop("scalar_tensor_tensor", hs, b2[:, :], FA[:, SC + 32 + oc: SC + 33 + oc],
                      tmp, op0=ALU.mult, op1=ALU.mult, sem=dve)
                evac2[g] = dve.v
                k.w("gpsimd", dve, dve.v)
                k.dma("gpsimd", dd["hbA"][oc * 128:(oc + 1) * 128, c * 512:(c + 1) * 512], hs, st)
                hstore[g] = st.v
            xn_read[c] = pe.v
        self.fence(k, s)

        for (wname, scoff, hin, hout) in (("h1T", 0, "hbA", "hbB"), ("h2T", 32, "hbB", "hbA")):
            WQ = 0
            HC = [32768, 49152]
            GST = [65536, 66048]
            k.dma("gpsimd", FA[:, SC + 64 + scoff: SC + 96 + scoff],
                  dd["sh1c" if wname == "h1T" else "sh2c"][:, :], wt)
            evg = {}
            gst_d = {}
            hread = {}
            for qt in range(4):
                if qt >= 1:
                    k.w("gpsimd", pe, hread[(qt - 1, 7)])
                for sl in range(4):
                    k.dma("gpsimd", sb3(BF, WQ + sl * 8192, 8, 1024),
                          ap3(dd[wname], sl * 1024, 8, qt * 1024, 1024), wt)
                wt_q = wt.v
                for c in range(8):
                    par = c % 2
                    if (qt * 8 + c) >= 2:
                        pc = qt * 8 + c - 2
                        k.w("sync", pe, hread[(pc // 8, pc % 8)])
                    for half in range(4):
                        k.dma("sync", sb3(BF, HC[par] + half * 4096, 8, 512),
                              ap3(dd[hin], half * 1024, 8, c * 512, 512), ld)
                    ldv = ld.v
                    for oc in range(8):
                        g = (qt * 8 + c) * 8 + oc
                        bank = self.ps[g % 2]
                        for sl in range(32):
                            if sl == 0:
                                k.w("tensor", ld, ldv)
                                k.w("tensor", wt, wt_q)
                                if g >= 2:
                                    k.w("tensor", acts, evg[g - 2])
                            k.mm(bank[:, :],
                                 BF[:, WQ + sl * 1024 + oc * 128: WQ + sl * 1024 + (oc + 1) * 128],
                                 BF[:, HC[par] + sl * 512: HC[par] + (sl + 1) * 512],
                                 start=(sl == 0), stop=(sl == 31), sem=(pe if sl == 31 else None))
                        mmv = pe.v
                        k.w("scalar", pe, mmv)
                        if g >= 2:
                            k.w("scalar", st, gst_d[g - 2])
                        gs = BF[:, GST[g % 2]: GST[g % 2] + 512]
                        k.act(gs, bank[:, :], AF.Gelu,
                              scale=FA[:, SC + 64 + scoff + qt * 8 + oc: SC + 65 + scoff + qt * 8 + oc],
                              sem=acts)
                        evg[g] = acts.v
                        k.w("gpsimd", acts, acts.v)
                        k.dma("gpsimd",
                              dd[hout][qt * 1024 + oc * 128: qt * 1024 + (oc + 1) * 128,
                                       c * 512:(c + 1) * 512], gs, st)
                        gst_d[g] = st.v
                    hread[(qt, c)] = pe.v
            self.fence(k, s)

        W3 = 0
        HC = [32768, 40960]
        XC = [0, 2048]
        OST = [4352, 4608]
        S3 = 4864
        for sl in range(4):
            k.dma("gpsimd", sb3(BF, W3 + sl * 8192, 8, 1024),
                  ap3(dd["w3T"], sl * 1024, 8, 0, 1024), wt)
        k.dma("gpsimd", FA[:, S3:S3 + 8], dd["s3c"][:, :], wt)
        wt_d = wt.v
        evd = {}
        ost_d = {}
        hread = {}
        for c in range(16):
            par = c % 2
            if c >= 2:
                k.w("sync", pe, hread[c - 2])
                k.w("sync", dve, evd[(c - 2) * 8 + 7])
            for half in range(4):
                k.dma("sync", sb3(BF, HC[par] + half * 2048, 8, 256),
                      ap3(dd["hbA"], half * 1024, 8, c * 256, 256), ld)
            k.dma("sync", sb3(FA, XC[par], 8, 256), ap3(dd[src], 0, 8, c * 256, 256), ld)
            ldv = ld.v
            for m in range(8):
                g = c * 8 + m
                bank = self.ps[g % 2]
                for sl in range(32):
                    if sl == 0:
                        k.w("tensor", ld, ldv)
                        if g == 0:
                            k.w("tensor", wt, wt_d)
                        if g >= 2:
                            k.w("tensor", dve, evd[g - 2])
                    k.mm(bank[:, 0:256],
                         BF[:, W3 + sl * 1024 + m * 128: W3 + sl * 1024 + (m + 1) * 128],
                         BF[:, HC[par] + sl * 256: HC[par] + (sl + 1) * 256],
                         start=(sl == 0), stop=(sl == 31), sem=(pe if sl == 31 else None))
                mmv = pe.v
                k.w("vector", pe, mmv)
                k.w("vector", ld, ldv)
                if g >= 2:
                    k.w("vector", st, ost_d[g - 2])
                os_ = FA[:, OST[g % 2]: OST[g % 2] + 256]
                k.vop("scalar_tensor_tensor", os_, bank[:, 0:256], FA[:, S3 + m: S3 + m + 1],
                      FA[:, XC[par] + m * 256: XC[par] + (m + 1) * 256],
                      op0=ALU.mult, op1=ALU.add, sem=dve)
                evd[g] = dve.v
                k.w("gpsimd", dve, dve.v)
                k.dma("gpsimd", dd[dst][m * 128:(m + 1) * 128, c * 256:(c + 1) * 256], os_, st)
                ost_d[g] = st.v
            hread[c] = pe.v
        k.w("gpsimd", st, st.v)

    def st_l1a(self, k, s, src, dst):
        FA, BF = self.FA, self.BF
        dd = self.d
        ld, wt, pe, acts, dve, st = s["LD"], s["WT"], s["PE"], s["ACT"], s["DVE"], s["ST"]

        WQKV = 0
        XNC = [24576, 28672]
        V2 = 32768
        QST = [66560, 70656]
        KST = [74752, 78848]
        QN = [82944, 83008]
        QSB, KSB = 0, 64
        SSC, DEN = 128, 136
        JNK = 160
        k.gop("memset", BF[:, V2:V2 + 33280], 1.0, sem=s["GP"])
        v2_ones = s["GP"].v
        for sl in range(4):
            k.dma("gpsimd", sb3(BF, WQKV + sl * 6144, 2, 3072),
                  ap3(dd["Wqkv"], sl * 256, 2, 0, 3072), wt)
        k.dma("gpsimd", FA[0:1, QSB:QSB + 64], dd["qs_row"][:, :], wt)
        k.dma("gpsimd", FA[0:1, KSB:KSB + 64], dd["ks_row"][:, :], wt)
        wt_a = wt.v
        k.w("tensor", wt, wt_a)
        k.mm(self.ps[4][:, 0:64], self.onesf[0:1, :], FA[0:1, QSB:QSB + 64],
             start=True, stop=True)
        k.mm(self.ps[5][:, 0:64], self.onesf[0:1, :], FA[0:1, KSB:KSB + 64],
             start=True, stop=True, sem=pe)
        k.w("vector", pe, pe.v)
        k.vop("tensor_copy", FA[:, QSB:QSB + 64], self.ps[4][:, 0:64])
        k.vop("tensor_copy", FA[:, KSB:KSB + 64], self.ps[5][:, 0:64], sem=dve)
        scb_done = dve.v

        ev = {}
        tr_done = {}
        qn_done = {}
        cp_done = {}
        xn_read = {}
        stq = {}
        last_tr = {}
        last_pb = {}
        for c in range(8):
            par = c % 2
            if c >= 2:
                k.w("sync", pe, xn_read[c - 2])
                k.w("vector", st, stq[c - 2])
            k.dma("sync", sb3(BF, XNC[par], 8, 512), ap3(dd["xnS"], 0, 8, c * 512, 512), ld)
            ldv = ld.v
            for sub in range(4):
                tile = c * 4 + sub
                for n in range(6):
                    g = (c * 4 + sub) * 6 + n
                    bank = self.ps[g % 4]
                    for sl in range(8):
                        if sl == 0:
                            k.w("tensor", ld, ldv)
                            if g == 0:
                                k.w("tensor", wt, wt_a)
                            if g >= 4:
                                k.w("tensor", dve, ev.get(g - 4, 0))
                                k.w("tensor", acts, ev.get(("a", g - 4), 0))
                        k.mm(bank[:, :],
                             BF[:, XNC[par] + sl * 512 + sub * 128: XNC[par] + sl * 512 + (sub + 1) * 128],
                             BF[:, WQKV + sl * 3072 + n * 512: WQKV + sl * 3072 + (n + 1) * 512],
                             start=(sl == 0), stop=(sl == 7), sem=(pe if sl == 7 else None))
                    mmv = pe.v
                    if n >= 4:
                        k.w("vector", pe, mmv)
                        if g == 0:
                            pass
                        k.w("vector", s["GP"], v2_ones)
                        for h8 in range(8):
                            h = (n - 4) * 8 + h8
                            slot = tile * 16 + h
                            k.vop("tensor_copy", BF[:, V2 + slot * 65: V2 + slot * 65 + 64],
                                  bank[:, h8 * 64:(h8 + 1) * 64],
                                  sem=(dve if h8 == 7 else None))
                        ev[g] = dve.v
                        ev[("a", g)] = ev.get(("a", g - 1), 0)
                        continue
                    isq = n < 2
                    scoff = QSB if isq else KSB
                    sstg = QST[par] if isq else KST[par]
                    k.w("scalar", pe, mmv)
                    for h8 in range(8):
                        k.act(FA[:, JNK:JNK + 64], bank[:, h8 * 64:(h8 + 1) * 64],
                              AF.Square, accum=FA[:, SSC + h8:SSC + h8 + 1],
                              sem=(acts if h8 == 7 else None))
                    k.w("scalar", acts, acts.v)
                    k.act(FA[:, SSC:SSC + 8], FA[:, SSC:SSC + 8], AF.Sqrt, sem=acts)
                    sq_done = acts.v
                    ev[("a", g)] = acts.v
                    k.w("vector", acts, sq_done)
                    k.vop("tensor_scalar", FA[:, DEN:DEN + 8], FA[:, SSC:SSC + 8],
                          0.125, 1e-8, op0=ALU.mult, op1=ALU.add, sem=dve)
                    k.w("vector", dve, dve.v)
                    k.vop("reciprocal", FA[:, DEN:DEN + 8], FA[:, DEN:DEN + 8], sem=dve)
                    k.w("vector", dve, dve.v)
                    if c == 0 and sub == 0 and n == 0:
                        k.w("vector", dve, scb_done)
                    for h8 in range(8):
                        h = (n % 2) * 8 + h8
                        qb = QN[h8 % 2]
                        if (h8 % 2) in last_tr:
                            k.w("vector", pe, last_tr[h8 % 2])
                        k.vop("scalar_tensor_tensor", BF[:, qb:qb + 64],
                              bank[:, h8 * 64:(h8 + 1) * 64], FA[:, DEN + h8:DEN + h8 + 1],
                              FA[:, scoff:scoff + 64], op0=ALU.mult, op1=ALU.mult, sem=dve)
                        qn_done[(g, h8)] = dve.v
                        k.w("tensor", dve, qn_done[(g, h8)])
                        if (h8 % 2) in last_pb:
                            k.w("tensor", dve, last_pb[h8 % 2])
                        k.tr(self.pb[h8 % 2][0:64, 0:128], BF[:, qb:qb + 64],
                             self.identb[:, :], sem=pe)
                        last_tr[h8 % 2] = pe.v
                        k.w("vector", pe, pe.v)
                        p_ = h // 2
                        r0 = (h % 2) * 64
                        k.vop("tensor_copy",
                              BF[r0:r0 + 64, sstg + p_ * 512 + sub * 128: sstg + p_ * 512 + (sub + 1) * 128],
                              self.pb[h8 % 2][0:64, 0:128], sem=dve)
                        last_pb[h8 % 2] = dve.v
                    ev[g] = dve.v
            xn_read[c] = pe.v
            k.w("gpsimd", dve, ev[(c * 4 + 3) * 6 + 3])
            for p_ in range(8):
                k.dma("gpsimd", dd["l1qt"][:, p_ * 4096 + c * 512: p_ * 4096 + (c + 1) * 512],
                      BF[:, QST[par] + p_ * 512: QST[par] + (p_ + 1) * 512], st)
                k.dma("gpsimd", dd["l1kt"][:, p_ * 4096 + c * 512: p_ * 4096 + (c + 1) * 512],
                      BF[:, KST[par] + p_ * 512: KST[par] + (p_ + 1) * 512], st)
            stq[c] = st.v
            if c >= 1:
                pass
        self.fence(k, s)

        KT = 0
        EA = [66048, 67072]
        EB = [68096, 69120]
        QB = 70144
        OUS = 74240
        ZR = 200
        ZBS = 1224
        k.dma("sync", BF[:, KT:KT + 32768], dd["l1kt"][:, :], ld)
        kt_done = ld.v
        sblk = {}
        expd = {}
        oud = {}
        oust = {}
        avd = {}
        for i in range(8):
            if i >= 1:
                k.w("sync", pe, sblk[i - 1])
                k.w("vector", st, oust[i - 1])
            qsrc = dd["l1qt"].rearrange("p (n c) -> p n c", c=4096)[:, :, i * 512:(i + 1) * 512]
            k.dma("sync", sb3(BF, QB, 8, 512), qsrc, ld)
            ldq = ld.v
            for p_ in range(8):
                avd = {}
                expd_l = {}

                def emit_S(j, ldq=None):
                    psA = self.ps[2 * (j % 2)]
                    psB = self.ps[1 + 2 * (j % 2)]
                    if j == 0:
                        k.w("tensor", ld, ldq)
                        if i == 0 and p_ == 0:
                            k.w("tensor", ld, kt_done)
                        k.w("tensor", dve, oud.get((i, p_ - 1) if p_ >= 1 else (i - 1, 7), 0))
                    if j >= 2:
                        k.w("tensor", acts, expd[(i, p_, j - 2)])
                    elif (i, p_) != (0, 0):
                        pp = (i, p_ - 1) if p_ >= 1 else (i - 1, 7)
                        k.w("tensor", acts, expd[(pp[0], pp[1], 31)])
                    k.mm(psA[:, :], BF[0:64, KT + p_ * 4096 + j * 128: KT + p_ * 4096 + (j + 1) * 128],
                         BF[0:64, QB + p_ * 512: QB + (p_ + 1) * 512], start=True, stop=True)
                    k.mm(psB[:, :], BF[64:128, KT + p_ * 4096 + j * 128: KT + p_ * 4096 + (j + 1) * 128],
                         BF[64:128, QB + p_ * 512: QB + (p_ + 1) * 512], start=True, stop=True,
                         sem=pe, tp=(64, 0))
                    smm = pe.v
                    k.w("scalar", pe, smm)
                    if j >= 2:
                        k.w("scalar", pe, avd.get((j - 2, 1), 0))
                    ea = BF[:, EA[j % 2]:EA[j % 2] + 512]
                    eb = BF[:, EB[j % 2]:EB[j % 2] + 512]
                    psA2 = self.ps[2 * (j % 2)]
                    psB2 = self.ps[1 + 2 * (j % 2)]
                    k.act(ea, psA2[:, :], AF.Exp, scale=0.125, sem=acts)
                    k.act(eb, psB2[:, :], AF.Exp, scale=0.125, sem=acts)
                    expd[(i, p_, j)] = acts.v

                def emit_AV(j):
                    ea = BF[:, EA[j % 2]:EA[j % 2] + 512]
                    eb = BF[:, EB[j % 2]:EB[j % 2] + 512]
                    k.w("tensor", acts, expd[(i, p_, j)])
                    k.mm(self.ps[4][0:65, :], BF[:, V2 + (j * 16 + 2 * p_) * 65: V2 + (j * 16 + 2 * p_) * 65 + 65],
                         ea, start=(j == 0), stop=(j == 31))
                    k.mm(self.ps[5][0:65, :], BF[:, V2 + (j * 16 + 2 * p_ + 1) * 65: V2 + (j * 16 + 2 * p_ + 1) * 65 + 65],
                         eb, start=(j == 0), stop=(j == 31), sem=pe)
                    avd[(j, 1)] = pe.v

                for j in range(32):
                    emit_S(j, ldq)
                    if j >= 1:
                        emit_AV(j - 1)
                emit_AV(31)
                av_last = pe.v
                k.w("vector", pe, av_last)
                zra = FA[0:1, ZR:ZR + 512]
                zrb = FA[0:1, ZR + 512:ZR + 1024]
                k.vop("reciprocal", zra, self.ps[4][64:65, 0:512])
                k.vop("reciprocal", zrb, self.ps[5][64:65, 0:512], sem=dve)
                k.w("vector", dve, dve.v)
                zb_done_prev = 0
                k.w("tensor", dve, dve.v)
                k.w("tensor", acts, expd[(i, p_, 31)])
                k.mm(self.ps[0][0:64, :], self.onesf[0:1, 0:64], zra, start=True, stop=True)
                k.mm(self.ps[1][0:64, :], self.onesf[0:1, 0:64], zrb, start=True, stop=True, sem=pe)
                zbv = pe.v
                k.w("vector", pe, zbv)
                k.vop("tensor_copy", FA[:, ZBS:ZBS + 512][0:64, :], self.ps[0][0:64, :])
                k.vop("tensor_copy", FA[:, ZBS + 512:ZBS + 1024][0:64, :], self.ps[1][0:64, :], sem=dve)
                k.w("vector", dve, dve.v)
                k.vop("tensor_mul", BF[0:64, OUS + p_ * 512: OUS + (p_ + 1) * 512],
                      self.ps[4][0:64, 0:512], FA[0:64, ZBS:ZBS + 512])
                k.vop("tensor_mul", BF[64:128, OUS + p_ * 512: OUS + (p_ + 1) * 512],
                      self.ps[5][0:64, 0:512], FA[0:64, ZBS + 512:ZBS + 1024], sem=dve)
                oud[(i, p_)] = dve.v
            sblk[i] = pe.v
            k.w("gpsimd", dve, oud[(i, 7)])
            if i >= 1:
                pass
            for p_ in range(8):
                k.dma("gpsimd", dd["l1ou"][:, p_ * 4096 + i * 512: p_ * 4096 + (i + 1) * 512],
                      BF[:, OUS + p_ * 512: OUS + (p_ + 1) * 512], st)
            oust[i] = st.v
        self.fence(k, s)

        WP = 0
        OUC = [8192, 10240]
        XC = [0, 2048]
        OST = [4352, 4608]
        BPJ = 4864
        for sl in range(4):
            k.dma("gpsimd", sb3(BF, WP + sl * 2048, 2, 1024),
                  ap3(dd["Wproj"], sl * 256, 2, 0, 1024), wt)
        k.dma("gpsimd", FA[:, BPJ:BPJ + 8], dd["bprojc"][:, :], wt)
        wt_p = wt.v
        evp = {}
        ostd = {}
        ourd = {}
        for c in range(16):
            par = c % 2
            if c >= 2:
                k.w("sync", pe, ourd[c - 2])
                k.w("sync", dve, evp[(c - 2) * 8 + 7])
            for p_ in range(8):
                k.dma("sync", BF[:, OUC[par] + p_ * 256: OUC[par] + (p_ + 1) * 256],
                      dd["l1ou"][:, p_ * 4096 + c * 256: p_ * 4096 + (c + 1) * 256], ld)
            k.dma("sync", sb3(FA, XC[par], 8, 256), ap3(dd[src], 0, 8, c * 256, 256), ld)
            ldv = ld.v
            for m in range(8):
                g = c * 8 + m
                bank = self.ps[g % 2]
                for sl in range(8):
                    if sl == 0:
                        k.w("tensor", ld, ldv)
                        if g == 0:
                            k.w("tensor", wt, wt_p)
                        if g >= 2:
                            k.w("tensor", dve, evp[g - 2])
                    k.mm(bank[:, 0:256],
                         BF[:, WP + sl * 1024 + m * 128: WP + sl * 1024 + (m + 1) * 128],
                         BF[:, OUC[par] + sl * 256: OUC[par] + (sl + 1) * 256],
                         start=(sl == 0), stop=(sl == 7), sem=(pe if sl == 7 else None))
                mmv = pe.v
                k.w("vector", pe, mmv)
                k.w("vector", ld, ldv)
                if g >= 2:
                    k.w("vector", st, ostd[g - 2])
                os_ = FA[:, OST[g % 2]: OST[g % 2] + 256]
                k.vop("scalar_tensor_tensor", os_, bank[:, 0:256], FA[:, BPJ + m: BPJ + m + 1],
                      FA[:, XC[par] + m * 256: XC[par] + (m + 1) * 256],
                      op0=ALU.add, op1=ALU.add, sem=dve)
                evp[g] = dve.v
                k.w("gpsimd", dve, dve.v)
                k.dma("gpsimd", dd[dst][m * 128:(m + 1) * 128, c * 256:(c + 1) * 256], os_, st)
                ostd[g] = st.v
            ourd[c] = pe.v
        k.w("gpsimd", st, st.v)

    def st_moe(self, k, s, src, dst):
        FA, BF = self.FA, self.BF
        dd = self.d
        ld, wt, pe, acts, dve, st = s["LD"], s["WT"], s["PE"], s["ACT"], s["DVE"], s["ST"]

        def vsync():
            k.w("vector", dve, dve.v)

        GTC = 81920

        XNT = [0, 1024]
        WGC = 2048
        SEL1, SEL2 = 2112, 2368
        G1C, G2C = 2624, 2656
        RW = 2688
        NM, SE, M1, I1, M2, RD = RW, RW + 1, RW + 2, RW + 3, RW + 4, RW + 5
        PRB, T1, CAND = RW + 16, RW + 24, RW + 32
        SELT = RW + 40
        CC = RW + 64
        CCO = CC + 128
        OF1, OF2 = CCO + 140, CCO + 150
        OF1B, OF2B = CCO + 160, CCO + 168
        B1R, B2R = CCO + 180, CCO + 188
        RNK, KP = CCO + 196, CCO + 198
        k.dma("gpsimd", sb3(FA, WGC, 8, 8), ap3(dd["wg"], 0, 8, 0, 8), wt)
        k.dma("gpsimd", self.ohot_sb[:, :], dd["ohot"][:, :], wt)
        k.dma("gpsimd", self.fc2b_sb[:, :], dd["fc2b"][:, :], wt)
        wt_r = wt.v
        ev_t = {}
        for t in range(32):
            par = t % 2
            if t >= 2:
                k.w("sync", dve, ev_t[t - 2])
            k.dma("sync", sb3(FA, XNT[par], 8, 128), ap3(dd["xn3f"], 0, 8, t * 128, 128), ld)
            ldv = ld.v
            bank = self.ps[par]
            for sl in range(8):
                if sl == 0:
                    k.w("tensor", ld, ldv)
                    k.w("tensor", wt, wt_r)
                    if t >= 2:
                        k.w("tensor", dve, ev_t[t - 2])
                k.mm(bank[:, 0:8], FA[:, XNT[par] + sl * 128: XNT[par] + (sl + 1) * 128],
                     FA[:, WGC + sl * 8: WGC + (sl + 1) * 8],
                     start=(sl == 0), stop=(sl == 7), sem=(pe if sl == 7 else None))
            k.w("vector", pe, pe.v)
            nm = FA[:, NM:NM + 1]
            k.vop("reduce_max", nm, bank[:, 0:8], axis=AX.X, negate=True, sem=dve)
            k.w("scalar", dve, dve.v)
            k.act(FA[:, PRB:PRB + 8], bank[:, 0:8], AF.Exp, bias=nm,
                  accum=FA[:, SE:SE + 1], sem=acts)
            k.w("vector", acts, acts.v)
            k.vop("reciprocal", FA[:, SE:SE + 1], FA[:, SE:SE + 1], sem=dve)
            vsync()
            k.vop("tensor_scalar_mul", FA[:, PRB:PRB + 8], FA[:, PRB:PRB + 8],
                  FA[:, SE:SE + 1], sem=dve)
            vsync()
            sel1 = FA[:, SEL1 + t * 8: SEL1 + (t + 1) * 8]
            sel2 = FA[:, SEL2 + t * 8: SEL2 + (t + 1) * 8]
            for (selx, mx, gcol) in ((sel1, M1, G1C), (sel2, M2, G2C)):
                k.vop("reduce_max", FA[:, mx:mx + 1], FA[:, PRB:PRB + 8], axis=AX.X, sem=dve)
                vsync()
                k.vop("tensor_scalar", FA[:, T1:T1 + 8], FA[:, PRB:PRB + 8],
                      FA[:, mx:mx + 1], None, op0=ALU.is_equal, sem=dve)
                vsync()
                k.vop("tensor_mul", FA[:, CAND:CAND + 8], FA[:, T1:T1 + 8],
                      self.eidxm8[:, :], sem=dve)
                vsync()
                k.vop("tensor_reduce", FA[:, I1:I1 + 1], FA[:, CAND:CAND + 8],
                      axis=AX.X, op=ALU.min, sem=dve)
                vsync()
                k.vop("tensor_scalar", selx, self.eidxm8[:, :], FA[:, I1:I1 + 1],
                      None, op0=ALU.is_equal, sem=dve)
                vsync()
                if mx == M1:
                    k.vop("tensor_scalar", FA[:, SELT:SELT + 8], selx, -1.0, 1.0,
                          op0=ALU.mult, op1=ALU.add, sem=dve)
                    vsync()
                    k.vop("tensor_mul", FA[:, PRB:PRB + 8], FA[:, PRB:PRB + 8],
                          FA[:, SELT:SELT + 8], sem=dve)
                    vsync()
            k.vop("tensor_add", FA[:, RD:RD + 1], FA[:, M1:M1 + 1], FA[:, M2:M2 + 1], sem=dve)
            vsync()
            k.vop("tensor_scalar_max", FA[:, RD:RD + 1], FA[:, RD:RD + 1], MOE_EPS, sem=dve)
            vsync()
            k.vop("reciprocal", FA[:, RD:RD + 1], FA[:, RD:RD + 1], sem=dve)
            vsync()
            k.vop("tensor_mul", FA[:, G1C + t:G1C + t + 1], FA[:, M1:M1 + 1],
                  FA[:, RD:RD + 1], sem=dve)
            k.vop("tensor_mul", FA[:, G2C + t:G2C + t + 1], FA[:, M2:M2 + 1],
                  FA[:, RD:RD + 1], sem=dve)
            ev_t[t] = dve.v
            k.w("tensor", dve, ev_t[t])
            k.mm(self.ps[4][0:1, 0:8], self.onesf[:, 0:1], sel1,
                 start=(t == 0), stop=(t == 31))
            k.mm(self.ps[5][0:1, 0:8], self.onesf[:, 0:1], sel2,
                 start=(t == 0), stop=(t == 31), sem=(pe if t == 31 else None))
        cnt_pe = pe.v
        k.vop("memset", FA[0:1, CC:CC + 128], 0.0, sem=dve)
        vsync()
        k.w("vector", pe, cnt_pe)
        k.vop("tensor_copy", FA[0:1, CC:CC + 8], self.ps[4][0:1, 0:8])
        k.vop("tensor_copy", FA[0:1, CC + 8:CC + 16], self.ps[5][0:1, 0:8], sem=dve)
        k.w("gpsimd", dve, dve.v)
        k.dma("gpsimd", dd["ccin"][:, :], FA[0:1, CC:CC + 128], s["GP"])
        k.w("gpsimd", s["GP"], s["GP"].v)
        k.gop("collective_compute", "AllGather", ALU.bypass,
              replica_groups=[[0, 1, 2, 3, 4, 5, 6, 7]],
              ins=[dd["ccin"].tensor.ap().opt()],
              outs=[dd["ccout"].tensor.ap().opt()], sem=s["GP"], inc=1)
        k.w("gpsimd", s["GP"], s["GP"].v)
        k.dma("gpsimd", FA[0:8, CCO:CCO + 128], dd["ccout"][:, :], s["GP"])
        k.dma("gpsimd", FA[0:8, CCO + 128:CCO + 129], dd["coremask"][:, :], s["GP"])
        cm_done = s["GP"].v
        k.w("tensor", s["GP"], cm_done)
        k.mm(self.ps[2][0:1, 0:16], FA[0:8, CCO + 128:CCO + 129], FA[0:8, CCO:CCO + 16],
             start=True, stop=True)
        k.mm(self.ps[3][0:1, 0:16], self.onesf[0:8, 0:1], FA[0:8, CCO:CCO + 16],
             start=True, stop=True, sem=pe)
        pfx_pe = pe.v
        k.w("vector", pe, pfx_pe)
        k.vop("tensor_scalar_add", FA[0:1, OF1:OF1 + 8], self.ps[2][0:1, 0:8], -1.0)
        k.vop("tensor_copy", FA[0:1, OF2:OF2 + 8], self.ps[3][0:1, 0:8], sem=dve)
        vsync()
        k.vop("scalar_tensor_tensor", FA[0:1, OF2:OF2 + 8], self.ps[2][0:1, 8:16], -1.0,
              FA[0:1, OF2:OF2 + 8], op0=ALU.add, op1=ALU.add, sem=dve)
        vsync()
        k.w("tensor", dve, dve.v)
        k.mm(self.ps[2][:, 32:40], self.onesf[0:1, :], FA[0:1, OF1:OF1 + 8],
             start=True, stop=True)
        k.mm(self.ps[3][:, 32:40], self.onesf[0:1, :], FA[0:1, OF2:OF2 + 8],
             start=True, stop=True, sem=pe)
        k.w("vector", pe, pe.v)
        k.vop("tensor_copy", FA[:, OF1B:OF1B + 8], self.ps[2][:, 32:40])
        k.vop("tensor_copy", FA[:, OF2B:OF2B + 8], self.ps[3][:, 32:40], sem=dve)
        vsync()
        k.vop("memset", FA[0:1, B1R:B1R + 8], 0.0)
        k.vop("memset", FA[0:1, B2R:B2R + 8], 0.0, sem=dve)
        vsync()
        gt_done = {}
        base_upd = {0: 0, 1: 0}
        for t in range(32):
            gtile = FA[:, CAND:CAND + 8]
            for (j, SEL, BR, OFB) in ((0, SEL1, B1R, OF1B), (1, SEL2, B2R, OF2B)):
                selx = FA[:, SEL + t * 8: SEL + (t + 1) * 8]
                gcol = (G1C if j == 0 else G2C) + t
                bank = self.ps[j]
                bsum = self.ps[2 + j]
                k.w("tensor", dve, dve.v)
                k.mm(bank[:, 0:8], self.ltri[:, :], selx, start=True, stop=False)
                k.mm(bank[:, 0:8], self.onesf[0:1, :], FA[0:1, BR:BR + 8],
                     start=False, stop=True)
                k.mm(bsum[0:1, 0:8], self.onesf[:, 0:1], selx, start=True, stop=True,
                     sem=pe)
                cum_pe = pe.v
                k.w("vector", pe, cum_pe)
                k.vop("tensor_add", FA[0:1, BR:BR + 8], FA[0:1, BR:BR + 8],
                      bsum[0:1, 0:8], sem=dve)
                base_upd[j] = dve.v
                vsync()
                k.vop("tensor_add", FA[:, T1:T1 + 8], bank[:, 0:8], FA[:, OFB:OFB + 8], sem=dve)
                vsync()
                k.vop("tensor_mul", FA[:, T1:T1 + 8], FA[:, T1:T1 + 8], selx, sem=dve)
                vsync()
                k.vop("reduce_sum", FA[:, RNK:RNK + 1], FA[:, T1:T1 + 8], axis=AX.X, sem=dve)
                vsync()
                k.vop("tensor_single_scalar", FA[:, KP:KP + 1], FA[:, RNK:RNK + 1],
                      float(CAP), op=ALU.is_lt, sem=dve)
                vsync()
                k.vop("tensor_mul", FA[:, KP:KP + 1], FA[:, KP:KP + 1],
                      FA[:, gcol:gcol + 1], sem=dve)
                vsync()
                if j == 0:
                    k.vop("tensor_scalar_mul", gtile, selx, FA[:, KP:KP + 1], sem=dve)
                else:
                    k.vop("scalar_tensor_tensor", gtile, selx, FA[:, KP:KP + 1],
                          gtile, op0=ALU.mult, op1=ALU.add, sem=dve)
                vsync()
            k.w("tensor", dve, dve.v)
            if t >= 2:
                k.w("tensor", dve, gt_done[t - 2])
            k.tr(self.ps[4 + t % 2][0:8, 0:128], gtile, self.identf[:, :], sem=pe)
            k.w("vector", pe, pe.v)
            k.vop("tensor_copy", BF[0:8, GTC + t * 128: GTC + (t + 1) * 128],
                  self.ps[4 + t % 2][0:8, 0:128], sem=dve)
            gt_done[t] = dve.v
        self.fence(k, s)

        WA, WB = 0, 32768
        XNC = [65536, 69632]
        HST = [73728, 74240]
        HC = [65536, 73728]
        YST = [86016, 86528]
        GB = [0, 512]
        TMP = [1024, 1536]
        F1B = 2048
        wa_done = [0] * NE
        wb_done = [0] * NE
        gb_used = {}
        pha_done = [0] * NE
        phb_done = [0] * NE
        hstore = {}
        ystore = {}
        for e in range(NE):
            if e >= 1:
                k.w("gpsimd", pe, pha_done[e - 1])
            for q in range(4):
                k.dma("gpsimd", sb3(BF, WA + q * 8192, 2, 4096),
                      ap3(dd["fc1T"], e * 1024 + q * 256, 2, 0, 4096), wt)
            k.dma("gpsimd", FA[:, F1B:F1B + 32],
                  dd["fc1b"][e * 128:(e + 1) * 128, :], wt)
            wa_done[e] = wt.v
            if e >= 1:
                k.w("gpsimd", pe, phb_done[e - 1])
            for q in range(4):
                k.dma("gpsimd", sb3(BF, WB + q * 8192, 8, 1024),
                      ap3(dd["fc2"], e * 4096 + q * 1024, 8, 0, 1024), wt)
            wb_done[e] = wt.v
            ga_done = {}
            xn_read = {}
            for c in range(8):
                par = c % 2
                if c >= 2:
                    k.w("sync", pe, xn_read[c - 2])
                k.dma("sync", sb3(BF, XNC[par], 8, 512), ap3(dd["xnS"], 0, 8, c * 512, 512), ld)
                ldv = ld.v
                for oc in range(32):
                    g = c * 32 + oc
                    bank = self.ps[g % 2]
                    for sl in range(8):
                        if sl == 0:
                            k.w("tensor", ld, ldv)
                            if g == 0:
                                k.w("tensor", wt, wa_done[e])
                            if g >= 2:
                                k.w("tensor", acts, ga_done[g - 2])
                        k.mm(bank[:, :],
                             BF[:, WA + sl * 4096 + oc * 128: WA + sl * 4096 + (oc + 1) * 128],
                             BF[:, XNC[par] + sl * 512: XNC[par] + (sl + 1) * 512],
                             start=(sl == 0), stop=(sl == 7), sem=(pe if sl == 7 else None))
                    mmv = pe.v
                    k.w("scalar", pe, mmv)
                    gk = e * 256 + g
                    if gk >= 2:
                        k.w("scalar", st, hstore.get(gk - 2, 0))
                    hs = BF[:, HST[g % 2]: HST[g % 2] + 512]
                    k.act(hs, bank[:, :], AF.Silu, bias=FA[:, F1B + oc:F1B + oc + 1], sem=acts)
                    ga_done[g] = acts.v
                    k.w("gpsimd", acts, acts.v)
                    k.dma("gpsimd", dd["hbA"][oc * 128:(oc + 1) * 128, c * 512:(c + 1) * 512],
                          hs, st)
                    hstore[gk] = st.v
                xn_read[c] = pe.v
            pha_done[e] = pe.v
            self.fence(k, s)
            yb_done = {}
            hread = {}
            for c in range(8):
                if c >= 1:
                    k.w("sync", pe, hread[c - 1])
                for half in range(2):
                    for q in range(2):
                        k.dma("sync", sb3(BF, HC[half] + q * 4096, 8, 512),
                              ap3(dd["hbA"], half * 2048 + q * 1024, 8, c * 512, 512), ld)
                ldv = ld.v
                par = c % 2
                k.w("tensor", wt, wb_done[e])
                if e * 8 + c >= 2:
                    k.w("tensor", dve, gb_used.get(e * 8 + c - 2, 0))
                k.mm(self.ps[4 + par][:, :], self.ohot_sb[0:8, e * 128:(e + 1) * 128],
                     BF[0:8, GTC + c * 512: GTC + (c + 1) * 512], start=True, stop=True, sem=pe)
                gbv = pe.v
                k.w("vector", pe, gbv)
                gb = FA[:, GB[par]:GB[par] + 512]
                k.vop("tensor_copy", gb, self.ps[4 + par][:, :], sem=dve)
                gb_done = dve.v
                for m in range(8):
                    g = c * 8 + m
                    bank = self.ps[g % 2]
                    for sl in range(32):
                        if sl == 0:
                            k.w("tensor", ld, ldv)
                            k.w("tensor", wt, wb_done[e])
                            if g >= 2:
                                k.w("tensor", dve, yb_done[g - 2])
                        half, sl16 = sl // 16, sl % 16
                        k.mm(bank[:, :],
                             BF[:, WB + sl * 1024 + m * 128: WB + sl * 1024 + (m + 1) * 128],
                             BF[:, HC[half] + sl16 * 512: HC[half] + (sl16 + 1) * 512],
                             start=(sl == 0), stop=(sl == 31), sem=(pe if sl == 31 else None))
                    mmv = pe.v
                    k.w("vector", pe, mmv)
                    if m == 0:
                        k.w("vector", dve, gb_done)
                    gk = e * 64 + g
                    if gk >= 2:
                        k.w("vector", st, ystore.get(gk - 2, 0))
                    ys = BF[:, YST[g % 2]: YST[g % 2] + 512]
                    k.vop("tensor_mul", ys, bank[:, :], gb, sem=dve)
                    yb_done[g] = dve.v
                    gb_used[e * 8 + c] = dve.v
                    k.w("gpsimd", dve, dve.v)
                    k.dma("gpsimd",
                          dd["ye"][e * 128:(e + 1) * 128,
                                   m * 4096 + c * 512: m * 4096 + (c + 1) * 512], ys, st)
                    ystore[gk] = st.v
                hread[c] = pe.v
            phb_done[e] = pe.v
            self.fence(k, s)

        CH = 256
        YEC = [0, 16384]
        RES, ACC, XNF = 0, 2048, 3072
        X3C, OSG, SQT, RWS = 3584, 4096, 4608, 5120
        k.dma("gpsimd", FA[:, RWS + 768:RWS + 776], dd["gammac"][:, :], wt)
        k.dma("gpsimd", FA[:, RWS + 776:RWS + 784], dd["betac"][:, :], wt)
        wt_f = wt.v
        GAM = RWS + 768
        res_done = {}
        outst = {}
        sqd = {}
        for c in range(16):
            par = c % 2
            if c >= 2:
                k.w("sync", dve, res_done.get(("acc", (c - 2) * 8 + 7), 0))
            for e in range(NE):
                ysrc = dd["ye"][e * 128:(e + 1) * 128, :].rearrange(
                    "p (n c2) -> p n c2", c2=4096)[:, :, c * 256:(c + 1) * 256]
                k.dma("sync", sb3(BF, YEC[par] + e * 2048, 8, 256), ysrc, ld)
            ye_ldv = ld.v
            for m in range(8):
                g = c * 8 + m
                if g >= 2:
                    k.w("sync", dve, res_done.get(("acc", g - 2), 0))
                xnb = FA[:, XNF + (m % 2) * 256: XNF + (m % 2) * 256 + 256]
                k.dma("sync", xnb,
                      dd["xn3f"][m * 128:(m + 1) * 128, c * 256:(c + 1) * 256], ld)
                ldv = ld.v
                if g >= 2:
                    k.w("tensor", dve, res_done.get(("bias", g - 2), 0))
                    if m < 2 and c >= 1:
                        k.w("tensor", dve, res_done.get(("x3", c * 8 - 1), 0))
                k.mm(self.ps[4 + g % 2][:, 0:256],
                     self.fc2b_sb[0:8, m * 128:(m + 1) * 128],
                     BF[0:8, GTC + c * 256: GTC + (c + 1) * 256], start=True, stop=True, sem=pe)
                biasv = pe.v
                k.w("vector", ld, ldv)
                if m == 0:
                    k.w("vector", ld, ye_ldv)
                ye_at = lambda e_: YEC[par] + e_ * 2048 + m * 256
                a_ = FA[:, ACC:ACC + 256]
                b_ = FA[:, ACC + 256:ACC + 512]
                c_ = FA[:, ACC + 512:ACC + 768]
                d_ = FA[:, ACC + 768:ACC + 1024]
                k.vop("tensor_add", a_, BF[:, ye_at(0):ye_at(0) + 256], BF[:, ye_at(1):ye_at(1) + 256])
                k.vop("tensor_add", b_, BF[:, ye_at(2):ye_at(2) + 256], BF[:, ye_at(3):ye_at(3) + 256])
                k.vop("tensor_add", c_, BF[:, ye_at(4):ye_at(4) + 256], BF[:, ye_at(5):ye_at(5) + 256])
                k.vop("tensor_add", d_, BF[:, ye_at(6):ye_at(6) + 256], BF[:, ye_at(7):ye_at(7) + 256])
                k.vop("tensor_add", a_, a_, b_)
                k.vop("tensor_add", c_, c_, d_)
                k.vop("tensor_add", a_, a_, c_)
                k.w("vector", pe, biasv)
                k.vop("tensor_add", a_, a_, self.ps[4 + g % 2][:, 0:256], sem=dve)
                res_done[("bias", g)] = dve.v
                rm = FA[:, RES + m * 256: RES + (m + 1) * 256]
                k.vop("tensor_add", rm, a_, xnb, sem=dve)
                res_done[("acc", g)] = dve.v
            res_v = dve.v
            bsum, bsq = self.ps[par], self.ps[2 + par]
            for m in range(8):
                if m == 0:
                    k.w("scalar", dve, res_v)
                sqb = FA[:, SQT + (m % 2) * 256: SQT + (m % 2) * 256 + 256]
                if (c, m - 2) in sqd:
                    k.w("scalar", pe, sqd[(c, m - 2)])
                elif c >= 1 and (c - 1, m + 6) in sqd:
                    k.w("scalar", pe, sqd[(c - 1, m + 6)])
                k.act(sqb, FA[:, RES + m * 256: RES + (m + 1) * 256], AF.Square, sem=acts)
                sqv = acts.v
                if m == 0:
                    k.w("tensor", dve, res_v)
                    if c >= 2:
                        k.w("tensor", dve, res_done[("stats", c - 2)])
                k.mm(bsum[0:1, 256:256 + CH], self.onesf[:, 0:1],
                     FA[:, RES + m * 256: RES + (m + 1) * 256], start=(m == 0), stop=(m == 7))
                k.w("tensor", acts, sqv)
                k.mm(bsq[0:1, 256:256 + CH], self.onesf[:, 0:1], sqb,
                     start=(m == 0), stop=(m == 7), sem=pe)
                sqd[(c, m)] = pe.v
            stat_pe = pe.v
            mu = FA[0:1, RWS:RWS + CH]
            v1 = FA[0:1, RWS + 256:RWS + 256 + CH]
            rs = FA[0:1, RWS + 512:RWS + 512 + CH]
            k.w("vector", pe, stat_pe)
            if c >= 2:
                k.w("vector", pe, res_done[("bcast", c - 2)])
            k.vop("tensor_scalar_mul", mu, bsum[0:1, 256:256 + CH], 1.0 / 1024.0, sem=dve)
            vsync()
            k.vop("tensor_mul", v1, mu, mu, sem=dve)
            vsync()
            k.vop("scalar_tensor_tensor", v1, bsq[0:1, 256:256 + CH], 1.0 / 1024.0, v1,
                  op0=ALU.mult, op1=ALU.subtract, sem=dve)
            k.w("scalar", dve, dve.v)
            k.act(v1, v1, AF.Sqrt, bias=self.smallf[0:1, 0:1], sem=acts)
            k.w("vector", acts, acts.v)
            k.vop("reciprocal", rs, v1, sem=dve)
            res_done[("stats", c)] = dve.v
            k.w("tensor", dve, res_done[("stats", c)])
            if c >= 2:
                k.w("tensor", dve, res_done[("x3", (c - 1) * 8 + 7)])
            k.mm(self.ps[4][:, 256:256 + CH], self.onesf[0:1, :], mu, start=True, stop=True)
            k.mm(self.ps[5][:, 256:256 + CH], self.onesf[0:1, :], rs, start=True, stop=True,
                 sem=pe)
            bc_pe = pe.v
            res_done[("bcast", c)] = bc_pe
            k.w("vector", pe, bc_pe)
            for m in range(8):
                g = c * 8 + m
                x3b = FA[:, X3C + (m % 2) * 256: X3C + (m % 2) * 256 + 256]
                if g >= 2:
                    k.w("sync", dve, res_done.get(("x3", g - 2), 0))
                k.dma("sync", x3b, dd[src][m * 128:(m + 1) * 128, c * 256:(c + 1) * 256], ld)
                ldx = ld.v
                rm = FA[:, RES + m * 256: RES + (m + 1) * 256]
                tm = FA[:, ACC:ACC + 256]
                t2 = FA[:, ACC + 256:ACC + 512]
                k.vop("tensor_sub", tm, rm, self.ps[4][:, 256:256 + CH])
                k.vop("tensor_mul", t2, tm, self.ps[5][:, 256:256 + CH])
                k.vop("tensor_scalar", tm, t2, FA[:, GAM + m:GAM + m + 1],
                      FA[:, GAM + 8 + m:GAM + 8 + m + 1], op0=ALU.mult, op1=ALU.add)
                k.w("vector", ld, ldx)
                k.w("vector", wt, wt_f)
                if g >= 2:
                    k.w("vector", st, outst.get(g - 2, 0))
                osg = FA[:, OSG + (m % 2) * 256: OSG + (m % 2) * 256 + 256]
                k.vop("tensor_add", osg, tm, x3b, sem=dve)
                res_done[("x3", g)] = dve.v
                k.w("gpsimd", dve, dve.v)
                k.dma("gpsimd", dd[dst][m * 128:(m + 1) * 128, c * 256:(c + 1) * 256], osg, st)
                outst[g] = st.v
        k.w("gpsimd", st, st.v)

    def build_in_maps(self, w, percore):
        maps = []
        for b in range(NCORES):
            m = {}
            for nm, sh, dty in WSPECS:
                if nm == "coremask":
                    m[nm] = percore[b]["coremask"]
                else:
                    m[nm] = w[nm]
            m["xT"] = percore[b]["xT"]
            maps.append(m)
        return maps


def run_stages(stages, inputs, out_name="outT", dbg_out=(), raw=False):
    w, percore = prep_host(inputs)
    p = Prog(stages, dbg_out=dbg_out)
    maps = p.build_in_maps(w, percore)
    res = run_bass_kernel_spmd(p.nc, maps, core_ids=list(range(NCORES)))
    if raw:
        return res
    return np.stack([np.asarray(r[out_name], dtype=np.float32).T for r in res.results])


def kernel(**inputs):
    stages = [
        ("ln", "xT", "xnS"),
        ("l0a", "xT", "x1"),
        ("ln", "x1", "xnS"),
        ("l0f", "x1", "x2"),
        ("ln", "x2", "xnS"),
        ("l1a", "x2", "x3"),
        ("ln", "x3", "xnS", "xn3f"),
        ("moe", "x3", "outT"),
    ]
    return run_stages(stages, inputs)


# version 04
# speedup vs baseline: 1.3005x; 1.0371x over previous
import numpy as np
import ml_dtypes
from contextlib import ExitStack
import concourse.bass as bass
import concourse.mybir as mybir
from concourse.bass_utils import run_bass_kernel_spmd

F32 = mybir.dt.float32
BF16 = mybir.dt.bfloat16
AF = mybir.ActivationFunctionType
ALU = mybir.AluOpType
AX = mybir.AxisListType
BFNP = ml_dtypes.bfloat16

D = 1024
T = 4096
NCORES = 8
HID = 4096
NH, HD = 16, 64
NE = 8
CAP = 10240
MOE_EPS = 1.1920929e-07

ENGS = ("sync", "tensor", "scalar", "vector", "gpsimd")


class Sem:
    __slots__ = ("h", "v")

    def __init__(self, h):
        self.h = h
        self.v = 0


class K:

    def __init__(self, nc, active, eng):
        self.nc = nc
        self.active = active
        self.eng = eng

    def w(self, engname, sem, val):
        if val > 0 and self.active == engname:
            self.eng.wait_ge(sem.h, val)

    def dma(self, engname, dst, src, sem, **kw):
        sem.v += 16
        if self.active == engname:
            self.eng.dma_start(dst, src, **kw).then_inc(sem.h, 16)
        return sem.v

    def mm(self, out, lhsT, rhs, start, stop, sem=None, tp=None):
        if sem is not None:
            sem.v += 1
        if self.active == "tensor":
            kw = {}
            if tp is not None:
                kw["tile_position"] = tp
            i = self.eng.matmul(out, lhsT, rhs, start=start, stop=stop, **kw)
            if sem is not None:
                i.then_inc(sem.h, 1)

    def tr(self, out, in_, ident, sem=None):
        if sem is not None:
            sem.v += 1
        if self.active == "tensor":
            i = self.eng.transpose(out, in_, ident)
            if sem is not None:
                i.then_inc(sem.h, 1)

    def act(self, out, in_, func, bias=0.0, scale=1.0, accum=None, sem=None):
        if sem is not None:
            sem.v += 1
        if self.active == "scalar":
            kw = {}
            if accum is not None:
                kw["accum_out"] = accum
            i = self.eng.activation(out, in_, func, bias=bias, scale=scale, **kw)
            if sem is not None:
                i.then_inc(sem.h, 1)

    def vop(self, meth, *args, sem=None, **kw):
        if sem is not None:
            sem.v += 1
        if self.active == "vector":
            i = getattr(self.eng, meth)(*args, **kw)
            if sem is not None:
                i.then_inc(sem.h, 1)

    def gop(self, meth, *args, sem=None, inc=1, **kw):
        if sem is not None:
            sem.v += inc
        if self.active == "gpsimd":
            i = getattr(self.eng, meth)(*args, **kw)
            if sem is not None:
                i.then_inc(sem.h, inc)

    def barrier(self, sbar):
        sbar.v += len(ENGS)
        tgt = sbar.v
        if self.eng is not None:
            self.eng.nop().then_inc(sbar.h, 1)
            self.eng.wait_ge(sbar.h, tgt)


def ap3(dram, r0, nslab, c0, ncols):
    return dram[r0:r0 + nslab * 128, c0:c0 + ncols].rearrange(
        "(n p) c -> p n c", p=128)


def sb3(sbt, col0, nslab, ncols):
    return sbt[:, col0:col0 + nslab * ncols].rearrange(
        "p (n c) -> p n c", c=ncols)



def col128(v):
    return np.ascontiguousarray(v.reshape(-1, 128).T)


def prep_host(inputs):
    f = np.ascontiguousarray
    w = {}
    w["Wq"] = f(inputs["L0_attn_Wq"])
    w["Wk"] = f(inputs["L0_attn_Wk"])
    w["Wv"] = f(inputs["L0_attn_Wv"])
    w["bq8"] = col128(inputs["L0_attn_bq"])
    w["bk8"] = col128(inputs["L0_attn_bk"])
    w["bv"] = f(inputs["L0_attn_bv"].reshape(1, D))
    w["w1T"] = f(inputs["L0_ffn_w1_w"].T)
    w["w2T"] = f(inputs["L0_ffn_w2_w"].T)
    w["h1T"] = f(inputs["L0_ffn_h1_w"].T)
    w["h2T"] = f(inputs["L0_ffn_h2_w"].T)
    w["w3T"] = f(inputs["L0_ffn_w3_w"].T)
    w["s1c"] = col128(inputs["L0_ffn_w1_s"])
    w["s2c"] = col128(inputs["L0_ffn_w2_s"])
    w["sh1c"] = col128(inputs["L0_ffn_h1_s"])
    w["sh2c"] = col128(inputs["L0_ffn_h2_s"])
    w["s3c"] = col128(inputs["L0_ffn_w3_s"])
    w["Wqkv"] = f(inputs["L1_attn_Wqkv"])
    w["Wproj"] = f(inputs["L1_attn_Wproj"])
    w["bprojc"] = col128(inputs["L1_attn_bproj"])
    w["qs_row"] = f(inputs["L1_attn_q_scale"].reshape(1, HD))
    w["ks_row"] = f(inputs["L1_attn_k_scale"].reshape(1, HD))
    w["wg"] = f(inputs["L1_ffn_wg"])
    w["fc1T"] = f(np.transpose(inputs["L1_ffn_fc1_w"], (0, 2, 1)).reshape(NE * D, HID))
    w["fc2"] = f(inputs["L1_ffn_fc2_w"].reshape(NE * HID, D))
    w["fc1b"] = f(np.concatenate([col128(inputs["L1_ffn_fc1_b"][e]) for e in range(NE)], axis=0))
    w["fc2b"] = f(inputs["L1_ffn_fc2_b"])
    oh = np.zeros((NE, NE * 128), dtype=BFNP)
    for e in range(NE):
        oh[e, e * 128:(e + 1) * 128] = 1.0
    w["ohot"] = oh
    w["gammac"] = col128(inputs["L1_ffn_ln_gamma"])
    w["betac"] = col128(inputs["L1_ffn_ln_beta"])

    w["identb"] = np.eye(128, dtype=BFNP)
    w["identf"] = np.eye(128, dtype=np.float32)
    p = np.arange(128)[:, None]
    c = np.arange(384)[None, :]
    w["cmask"] = (p <= c - 128).astype(BFNP)
    w["ltri"] = np.triu(np.ones((128, 128), dtype=np.float32))
    w["eidxm8"] = np.broadcast_to(
        (np.arange(8, dtype=np.float32) - 8.0)[None, :], (128, 8)).copy()

    percore = []
    x = inputs["L0_attn_x"]
    for b in range(NCORES):
        cm = np.zeros((8, 1), dtype=np.float32)
        cm[:b] = 1.0
        percore.append({"xT": f(x[b].T.astype(np.float32)), "coremask": cm})
    return w, percore


WSPECS = [
    ("Wq", [D, D], F32), ("Wk", [D, D], F32), ("Wv", [D, D], F32),
    ("bq8", [128, 8], F32), ("bk8", [128, 8], F32), ("bv", [1, D], F32),
    ("w1T", [D, HID], F32), ("w2T", [D, HID], F32),
    ("h1T", [HID, HID], F32), ("h2T", [HID, HID], F32), ("w3T", [HID, D], F32),
    ("s1c", [128, 32], F32), ("s2c", [128, 32], F32), ("sh1c", [128, 32], F32),
    ("sh2c", [128, 32], F32), ("s3c", [128, 8], F32),
    ("Wqkv", [D, 3 * D], F32), ("Wproj", [D, D], F32), ("bprojc", [128, 8], F32),
    ("qs_row", [1, HD], F32), ("ks_row", [1, HD], F32),
    ("wg", [D, NE], F32), ("fc1T", [NE * D, HID], F32), ("fc2", [NE * HID, D], F32),
    ("fc1b", [NE * 128, 32], F32), ("fc2b", [NE, D], F32),
    ("gammac", [128, 8], F32), ("betac", [128, 8], F32),
    ("identf", [128, 128], F32), ("ltri", [128, 128], F32),
    ("eidxm8", [128, 8], F32), ("coremask", [8, 1], F32),
    ("identb", [128, 128], BF16), ("cmask", [128, 384], BF16),
    ("ohot", [NE, NE * 128], BF16),
]

BF_COLS = 90112
FA_COLS = 6144


class Prog:
    def __init__(self, stages, ext_in="xT", ext_out="outT", dbg_out=()):
        self.stages = stages
        self.dbg_out = set(dbg_out)
        nc = bass.Bass("TRN2", target_bir_lowering=False)
        self.nc = nc
        self.d = {}

        def dram(name, shape, dtype, kind):
            self.d[name] = nc.dram_tensor(name, shape, dtype, kind=kind).ap()

        dram("xT", [D, T], F32, "ExternalInput")
        dram("outT", [D, T], F32, "ExternalOutput")
        for nm, sh, dty in WSPECS:
            dram(nm, sh, dty, "ExternalInput")
        for nm, sh, dty in [
                ("x1", [D, T], F32), ("x2", [D, T], F32), ("x3", [D, T], F32),
                ("xnS", [D, T], BF16), ("xn3f", [D, T], F32),
                ("l0qt", [D, T], BF16), ("l0kt", [D, T], BF16), ("l0ou", [D, T], F32),
                ("hbA", [HID, T], BF16), ("hbB", [HID, T], BF16),
                ("l1qt", [128, 8 * T], BF16), ("l1kt", [128, 8 * T], BF16),
                ("l1ou", [128, 8 * T], BF16),
                ("ccin", [1, 128], F32), ("ccout", [8, 128], F32),
                ("ye", [NE * 128, 8 * T], BF16), ("gtd", [8, T], BF16),
                ("resd", [D, T], F32)]:
            dram(nm, sh, dty, "ExternalOutput" if nm in self.dbg_out else "Internal")

        self.stack = ExitStack()
        e = self.stack.enter_context
        self.BF = e(nc.sbuf_tensor("BF", [128, BF_COLS], BF16))
        self.FA = e(nc.sbuf_tensor("FA", [128, FA_COLS], F32))
        self.identb = e(nc.sbuf_tensor("identb_sb", [128, 128], BF16))
        self.identf = e(nc.sbuf_tensor("identf_sb", [128, 128], F32))
        self.onesf = e(nc.sbuf_tensor("onesf_sb", [128, 128], F32))
        self.cmask = e(nc.sbuf_tensor("cmask_sb", [128, 384], BF16))
        self.ltri = e(nc.sbuf_tensor("ltri_sb", [128, 128], F32))
        self.eidxm8 = e(nc.sbuf_tensor("eidxm8_sb", [128, 8], F32))
        self.smallf = e(nc.sbuf_tensor("smallf_sb", [128, 8], F32))
        self.ohot_sb = e(nc.sbuf_tensor("ohot_sb", [8, NE * 128], BF16))
        self.gt_sb = e(nc.sbuf_tensor("gt_sb", [8, 512], BF16))
        self.fc2b_sb = e(nc.sbuf_tensor("fc2b_sb", [8, D], BF16))
        self.ps = [e(nc.psum_tensor(f"ps{j}", [128, 512], F32)) for j in range(6)]
        self.pb = [e(nc.psum_tensor(f"pb{j}", [128, 1024], BF16)) for j in range(2)]
        self.sems = {}
        for pos in range(len(stages) + 1):
            for nm in ("LD", "WT", "PE", "ACT", "DVE", "ST", "GP"):
                key = f"{nm}{pos}"
                self.sems[key] = Sem(e(nc.semaphore(key)))
        self.sems["BAR"] = Sem(e(nc.semaphore("BAR")))

        with nc.Block() as blk:
            for name, meth in (("sync", blk.sync), ("tensor", blk.tensor),
                               ("scalar", blk.scalar), ("vector", blk.vector),
                               ("gpsimd", blk.gpsimd)):
                def body(eng, name=name):
                    for s in self.sems.values():
                        s.v = 0
                    self.walk(K(nc, name, eng))
                meth(body)
        self.stack.close()

    def walk(self, k):
        sm = self.sems
        pre = sm["LD0"]
        k.dma("gpsimd", self.identb[:, :], self.d["identb"][:, :], pre)
        k.dma("gpsimd", self.identf[:, :], self.d["identf"][:, :], pre)
        k.dma("gpsimd", self.cmask[:, :], self.d["cmask"][:, :], pre)
        k.dma("gpsimd", self.ltri[:, :], self.d["ltri"][:, :], pre)
        k.dma("gpsimd", self.eidxm8[:, :], self.d["eidxm8"][:, :], pre)
        k.gop("memset", self.onesf[:, :], 1.0, sem=sm["GP0"])
        k.gop("memset", self.smallf[:, 0:1], 1e-5, sem=sm["GP0"])
        for e in ("vector", "tensor", "scalar"):
            k.w(e, pre, pre.v)
            k.w(e, sm["GP0"], sm["GP0"].v)
        k.barrier(sm["BAR"])
        for idx, st in enumerate(self.stages, start=1):
            s = {nm: sm[f"{nm}{idx}"] for nm in ("LD", "WT", "PE", "ACT", "DVE", "ST", "GP")}
            getattr(self, "st_" + st[0])(k, s, *st[1:])
            k.barrier(sm["BAR"])

    def fence(self, k, s):
        k.w("gpsimd", s["ST"], s["ST"].v)
        k.w("gpsimd", s["WT"], s["WT"].v)
        k.w("sync", s["LD"], s["LD"].v)
        k.barrier(self.sems["BAR"])

    def st_ln(self, k, s, src, dstb, dstf=None):
        CH = 128
        NCH = T // CH
        FA, BF = self.FA, self.BF
        src_d, dstb_d = self.d[src], self.d[dstb]
        dstf_d = self.d[dstf] if dstf else None
        XB = [0, 1024]
        SQ = [2048, 2176]
        TMP = 2304
        FST = [2432, 3456]
        ROWS = 4480
        ld, pe, acts, dve, st = s["LD"], s["PE"], s["ACT"], s["DVE"], s["ST"]

        mmsq = {}
        stats_done = [0] * NCH
        bcast_done = [0] * NCH
        apply_done = [0] * NCH
        store_done = [0] * NCH
        for c in range(NCH):
            par = c % 2
            mu = FA[0:1, ROWS + par * 512: ROWS + par * 512 + CH]
            v1 = FA[0:1, ROWS + par * 512 + 128: ROWS + par * 512 + 128 + CH]
            rs = FA[0:1, ROWS + par * 512 + 256: ROWS + par * 512 + 256 + CH]
            sd = FA[0:1, ROWS + par * 512 + 384: ROWS + par * 512 + 384 + CH]
            if c >= 2:
                k.w("sync", dve, apply_done[c - 2])
            k.dma("sync", sb3(FA, XB[par], 8, CH), ap3(src_d, 0, 8, c * CH, CH), ld)
            ldv = ld.v
            bsum, bsq = self.ps[par], self.ps[2 + par]
            bmu, brs = self.ps[4], self.ps[5]
            for sl in range(8):
                xsl = FA[:, XB[par] + sl * CH: XB[par] + (sl + 1) * CH]
                sqb = FA[:, SQ[sl % 2]: SQ[sl % 2] + CH]
                if sl == 0:
                    k.w("scalar", ld, ldv)
                    if c >= 1:
                        k.w("scalar", pe, mmsq[(c - 1, 7)])
                elif sl in (1,) and c >= 1:
                    k.w("scalar", pe, mmsq[(c - 1, 7)])
                elif sl >= 2:
                    k.w("scalar", pe, mmsq[(c, sl - 2)])
                k.act(sqb, xsl, AF.Square, sem=acts)
                sqv = acts.v
                if sl == 0:
                    k.w("tensor", ld, ldv)
                    if c >= 2:
                        k.w("tensor", dve, stats_done[c - 2])
                k.mm(bsum[0:1, 0:CH], self.onesf[:, 0:1], xsl,
                     start=(sl == 0), stop=(sl == 7))
                k.w("tensor", acts, sqv)
                k.mm(bsq[0:1, 0:CH], self.onesf[:, 0:1], sqb,
                     start=(sl == 0), stop=(sl == 7), sem=pe)
                mmsq[(c, sl)] = pe.v
            k.w("vector", pe, mmsq[(c, 7)])
            if c >= 2:
                k.w("vector", pe, bcast_done[c - 2])
            k.vop("tensor_scalar_mul", mu, bsum[0:1, 0:CH], 1.0 / 1024.0, sem=dve)
            k.w("vector", dve, dve.v)
            k.vop("tensor_mul", v1, mu, mu, sem=dve)
            k.w("vector", dve, dve.v)
            k.vop("scalar_tensor_tensor", v1, bsq[0:1, 0:CH], 1.0 / 1024.0, v1,
                  op0=ALU.mult, op1=ALU.subtract, sem=dve)
            k.w("scalar", dve, dve.v)
            k.act(sd, v1, AF.Sqrt, bias=self.smallf[0:1, 0:1], sem=acts)
            k.w("vector", acts, acts.v)
            k.vop("reciprocal", rs, sd, sem=dve)
            stats_done[c] = dve.v
            if c >= 1:
                k.w("tensor", dve, apply_done[c - 1])
            k.w("tensor", dve, stats_done[c])
            k.mm(bmu[:, 0:CH], self.onesf[0:1, :], mu, start=True, stop=True)
            k.mm(brs[:, 0:CH], self.onesf[0:1, :], rs, start=True, stop=True, sem=pe)
            bcast_done[c] = pe.v
            k.w("vector", pe, bcast_done[c])
            if c >= 2:
                k.w("vector", st, store_done[c - 2])
            for sl in range(8):
                xsl = FA[:, XB[par] + sl * CH: XB[par] + (sl + 1) * CH]
                tm = FA[:, TMP: TMP + CH]
                ob = BF[:, par * 1024 + sl * CH: par * 1024 + (sl + 1) * CH]
                k.vop("tensor_sub", tm, xsl, bmu[:, 0:CH])
                k.vop("tensor_mul", ob, tm, brs[:, 0:CH],
                      sem=(dve if (sl == 7 and not dstf) else None))
                if dstf:
                    fo = FA[:, FST[par] + sl * CH: FST[par] + (sl + 1) * CH]
                    k.vop("tensor_mul", fo, tm, brs[:, 0:CH],
                          sem=(dve if sl == 7 else None))
            apply_done[c] = dve.v
            k.w("gpsimd", dve, apply_done[c])
            k.dma("gpsimd", ap3(dstb_d, 0, 8, c * CH, CH), sb3(BF, par * 1024, 8, CH), st)
            if dstf:
                k.dma("gpsimd", ap3(dstf_d, 0, 8, c * CH, CH), sb3(FA, FST[par], 8, CH), st)
            store_done[c] = st.v
        k.w("gpsimd", st, store_done[NCH - 1])

    def st_l0a(self, k, s, src, dst):
        FA, BF = self.FA, self.BF
        dd = self.d
        ld, wt, pe, acts, dve, st = s["LD"], s["WT"], s["PE"], s["ACT"], s["DVE"], s["ST"]
        WQ, WK, WV = 0, 8192, 16384
        XN = [24576, 28672]
        VT = 32768
        QS = [65536, 66048]
        KS = [66560, 67072]
        BQ8, BK8 = 0, 8
        BVB = 16
        k.dma("gpsimd", sb3(BF, WQ, 8, 1024), ap3(dd["Wq"], 0, 8, 0, 1024), wt)
        k.dma("gpsimd", sb3(BF, WK, 8, 1024), ap3(dd["Wk"], 0, 8, 0, 1024), wt)
        k.dma("gpsimd", sb3(BF, WV, 8, 1024), ap3(dd["Wv"], 0, 8, 0, 1024), wt)
        k.dma("gpsimd", FA[:, BQ8:BQ8 + 8], dd["bq8"][:, :], wt)
        k.dma("gpsimd", FA[:, BK8:BK8 + 8], dd["bk8"][:, :], wt)
        k.dma("gpsimd", FA[0:1, BVB:BVB + D], dd["bv"][:, :], wt)
        wt_all = wt.v
        k.w("tensor", wt, wt_all)
        k.mm(self.ps[4][:, :], self.onesf[0:1, :], FA[0:1, BVB:BVB + 512],
             start=True, stop=True)
        k.mm(self.ps[5][:, :], self.onesf[0:1, :], FA[0:1, BVB + 512:BVB + 1024],
             start=True, stop=True, sem=pe)
        k.w("vector", pe, pe.v)
        k.vop("tensor_copy", FA[:, BVB:BVB + 512], self.ps[4][:, :])
        k.vop("tensor_copy", FA[:, BVB + 512:BVB + 1024], self.ps[5][:, :], sem=dve)
        bvb_done = dve.v

        q_evac = {}
        v_evac = {}
        qk_store = {}
        xn_read = {}
        for c in range(8):
            par = c % 2
            if c >= 2:
                k.w("sync", pe, xn_read[c - 2])
            k.dma("sync", sb3(BF, XN[par], 8, 512), ap3(dd["xnS"], 0, 8, c * 512, 512), ld)
            ldv = ld.v
            for mi in range(16):
                which = "q" if mi < 8 else "k"
                m = mi % 8
                woff = WQ if which == "q" else WK
                boff = BQ8 if which == "q" else BK8
                qdram = dd["l0qt"] if which == "q" else dd["l0kt"]
                stg = QS if which == "q" else KS
                g = c * 16 + mi
                bank = self.ps[g % 2]
                for sl in range(8):
                    if sl == 0:
                        k.w("tensor", ld, ldv)
                        if g == 0:
                            k.w("tensor", wt, wt_all)
                        if g >= 2:
                            k.w("tensor", dve, q_evac[g - 2])
                    k.mm(bank[:, :], BF[:, woff + sl * 1024 + m * 128: woff + sl * 1024 + (m + 1) * 128],
                         BF[:, XN[par] + sl * 512: XN[par] + (sl + 1) * 512],
                         start=(sl == 0), stop=(sl == 7), sem=(pe if sl == 7 else None))
                mmv = pe.v
                k.w("vector", pe, mmv)
                if g >= 2:
                    k.w("vector", st, qk_store[g - 2])
                k.vop("tensor_scalar_add", BF[:, stg[m % 2]: stg[m % 2] + 512],
                      bank[:, :], FA[:, boff + m: boff + m + 1], sem=dve)
                q_evac[g] = dve.v
                k.w("gpsimd", dve, dve.v)
                k.dma("gpsimd", qdram[m * 128:(m + 1) * 128, c * 512:(c + 1) * 512],
                      BF[:, stg[m % 2]: stg[m % 2] + 512], st)
                qk_store[g] = st.v
            for sub in range(4):
                tile = c * 4 + sub
                for half in range(2):
                    gv = c * 8 + sub * 2 + half
                    bank = self.ps[2 + gv % 2]
                    for sl in range(8):
                        if sl == 0:
                            k.w("tensor", ld, ldv)
                            if gv >= 2:
                                k.w("tensor", dve, v_evac[gv - 2])
                        k.mm(bank[:, :],
                             BF[:, XN[par] + sl * 512 + sub * 128: XN[par] + sl * 512 + (sub + 1) * 128],
                             BF[:, WV + sl * 1024 + half * 512: WV + sl * 1024 + (half + 1) * 512],
                             start=(sl == 0), stop=(sl == 7), sem=(pe if sl == 7 else None))
                    mmv = pe.v
                    k.w("vector", pe, mmv)
                    if gv == 0:
                        k.w("vector", dve, bvb_done)
                    k.vop("tensor_add", BF[:, VT + tile * 1024 + half * 512: VT + tile * 1024 + (half + 1) * 512],
                          bank[:, :], FA[:, BVB + half * 512: BVB + (half + 1) * 512], sem=dve)
                    v_evac[gv] = dve.v
            xn_read[c] = pe.v
        p0_pe = pe.v
        p0_dve = dve.v
        p0_st = st.v

        KT = 0
        EB = 65536
        QB = 74240
        ZC, OUS, ZCOL, RZC = 1152, 1760, 2304, 2320
        k.w("gpsimd", st, p0_st)
        k.w("gpsimd", dve, p0_dve)
        k.dma("gpsimd", sb3(BF, KT, 8, 4096), ap3(dd["l0kt"], 0, 8, 0, 4096), wt)
        kt_done = wt.v
        k.vop("memset", FA[:, ZC:ZC + 544], 0.0, sem=dve)
        zc_zero = dve.v
        zci = 0
        av_done = [0] * 16
        exp_done = [0] * 16
        s_done = [0] * 16
        ou_stored = {}
        ou_evac = {}
        for i in range(16):
            nj = 2 * i + 2
            if i >= 1:
                k.w("sync", pe, s_done[i - 1])
            k.dma("sync", sb3(BF, QB, 8, 256), ap3(dd["l0qt"], 0, 8, i * 256, 256), ld)
            qldv = ld.v
            if i >= 1:
                k.w("scalar", pe, av_done[i - 1])
            for j in range(nj):
                bank = self.ps[j % 2]
                for sl in range(8):
                    if sl == 0:
                        k.w("tensor", ld, qldv)
                        if i == 0 and j == 0:
                            k.w("tensor", wt, kt_done)
                        if j >= 2:
                            k.w("tensor", acts, exp_j[j - 2])
                        elif i >= 1:
                            k.w("tensor", acts, exp_done[i - 1])
                    k.mm(bank[:, 0:256],
                         BF[:, KT + sl * 4096 + j * 128: KT + sl * 4096 + (j + 1) * 128],
                         BF[:, QB + sl * 256: QB + (sl + 1) * 256],
                         start=(sl == 0), stop=(sl == 7), sem=(pe if sl == 7 else None))
                if j == 0:
                    exp_j = {}
                smm = pe.v
                k.w("scalar", pe, smm)
                diag = (j >= 2 * i)
                etile = BF[:, EB + j * 256: EB + (j + 1) * 256]
                if not diag:
                    if zci == 0:
                        k.w("scalar", dve, zc_zero)
                    k.act(etile, bank[:, 0:256], AF.Exp, scale=1.0 / 32.0,
                          accum=FA[:, ZC + zci: ZC + zci + 1], sem=acts)
                else:
                    k.act(etile, bank[:, 0:256], AF.Exp, scale=1.0 / 32.0, sem=acts)
                    k.w("vector", acts, acts.v)
                    moff = 128 if j == 2 * i else 0
                    k.vop("tensor_mul", etile, etile, self.cmask[:, moff:moff + 256])
                    k.w("vector", dve, zc_zero)
                    k.vop("reduce_sum", FA[:, ZC + zci: ZC + zci + 1], etile,
                          axis=AX.X, sem=dve)
                exp_j[j] = acts.v
                zci += 1
            s_done[i] = pe.v
            exp_done[i] = acts.v
            mask_done_i = dve.v
            for m in range(8):
                bank = self.ps[2 + m % 2]
                for j in range(nj):
                    if j == 0:
                        k.w("tensor", acts, exp_done[i])
                        k.w("tensor", dve, mask_done_i)
                        if m >= 2:
                            k.w("tensor", dve, ou_evac[(i, m - 2)])
                        elif i >= 1:
                            k.w("tensor", dve, ou_evac[(i - 1, m + 6)])
                    k.mm(bank[:, 0:256],
                         BF[:, VT + j * 1024 + m * 128: VT + j * 1024 + (m + 1) * 128],
                         BF[:, EB + j * 256: EB + (j + 1) * 256],
                         start=(j == 0), stop=(j == nj - 1), sem=(pe if j == nj - 1 else None))
                avv = pe.v
                k.w("vector", pe, avv)
                idx = i * 8 + m
                if idx >= 2:
                    k.w("vector", st, ou_stored.get(idx - 2, 0))
                ob = FA[:, OUS + (m % 2) * 256: OUS + (m % 2) * 256 + 256]
                k.vop("tensor_copy", ob, bank[:, 0:256], sem=dve)
                ou_evac[(i, m)] = dve.v
                k.w("gpsimd", dve, dve.v)
                k.dma("gpsimd", dd["l0ou"][m * 128:(m + 1) * 128, i * 256:(i + 1) * 256], ob, st)
                ou_stored[idx] = st.v
            av_done[i] = pe.v
        k.w("vector", acts, exp_done[15])
        k.vop("reduce_sum", FA[:, ZCOL:ZCOL + 1], FA[:, ZC:ZC + 544], axis=AX.X, sem=dve)
        k.w("tensor", dve, dve.v)
        k.tr(self.ps[0][0:1, 0:128], FA[:, ZCOL:ZCOL + 1], self.identf[:, :], sem=pe)
        k.w("vector", pe, pe.v)
        k.vop("reduce_sum", FA[0:1, ZCOL + 2:ZCOL + 3], self.ps[0][0:1, 0:128],
              axis=AX.X, sem=dve)
        k.w("vector", dve, dve.v)
        k.vop("reciprocal", FA[0:1, ZCOL + 4:ZCOL + 5], FA[0:1, ZCOL + 2:ZCOL + 3], sem=dve)
        k.w("tensor", dve, dve.v)
        k.mm(self.ps[1][:, 0:1], self.onesf[0:1, :], FA[0:1, ZCOL + 4:ZCOL + 5],
             start=True, stop=True, sem=pe)
        k.w("vector", pe, pe.v)
        k.vop("tensor_copy", FA[:, RZC:RZC + 1], self.ps[1][:, 0:1], sem=dve)
        rz_done = dve.v

        XC, OC, OO = 2560, 3584, 4608
        res_done = {}
        out_done = {}
        k.w("sync", st, ou_stored[127])
        for sl in range(8):
            for h in range(8):
                idx = sl * 8 + h
                par = idx % 2
                xc = FA[:, XC + par * 512: XC + par * 512 + 512]
                oc = FA[:, OC + par * 512: OC + par * 512 + 512]
                oo = FA[:, OO + par * 512: OO + par * 512 + 512]
                if idx >= 2:
                    k.w("sync", dve, res_done[idx - 2])
                k.dma("sync", xc, dd[src][sl * 128:(sl + 1) * 128, h * 512:(h + 1) * 512], ld)
                k.dma("sync", oc, dd["l0ou"][sl * 128:(sl + 1) * 128, h * 512:(h + 1) * 512], ld)
                ldv = ld.v
                k.w("vector", ld, ldv)
                if idx == 0:
                    k.w("vector", dve, rz_done)
                if idx >= 2:
                    k.w("vector", st, out_done[idx - 2])
                k.vop("scalar_tensor_tensor", oo, oc, FA[:, RZC:RZC + 1], xc,
                      op0=ALU.mult, op1=ALU.add, sem=dve)
                res_done[idx] = dve.v
                k.w("gpsimd", dve, dve.v)
                k.dma("gpsimd", dd[dst][sl * 128:(sl + 1) * 128, h * 512:(h + 1) * 512], oo, st)
                out_done[idx] = st.v
        k.w("gpsimd", st, st.v)

    def st_l0f(self, k, s, src, dst):
        FA, BF = self.FA, self.BF
        dd = self.d
        ld, wt, pe, acts, dve, st = s["LD"], s["WT"], s["PE"], s["ACT"], s["DVE"], s["ST"]

        W1, W2 = 0, 32768
        XNC = [65536, 69632]
        HST = [73728, 74240]
        TMPA = [0, 512]
        SC = 1024
        for sl in range(4):
            k.dma("gpsimd", sb3(BF, W1 + sl * 8192, 2, 4096),
                  ap3(dd["w1T"], sl * 256, 2, 0, 4096), wt)
        for sl in range(4):
            k.dma("gpsimd", sb3(BF, W2 + sl * 8192, 2, 4096),
                  ap3(dd["w2T"], sl * 256, 2, 0, 4096), wt)
        k.dma("gpsimd", FA[:, SC:SC + 32], dd["s1c"][:, :], wt)
        k.dma("gpsimd", FA[:, SC + 32:SC + 64], dd["s2c"][:, :], wt)
        wt_a = wt.v
        evac1 = {}
        evac2 = {}
        hstore = {}
        xn_read = {}
        for c in range(8):
            par = c % 2
            if c >= 2:
                k.w("sync", pe, xn_read[c - 2])
            k.dma("sync", sb3(BF, XNC[par], 8, 512), ap3(dd["xnS"], 0, 8, c * 512, 512), ld)
            ldv = ld.v
            for oc in range(32):
                g = c * 32 + oc
                b1, b2 = self.ps[g % 2], self.ps[2 + g % 2]
                for sl in range(8):
                    if sl == 0:
                        k.w("tensor", ld, ldv)
                        if g == 0:
                            k.w("tensor", wt, wt_a)
                        if g >= 2:
                            k.w("tensor", acts, evac1[g - 2])
                            k.w("tensor", dve, evac2[g - 2])
                    k.mm(b1[:, :], BF[:, W1 + sl * 4096 + oc * 128: W1 + sl * 4096 + (oc + 1) * 128],
                         BF[:, XNC[par] + sl * 512: XNC[par] + (sl + 1) * 512],
                         start=(sl == 0), stop=(sl == 7))
                    k.mm(b2[:, :], BF[:, W2 + sl * 4096 + oc * 128: W2 + sl * 4096 + (oc + 1) * 128],
                         BF[:, XNC[par] + sl * 512: XNC[par] + (sl + 1) * 512],
                         start=(sl == 0), stop=(sl == 7), sem=(pe if sl == 7 else None))
                mmv = pe.v
                k.w("scalar", pe, mmv)
                if g >= 2:
                    k.w("scalar", dve, evac2[g - 2])
                tmp = FA[:, TMPA[g % 2]: TMPA[g % 2] + 512]
                k.act(tmp, b1[:, :], AF.Silu, scale=FA[:, SC + oc: SC + oc + 1], sem=acts)
                evac1[g] = acts.v
                k.w("vector", acts, acts.v)
                if g >= 2:
                    k.w("vector", st, hstore[g - 2])
                hs = BF[:, HST[g % 2]: HST[g % 2] + 512]
                k.vop("scalar_tensor_tensor", hs, b2[:, :], FA[:, SC + 32 + oc: SC + 33 + oc],
                      tmp, op0=ALU.mult, op1=ALU.mult, sem=dve)
                evac2[g] = dve.v
                k.w("gpsimd", dve, dve.v)
                k.dma("gpsimd", dd["hbA"][oc * 128:(oc + 1) * 128, c * 512:(c + 1) * 512], hs, st)
                hstore[g] = st.v
            xn_read[c] = pe.v
        self.fence(k, s)

        for (wname, scoff, hin, hout) in (("h1T", 0, "hbA", "hbB"), ("h2T", 32, "hbB", "hbA")):
            WQ = 0
            HC = [32768, 49152]
            GST = [65536, 66048]
            k.dma("gpsimd", FA[:, SC + 64 + scoff: SC + 96 + scoff],
                  dd["sh1c" if wname == "h1T" else "sh2c"][:, :], wt)
            evg = {}
            gst_d = {}
            hread = {}
            for qt in range(4):
                if qt >= 1:
                    k.w("gpsimd", pe, hread[(qt - 1, 7)])
                for sl in range(4):
                    k.dma("gpsimd", sb3(BF, WQ + sl * 8192, 8, 1024),
                          ap3(dd[wname], sl * 1024, 8, qt * 1024, 1024), wt)
                wt_q = wt.v
                for c in range(8):
                    par = c % 2
                    if (qt * 8 + c) >= 2:
                        pc = qt * 8 + c - 2
                        k.w("sync", pe, hread[(pc // 8, pc % 8)])
                    for half in range(4):
                        k.dma("sync", sb3(BF, HC[par] + half * 4096, 8, 512),
                              ap3(dd[hin], half * 1024, 8, c * 512, 512), ld)
                    ldv = ld.v
                    for oc in range(8):
                        g = (qt * 8 + c) * 8 + oc
                        bank = self.ps[g % 2]
                        for sl in range(32):
                            if sl == 0:
                                k.w("tensor", ld, ldv)
                                k.w("tensor", wt, wt_q)
                                if g >= 2:
                                    k.w("tensor", acts, evg[g - 2])
                            k.mm(bank[:, :],
                                 BF[:, WQ + sl * 1024 + oc * 128: WQ + sl * 1024 + (oc + 1) * 128],
                                 BF[:, HC[par] + sl * 512: HC[par] + (sl + 1) * 512],
                                 start=(sl == 0), stop=(sl == 31), sem=(pe if sl == 31 else None))
                        mmv = pe.v
                        k.w("scalar", pe, mmv)
                        if g >= 2:
                            k.w("scalar", st, gst_d[g - 2])
                        gs = BF[:, GST[g % 2]: GST[g % 2] + 512]
                        k.act(gs, bank[:, :], AF.Gelu,
                              scale=FA[:, SC + 64 + scoff + qt * 8 + oc: SC + 65 + scoff + qt * 8 + oc],
                              sem=acts)
                        evg[g] = acts.v
                        k.w("gpsimd", acts, acts.v)
                        k.dma("gpsimd",
                              dd[hout][qt * 1024 + oc * 128: qt * 1024 + (oc + 1) * 128,
                                       c * 512:(c + 1) * 512], gs, st)
                        gst_d[g] = st.v
                    hread[(qt, c)] = pe.v
            self.fence(k, s)

        W3 = 0
        HC = [32768, 40960]
        XC = [0, 2048]
        OST = [4352, 4608]
        S3 = 4864
        for sl in range(4):
            k.dma("gpsimd", sb3(BF, W3 + sl * 8192, 8, 1024),
                  ap3(dd["w3T"], sl * 1024, 8, 0, 1024), wt)
        k.dma("gpsimd", FA[:, S3:S3 + 8], dd["s3c"][:, :], wt)
        wt_d = wt.v
        evd = {}
        ost_d = {}
        hread = {}
        for c in range(16):
            par = c % 2
            if c >= 2:
                k.w("sync", pe, hread[c - 2])
                k.w("sync", dve, evd[(c - 2) * 8 + 7])
            for half in range(4):
                k.dma("sync", sb3(BF, HC[par] + half * 2048, 8, 256),
                      ap3(dd["hbA"], half * 1024, 8, c * 256, 256), ld)
            k.dma("sync", sb3(FA, XC[par], 8, 256), ap3(dd[src], 0, 8, c * 256, 256), ld)
            ldv = ld.v
            for m in range(8):
                g = c * 8 + m
                bank = self.ps[g % 2]
                for sl in range(32):
                    if sl == 0:
                        k.w("tensor", ld, ldv)
                        if g == 0:
                            k.w("tensor", wt, wt_d)
                        if g >= 2:
                            k.w("tensor", dve, evd[g - 2])
                    k.mm(bank[:, 0:256],
                         BF[:, W3 + sl * 1024 + m * 128: W3 + sl * 1024 + (m + 1) * 128],
                         BF[:, HC[par] + sl * 256: HC[par] + (sl + 1) * 256],
                         start=(sl == 0), stop=(sl == 31), sem=(pe if sl == 31 else None))
                mmv = pe.v
                k.w("vector", pe, mmv)
                k.w("vector", ld, ldv)
                if g >= 2:
                    k.w("vector", st, ost_d[g - 2])
                os_ = FA[:, OST[g % 2]: OST[g % 2] + 256]
                k.vop("scalar_tensor_tensor", os_, bank[:, 0:256], FA[:, S3 + m: S3 + m + 1],
                      FA[:, XC[par] + m * 256: XC[par] + (m + 1) * 256],
                      op0=ALU.mult, op1=ALU.add, sem=dve)
                evd[g] = dve.v
                k.w("gpsimd", dve, dve.v)
                k.dma("gpsimd", dd[dst][m * 128:(m + 1) * 128, c * 256:(c + 1) * 256], os_, st)
                ost_d[g] = st.v
            hread[c] = pe.v
        k.w("gpsimd", st, st.v)

    def st_l1a(self, k, s, src, dst):
        FA, BF = self.FA, self.BF
        dd = self.d
        ld, wt, pe, acts, dve, st = s["LD"], s["WT"], s["PE"], s["ACT"], s["DVE"], s["ST"]

        WQKV = 0
        XNC = [24576, 28672]
        V2 = 32768
        QST = [66560, 70656]
        KST = [74752, 78848]
        QN = [82944, 83008]
        QSB, KSB = 0, 64
        SSC, DEN = 128, 136
        JNK = 160
        k.gop("memset", BF[:, V2:V2 + 33280], 1.0, sem=s["GP"])
        v2_ones = s["GP"].v
        for sl in range(4):
            k.dma("gpsimd", sb3(BF, WQKV + sl * 6144, 2, 3072),
                  ap3(dd["Wqkv"], sl * 256, 2, 0, 3072), wt)
        k.dma("gpsimd", FA[0:1, QSB:QSB + 64], dd["qs_row"][:, :], wt)
        k.dma("gpsimd", FA[0:1, KSB:KSB + 64], dd["ks_row"][:, :], wt)
        wt_a = wt.v
        k.w("tensor", wt, wt_a)
        k.mm(self.ps[4][:, 0:64], self.onesf[0:1, :], FA[0:1, QSB:QSB + 64],
             start=True, stop=True)
        k.mm(self.ps[5][:, 0:64], self.onesf[0:1, :], FA[0:1, KSB:KSB + 64],
             start=True, stop=True, sem=pe)
        k.w("vector", pe, pe.v)
        k.vop("tensor_copy", FA[:, QSB:QSB + 64], self.ps[4][:, 0:64])
        k.vop("tensor_copy", FA[:, KSB:KSB + 64], self.ps[5][:, 0:64], sem=dve)
        scb_done = dve.v

        ev = {}
        tr_done = {}
        qn_done = {}
        cp_done = {}
        xn_read = {}
        stq = {}
        last_tr = {}
        last_pb = {}
        for c in range(8):
            par = c % 2
            if c >= 2:
                k.w("sync", pe, xn_read[c - 2])
                k.w("vector", st, stq[c - 2])
            k.dma("sync", sb3(BF, XNC[par], 8, 512), ap3(dd["xnS"], 0, 8, c * 512, 512), ld)
            ldv = ld.v
            for sub in range(4):
                tile = c * 4 + sub
                for n in range(6):
                    g = (c * 4 + sub) * 6 + n
                    bank = self.ps[g % 4]
                    for sl in range(8):
                        if sl == 0:
                            k.w("tensor", ld, ldv)
                            if g == 0:
                                k.w("tensor", wt, wt_a)
                            if g >= 4:
                                k.w("tensor", dve, ev.get(g - 4, 0))
                                k.w("tensor", acts, ev.get(("a", g - 4), 0))
                        k.mm(bank[:, :],
                             BF[:, XNC[par] + sl * 512 + sub * 128: XNC[par] + sl * 512 + (sub + 1) * 128],
                             BF[:, WQKV + sl * 3072 + n * 512: WQKV + sl * 3072 + (n + 1) * 512],
                             start=(sl == 0), stop=(sl == 7), sem=(pe if sl == 7 else None))
                    mmv = pe.v
                    if n >= 4:
                        k.w("vector", pe, mmv)
                        if g == 0:
                            pass
                        k.w("vector", s["GP"], v2_ones)
                        for h8 in range(8):
                            h = (n - 4) * 8 + h8
                            slot = tile * 16 + h
                            k.vop("tensor_copy", BF[:, V2 + slot * 65: V2 + slot * 65 + 64],
                                  bank[:, h8 * 64:(h8 + 1) * 64],
                                  sem=(dve if h8 == 7 else None))
                        ev[g] = dve.v
                        ev[("a", g)] = ev.get(("a", g - 1), 0)
                        continue
                    isq = n < 2
                    scoff = QSB if isq else KSB
                    sstg = QST[par] if isq else KST[par]
                    k.w("scalar", pe, mmv)
                    for h8 in range(8):
                        k.act(FA[:, JNK:JNK + 64], bank[:, h8 * 64:(h8 + 1) * 64],
                              AF.Square, accum=FA[:, SSC + h8:SSC + h8 + 1],
                              sem=(acts if h8 == 7 else None))
                    k.w("scalar", acts, acts.v)
                    k.act(FA[:, SSC:SSC + 8], FA[:, SSC:SSC + 8], AF.Sqrt, sem=acts)
                    sq_done = acts.v
                    ev[("a", g)] = acts.v
                    k.w("vector", acts, sq_done)
                    k.vop("tensor_scalar", FA[:, DEN:DEN + 8], FA[:, SSC:SSC + 8],
                          0.125, 1e-8, op0=ALU.mult, op1=ALU.add, sem=dve)
                    k.w("vector", dve, dve.v)
                    k.vop("reciprocal", FA[:, DEN:DEN + 8], FA[:, DEN:DEN + 8], sem=dve)
                    k.w("vector", dve, dve.v)
                    if c == 0 and sub == 0 and n == 0:
                        k.w("vector", dve, scb_done)
                    for h8 in range(8):
                        h = (n % 2) * 8 + h8
                        qb = QN[h8 % 2]
                        if (h8 % 2) in last_tr:
                            k.w("vector", pe, last_tr[h8 % 2])
                        k.vop("scalar_tensor_tensor", BF[:, qb:qb + 64],
                              bank[:, h8 * 64:(h8 + 1) * 64], FA[:, DEN + h8:DEN + h8 + 1],
                              FA[:, scoff:scoff + 64], op0=ALU.mult, op1=ALU.mult, sem=dve)
                        qn_done[(g, h8)] = dve.v
                        k.w("tensor", dve, qn_done[(g, h8)])
                        if (h8 % 2) in last_pb:
                            k.w("tensor", dve, last_pb[h8 % 2])
                        k.tr(self.pb[h8 % 2][0:64, 0:128], BF[:, qb:qb + 64],
                             self.identb[:, :], sem=pe)
                        last_tr[h8 % 2] = pe.v
                        k.w("vector", pe, pe.v)
                        p_ = h // 2
                        r0 = (h % 2) * 64
                        k.vop("tensor_copy",
                              BF[r0:r0 + 64, sstg + p_ * 512 + sub * 128: sstg + p_ * 512 + (sub + 1) * 128],
                              self.pb[h8 % 2][0:64, 0:128], sem=dve)
                        last_pb[h8 % 2] = dve.v
                    ev[g] = dve.v
            xn_read[c] = pe.v
            k.w("gpsimd", dve, ev[(c * 4 + 3) * 6 + 3])
            for p_ in range(8):
                k.dma("gpsimd", dd["l1qt"][:, p_ * 4096 + c * 512: p_ * 4096 + (c + 1) * 512],
                      BF[:, QST[par] + p_ * 512: QST[par] + (p_ + 1) * 512], st)
                k.dma("gpsimd", dd["l1kt"][:, p_ * 4096 + c * 512: p_ * 4096 + (c + 1) * 512],
                      BF[:, KST[par] + p_ * 512: KST[par] + (p_ + 1) * 512], st)
            stq[c] = st.v
            if c >= 1:
                pass
        self.fence(k, s)

        KT = 0
        EA = [66048, 67072]
        EB = [68096, 69120]
        QB = 70144
        OUS = 74240
        ZR = 200
        ZBS = 1224
        k.dma("sync", BF[:, KT:KT + 32768], dd["l1kt"][:, :], ld)
        kt_done = ld.v
        sblk = {}
        expd = {}
        oud = {}
        oust = {}
        avd = {}
        for i in range(8):
            if i >= 1:
                k.w("sync", pe, sblk[i - 1])
                k.w("vector", st, oust[i - 1])
            qsrc = dd["l1qt"].rearrange("p (n c) -> p n c", c=4096)[:, :, i * 512:(i + 1) * 512]
            k.dma("sync", sb3(BF, QB, 8, 512), qsrc, ld)
            ldq = ld.v
            for p_ in range(8):
                avd = {}
                expd_l = {}

                def emit_S(j, ldq=None):
                    psA = self.ps[2 * (j % 2)]
                    psB = self.ps[1 + 2 * (j % 2)]
                    if j == 0:
                        k.w("tensor", ld, ldq)
                        if i == 0 and p_ == 0:
                            k.w("tensor", ld, kt_done)
                        k.w("tensor", dve, oud.get((i, p_ - 1) if p_ >= 1 else (i - 1, 7), 0))
                    if j >= 2:
                        k.w("tensor", acts, expd[(i, p_, j - 2)])
                    elif (i, p_) != (0, 0):
                        pp = (i, p_ - 1) if p_ >= 1 else (i - 1, 7)
                        k.w("tensor", acts, expd[(pp[0], pp[1], 31)])
                    k.mm(psA[:, :], BF[0:64, KT + p_ * 4096 + j * 128: KT + p_ * 4096 + (j + 1) * 128],
                         BF[0:64, QB + p_ * 512: QB + (p_ + 1) * 512], start=True, stop=True)
                    k.mm(psB[:, :], BF[64:128, KT + p_ * 4096 + j * 128: KT + p_ * 4096 + (j + 1) * 128],
                         BF[64:128, QB + p_ * 512: QB + (p_ + 1) * 512], start=True, stop=True,
                         sem=pe, tp=(64, 0))
                    smm = pe.v
                    k.w("scalar", pe, smm)
                    if j >= 2:
                        k.w("scalar", pe, avd.get((j - 2, 1), 0))
                    ea = BF[:, EA[j % 2]:EA[j % 2] + 512]
                    eb = BF[:, EB[j % 2]:EB[j % 2] + 512]
                    psA2 = self.ps[2 * (j % 2)]
                    psB2 = self.ps[1 + 2 * (j % 2)]
                    k.act(ea, psA2[:, :], AF.Exp, scale=0.125, sem=acts)
                    k.act(eb, psB2[:, :], AF.Exp, scale=0.125, sem=acts)
                    expd[(i, p_, j)] = acts.v

                def emit_AV(j):
                    ea = BF[:, EA[j % 2]:EA[j % 2] + 512]
                    eb = BF[:, EB[j % 2]:EB[j % 2] + 512]
                    k.w("tensor", acts, expd[(i, p_, j)])
                    k.mm(self.ps[4][0:65, :], BF[:, V2 + (j * 16 + 2 * p_) * 65: V2 + (j * 16 + 2 * p_) * 65 + 65],
                         ea, start=(j == 0), stop=(j == 31))
                    k.mm(self.ps[5][0:65, :], BF[:, V2 + (j * 16 + 2 * p_ + 1) * 65: V2 + (j * 16 + 2 * p_ + 1) * 65 + 65],
                         eb, start=(j == 0), stop=(j == 31), sem=pe)
                    avd[(j, 1)] = pe.v

                for j in range(32):
                    emit_S(j, ldq)
                    if j >= 1:
                        emit_AV(j - 1)
                emit_AV(31)
                av_last = pe.v
                k.w("vector", pe, av_last)
                zra = FA[0:1, ZR:ZR + 512]
                zrb = FA[0:1, ZR + 512:ZR + 1024]
                k.vop("reciprocal", zra, self.ps[4][64:65, 0:512])
                k.vop("reciprocal", zrb, self.ps[5][64:65, 0:512], sem=dve)
                k.w("vector", dve, dve.v)
                zb_done_prev = 0
                k.w("tensor", dve, dve.v)
                k.w("tensor", acts, expd[(i, p_, 31)])
                k.mm(self.ps[0][0:64, :], self.onesf[0:1, 0:64], zra, start=True, stop=True)
                k.mm(self.ps[1][0:64, :], self.onesf[0:1, 0:64], zrb, start=True, stop=True, sem=pe)
                zbv = pe.v
                k.w("vector", pe, zbv)
                k.vop("tensor_copy", FA[:, ZBS:ZBS + 512][0:64, :], self.ps[0][0:64, :])
                k.vop("tensor_copy", FA[:, ZBS + 512:ZBS + 1024][0:64, :], self.ps[1][0:64, :], sem=dve)
                k.w("vector", dve, dve.v)
                k.vop("tensor_mul", BF[0:64, OUS + p_ * 512: OUS + (p_ + 1) * 512],
                      self.ps[4][0:64, 0:512], FA[0:64, ZBS:ZBS + 512])
                k.vop("tensor_mul", BF[64:128, OUS + p_ * 512: OUS + (p_ + 1) * 512],
                      self.ps[5][0:64, 0:512], FA[0:64, ZBS + 512:ZBS + 1024], sem=dve)
                oud[(i, p_)] = dve.v
            sblk[i] = pe.v
            k.w("gpsimd", dve, oud[(i, 7)])
            if i >= 1:
                pass
            for p_ in range(8):
                k.dma("gpsimd", dd["l1ou"][:, p_ * 4096 + i * 512: p_ * 4096 + (i + 1) * 512],
                      BF[:, OUS + p_ * 512: OUS + (p_ + 1) * 512], st)
            oust[i] = st.v
        self.fence(k, s)

        WP = 0
        OUC = [8192, 10240]
        XC = [0, 2048]
        OST = [4352, 4608]
        BPJ = 4864
        for sl in range(4):
            k.dma("gpsimd", sb3(BF, WP + sl * 2048, 2, 1024),
                  ap3(dd["Wproj"], sl * 256, 2, 0, 1024), wt)
        k.dma("gpsimd", FA[:, BPJ:BPJ + 8], dd["bprojc"][:, :], wt)
        wt_p = wt.v
        evp = {}
        ostd = {}
        ourd = {}
        for c in range(16):
            par = c % 2
            if c >= 2:
                k.w("sync", pe, ourd[c - 2])
                k.w("sync", dve, evp[(c - 2) * 8 + 7])
            for p_ in range(8):
                k.dma("sync", BF[:, OUC[par] + p_ * 256: OUC[par] + (p_ + 1) * 256],
                      dd["l1ou"][:, p_ * 4096 + c * 256: p_ * 4096 + (c + 1) * 256], ld)
            k.dma("sync", sb3(FA, XC[par], 8, 256), ap3(dd[src], 0, 8, c * 256, 256), ld)
            ldv = ld.v
            for m in range(8):
                g = c * 8 + m
                bank = self.ps[g % 2]
                for sl in range(8):
                    if sl == 0:
                        k.w("tensor", ld, ldv)
                        if g == 0:
                            k.w("tensor", wt, wt_p)
                        if g >= 2:
                            k.w("tensor", dve, evp[g - 2])
                    k.mm(bank[:, 0:256],
                         BF[:, WP + sl * 1024 + m * 128: WP + sl * 1024 + (m + 1) * 128],
                         BF[:, OUC[par] + sl * 256: OUC[par] + (sl + 1) * 256],
                         start=(sl == 0), stop=(sl == 7), sem=(pe if sl == 7 else None))
                mmv = pe.v
                k.w("vector", pe, mmv)
                k.w("vector", ld, ldv)
                if g >= 2:
                    k.w("vector", st, ostd[g - 2])
                os_ = FA[:, OST[g % 2]: OST[g % 2] + 256]
                k.vop("scalar_tensor_tensor", os_, bank[:, 0:256], FA[:, BPJ + m: BPJ + m + 1],
                      FA[:, XC[par] + m * 256: XC[par] + (m + 1) * 256],
                      op0=ALU.add, op1=ALU.add, sem=dve)
                evp[g] = dve.v
                k.w("gpsimd", dve, dve.v)
                k.dma("gpsimd", dd[dst][m * 128:(m + 1) * 128, c * 256:(c + 1) * 256], os_, st)
                ostd[g] = st.v
            ourd[c] = pe.v
        k.w("gpsimd", st, st.v)

    def st_moe(self, k, s, src, dst):
        FA, BF = self.FA, self.BF
        dd = self.d
        ld, wt, pe, acts, dve, st = s["LD"], s["WT"], s["PE"], s["ACT"], s["DVE"], s["ST"]

        def vsync():
            k.w("vector", dve, dve.v)

        GTC = 0

        XNT = [0, 1024]
        WGC = 2048
        SEL1, SEL2 = 2112, 2368
        G1C, G2C = 2624, 2656
        RW = 2688
        NM, SE, M1, I1, M2, RD = RW, RW + 1, RW + 2, RW + 3, RW + 4, RW + 5
        PRB, T1, CAND = RW + 16, RW + 24, RW + 32
        SELT = RW + 40
        CC = RW + 64
        CCO = CC + 128
        OF1, OF2 = CCO + 140, CCO + 150
        OF1B, OF2B = CCO + 160, CCO + 168
        B1R, B2R = CCO + 180, CCO + 188
        RNK, KP = CCO + 196, CCO + 198
        k.dma("gpsimd", sb3(FA, WGC, 8, 8), ap3(dd["wg"], 0, 8, 0, 8), wt)
        k.dma("gpsimd", self.ohot_sb[:, :], dd["ohot"][:, :], wt)
        k.dma("gpsimd", self.fc2b_sb[:, :], dd["fc2b"][:, :], wt)
        wt_r = wt.v
        ev_t = {}
        for t in range(32):
            par = t % 2
            if t >= 2:
                k.w("sync", dve, ev_t[t - 2])
            k.dma("sync", sb3(FA, XNT[par], 8, 128), ap3(dd["xn3f"], 0, 8, t * 128, 128), ld)
            ldv = ld.v
            bank = self.ps[par]
            for sl in range(8):
                if sl == 0:
                    k.w("tensor", ld, ldv)
                    k.w("tensor", wt, wt_r)
                    if t >= 2:
                        k.w("tensor", dve, ev_t[t - 2])
                k.mm(bank[:, 0:8], FA[:, XNT[par] + sl * 128: XNT[par] + (sl + 1) * 128],
                     FA[:, WGC + sl * 8: WGC + (sl + 1) * 8],
                     start=(sl == 0), stop=(sl == 7), sem=(pe if sl == 7 else None))
            k.w("vector", pe, pe.v)
            nm = FA[:, NM:NM + 1]
            k.vop("reduce_max", nm, bank[:, 0:8], axis=AX.X, negate=True, sem=dve)
            k.w("scalar", dve, dve.v)
            k.act(FA[:, PRB:PRB + 8], bank[:, 0:8], AF.Exp, bias=nm,
                  accum=FA[:, SE:SE + 1], sem=acts)
            k.w("vector", acts, acts.v)
            k.vop("reciprocal", FA[:, SE:SE + 1], FA[:, SE:SE + 1], sem=dve)
            vsync()
            k.vop("tensor_scalar_mul", FA[:, PRB:PRB + 8], FA[:, PRB:PRB + 8],
                  FA[:, SE:SE + 1], sem=dve)
            vsync()
            sel1 = FA[:, SEL1 + t * 8: SEL1 + (t + 1) * 8]
            sel2 = FA[:, SEL2 + t * 8: SEL2 + (t + 1) * 8]
            for (selx, mx, gcol) in ((sel1, M1, G1C), (sel2, M2, G2C)):
                k.vop("reduce_max", FA[:, mx:mx + 1], FA[:, PRB:PRB + 8], axis=AX.X, sem=dve)
                vsync()
                k.vop("tensor_scalar", FA[:, T1:T1 + 8], FA[:, PRB:PRB + 8],
                      FA[:, mx:mx + 1], None, op0=ALU.is_equal, sem=dve)
                vsync()
                k.vop("tensor_mul", FA[:, CAND:CAND + 8], FA[:, T1:T1 + 8],
                      self.eidxm8[:, :], sem=dve)
                vsync()
                k.vop("tensor_reduce", FA[:, I1:I1 + 1], FA[:, CAND:CAND + 8],
                      axis=AX.X, op=ALU.min, sem=dve)
                vsync()
                k.vop("tensor_scalar", selx, self.eidxm8[:, :], FA[:, I1:I1 + 1],
                      None, op0=ALU.is_equal, sem=dve)
                vsync()
                if mx == M1:
                    k.vop("tensor_scalar", FA[:, SELT:SELT + 8], selx, -1.0, 1.0,
                          op0=ALU.mult, op1=ALU.add, sem=dve)
                    vsync()
                    k.vop("tensor_mul", FA[:, PRB:PRB + 8], FA[:, PRB:PRB + 8],
                          FA[:, SELT:SELT + 8], sem=dve)
                    vsync()
            k.vop("tensor_add", FA[:, RD:RD + 1], FA[:, M1:M1 + 1], FA[:, M2:M2 + 1], sem=dve)
            vsync()
            k.vop("tensor_scalar_max", FA[:, RD:RD + 1], FA[:, RD:RD + 1], MOE_EPS, sem=dve)
            vsync()
            k.vop("reciprocal", FA[:, RD:RD + 1], FA[:, RD:RD + 1], sem=dve)
            vsync()
            k.vop("tensor_mul", FA[:, G1C + t:G1C + t + 1], FA[:, M1:M1 + 1],
                  FA[:, RD:RD + 1], sem=dve)
            k.vop("tensor_mul", FA[:, G2C + t:G2C + t + 1], FA[:, M2:M2 + 1],
                  FA[:, RD:RD + 1], sem=dve)
            ev_t[t] = dve.v
            k.w("tensor", dve, ev_t[t])
            k.mm(self.ps[4][0:1, 0:8], self.onesf[:, 0:1], sel1,
                 start=(t == 0), stop=(t == 31))
            k.mm(self.ps[5][0:1, 0:8], self.onesf[:, 0:1], sel2,
                 start=(t == 0), stop=(t == 31), sem=(pe if t == 31 else None))
        cnt_pe = pe.v
        k.vop("memset", FA[0:1, CC:CC + 128], 0.0, sem=dve)
        vsync()
        k.w("vector", pe, cnt_pe)
        k.vop("tensor_copy", FA[0:1, CC:CC + 8], self.ps[4][0:1, 0:8])
        k.vop("tensor_copy", FA[0:1, CC + 8:CC + 16], self.ps[5][0:1, 0:8], sem=dve)
        k.w("gpsimd", dve, dve.v)
        k.dma("gpsimd", dd["ccin"][:, :], FA[0:1, CC:CC + 128], s["GP"])
        k.w("gpsimd", s["GP"], s["GP"].v)
        k.gop("collective_compute", "AllGather", ALU.bypass,
              replica_groups=[[0, 1, 2, 3, 4, 5, 6, 7]],
              ins=[dd["ccin"].tensor.ap().opt()],
              outs=[dd["ccout"].tensor.ap().opt()], sem=s["GP"], inc=1)
        k.w("gpsimd", s["GP"], s["GP"].v)
        k.dma("gpsimd", FA[0:8, CCO:CCO + 128], dd["ccout"][:, :], s["GP"])
        k.dma("gpsimd", FA[0:8, CCO + 128:CCO + 129], dd["coremask"][:, :], s["GP"])
        cm_done = s["GP"].v
        k.w("tensor", s["GP"], cm_done)
        k.mm(self.ps[2][0:1, 0:16], FA[0:8, CCO + 128:CCO + 129], FA[0:8, CCO:CCO + 16],
             start=True, stop=True)
        k.mm(self.ps[3][0:1, 0:16], self.onesf[0:8, 0:1], FA[0:8, CCO:CCO + 16],
             start=True, stop=True, sem=pe)
        pfx_pe = pe.v
        k.w("vector", pe, pfx_pe)
        k.vop("tensor_scalar_add", FA[0:1, OF1:OF1 + 8], self.ps[2][0:1, 0:8], -1.0)
        k.vop("tensor_copy", FA[0:1, OF2:OF2 + 8], self.ps[3][0:1, 0:8], sem=dve)
        vsync()
        k.vop("scalar_tensor_tensor", FA[0:1, OF2:OF2 + 8], self.ps[2][0:1, 8:16], -1.0,
              FA[0:1, OF2:OF2 + 8], op0=ALU.add, op1=ALU.add, sem=dve)
        vsync()
        k.w("tensor", dve, dve.v)
        k.mm(self.ps[2][:, 32:40], self.onesf[0:1, :], FA[0:1, OF1:OF1 + 8],
             start=True, stop=True)
        k.mm(self.ps[3][:, 32:40], self.onesf[0:1, :], FA[0:1, OF2:OF2 + 8],
             start=True, stop=True, sem=pe)
        k.w("vector", pe, pe.v)
        k.vop("tensor_copy", FA[:, OF1B:OF1B + 8], self.ps[2][:, 32:40])
        k.vop("tensor_copy", FA[:, OF2B:OF2B + 8], self.ps[3][:, 32:40], sem=dve)
        vsync()
        k.vop("memset", FA[0:1, B1R:B1R + 8], 0.0)
        k.vop("memset", FA[0:1, B2R:B2R + 8], 0.0, sem=dve)
        vsync()
        gt_done = {}
        base_upd = {0: 0, 1: 0}
        for t in range(32):
            gtile = FA[:, CAND:CAND + 8]
            for (j, SEL, BR, OFB) in ((0, SEL1, B1R, OF1B), (1, SEL2, B2R, OF2B)):
                selx = FA[:, SEL + t * 8: SEL + (t + 1) * 8]
                gcol = (G1C if j == 0 else G2C) + t
                bank = self.ps[j]
                bsum = self.ps[2 + j]
                k.w("tensor", dve, dve.v)
                k.mm(bank[:, 0:8], self.ltri[:, :], selx, start=True, stop=False)
                k.mm(bank[:, 0:8], self.onesf[0:1, :], FA[0:1, BR:BR + 8],
                     start=False, stop=True)
                k.mm(bsum[0:1, 0:8], self.onesf[:, 0:1], selx, start=True, stop=True,
                     sem=pe)
                cum_pe = pe.v
                k.w("vector", pe, cum_pe)
                k.vop("tensor_add", FA[0:1, BR:BR + 8], FA[0:1, BR:BR + 8],
                      bsum[0:1, 0:8], sem=dve)
                base_upd[j] = dve.v
                vsync()
                k.vop("tensor_add", FA[:, T1:T1 + 8], bank[:, 0:8], FA[:, OFB:OFB + 8], sem=dve)
                vsync()
                k.vop("tensor_mul", FA[:, T1:T1 + 8], FA[:, T1:T1 + 8], selx, sem=dve)
                vsync()
                k.vop("reduce_sum", FA[:, RNK:RNK + 1], FA[:, T1:T1 + 8], axis=AX.X, sem=dve)
                vsync()
                k.vop("tensor_single_scalar", FA[:, KP:KP + 1], FA[:, RNK:RNK + 1],
                      float(CAP), op=ALU.is_lt, sem=dve)
                vsync()
                k.vop("tensor_mul", FA[:, KP:KP + 1], FA[:, KP:KP + 1],
                      FA[:, gcol:gcol + 1], sem=dve)
                vsync()
                if j == 0:
                    k.vop("tensor_scalar_mul", gtile, selx, FA[:, KP:KP + 1], sem=dve)
                else:
                    k.vop("scalar_tensor_tensor", gtile, selx, FA[:, KP:KP + 1],
                          gtile, op0=ALU.mult, op1=ALU.add, sem=dve)
                vsync()
            k.w("tensor", dve, dve.v)
            if t >= 2:
                k.w("tensor", dve, gt_done[t - 2])
            k.tr(self.ps[4 + t % 2][0:8, 0:128], gtile, self.identf[:, :], sem=pe)
            k.w("vector", pe, pe.v)
            k.vop("tensor_copy", BF[0:8, GTC + t * 128: GTC + (t + 1) * 128],
                  self.ps[4 + t % 2][0:8, 0:128], sem=dve)
            gt_done[t] = dve.v
        k.w("gpsimd", dve, gt_done[31])
        k.dma("gpsimd", dd["gtd"][:, :], BF[0:8, GTC:GTC + T], st)
        self.fence(k, s)

        WA, WB = 0, 32768
        XNC = [65536, 69632]
        HCB = [73728, 81920]
        HST = [3072, 3584]
        YST = [4096, 4352]
        GB = [0, 512]
        F1B = 2048
        wa_done = [0] * NE
        wb_done = [0] * NE
        pha_done = [0] * NE
        phb_done = [0] * NE
        hA_st = [0] * NE
        hB_ld = [0] * NE
        gb_used = {}
        gt_read = {}
        hstore = {}
        ystore = {}
        for e in range(NE):
            hb = dd["hbA"] if e % 2 == 0 else dd["hbB"]
            if e >= 1:
                k.w("gpsimd", pe, pha_done[e - 1])
            for q in range(4):
                k.dma("gpsimd", sb3(BF, WA + q * 8192, 2, 4096),
                      ap3(dd["fc1T"], e * 1024 + q * 256, 2, 0, 4096), wt)
            k.dma("gpsimd", FA[:, F1B + (e % 2) * 32:F1B + (e % 2) * 32 + 32],
                  dd["fc1b"][e * 128:(e + 1) * 128, :], wt)
            wa_done[e] = wt.v
            if e >= 1:
                k.w("gpsimd", pe, phb_done[e - 1])
            for q in range(4):
                k.dma("gpsimd", sb3(BF, WB + q * 8192, 8, 1024),
                      ap3(dd["fc2"], e * 4096 + q * 1024, 8, 0, 1024), wt)
            wb_done[e] = wt.v
            ga_done = {}
            xn_read = {}
            for c in range(8):
                par = c % 2
                if c >= 2:
                    k.w("sync", pe, xn_read[c - 2])
                k.dma("sync", sb3(BF, XNC[par], 8, 512), ap3(dd["xnS"], 0, 8, c * 512, 512), ld)
                ldv = ld.v
                for oc in range(32):
                    g = c * 32 + oc
                    bank = self.ps[g % 2]
                    for sl in range(8):
                        if sl == 0:
                            k.w("tensor", ld, ldv)
                            if g == 0:
                                k.w("tensor", wt, wa_done[e])
                            if g >= 2:
                                k.w("tensor", acts, ga_done[g - 2])
                        k.mm(bank[:, :],
                             BF[:, WA + sl * 4096 + oc * 128: WA + sl * 4096 + (oc + 1) * 128],
                             BF[:, XNC[par] + sl * 512: XNC[par] + (sl + 1) * 512],
                             start=(sl == 0), stop=(sl == 7), sem=(pe if sl == 7 else None))
                    mmv = pe.v
                    k.w("scalar", pe, mmv)
                    gk = e * 256 + g
                    if gk >= 2:
                        k.w("scalar", st, hstore.get(gk - 2, 0))
                    hs = FA[:, HST[g % 2]: HST[g % 2] + 512]
                    k.act(hs, bank[:, :], AF.Silu,
                          bias=FA[:, F1B + (e % 2) * 32 + oc:F1B + (e % 2) * 32 + oc + 1],
                          sem=acts)
                    ga_done[g] = acts.v
                    k.w("gpsimd", acts, acts.v)
                    if g == 0 and e >= 2:
                        k.w("gpsimd", ld, hB_ld[e - 2])
                    k.dma("gpsimd", hb[oc * 128:(oc + 1) * 128, c * 512:(c + 1) * 512],
                          hs, st)
                    hstore[gk] = st.v
                xn_read[c] = pe.v
            pha_done[e] = pe.v
            hA_st[e] = st.v
            yb_done = {}
            hread = {}
            gb_d = {}
            for c in range(16):
                par = c % 2
                if c >= 2:
                    k.w("sync", pe, hread[c - 2])
                k.w("sync", st, hA_st[e])
                for q in range(4):
                    k.dma("sync", sb3(BF, HCB[par] + q * 2048, 8, 256),
                          ap3(hb, q * 1024, 8, c * 256, 256), ld)
                ldv = ld.v
                if c % 2 == 0:
                    cc2 = c // 2
                    gidx = e * 8 + cc2
                    if gidx >= 1:
                        k.w("sync", pe, gt_read.get(gidx - 1, 0))
                    k.dma("sync", self.gt_sb[:, :], dd["gtd"][:, cc2 * 512:(cc2 + 1) * 512], ld)
                    gt_ldv = ld.v
                    k.w("tensor", wt, wb_done[e])
                    k.w("tensor", ld, gt_ldv)
                    if gidx >= 2:
                        k.w("tensor", dve, gb_used.get(gidx - 2, 0))
                    k.mm(self.ps[4 + cc2 % 2][:, :], self.ohot_sb[0:8, e * 128:(e + 1) * 128],
                         self.gt_sb[0:8, :], start=True, stop=True, sem=pe)
                    gt_read[gidx] = pe.v
                    gbv = pe.v
                    k.w("vector", pe, gbv)
                    gb = FA[:, GB[cc2 % 2]:GB[cc2 % 2] + 512]
                    k.vop("tensor_copy", gb, self.ps[4 + cc2 % 2][:, :], sem=dve)
                    gb_d[cc2] = dve.v
                for m in range(8):
                    g = c * 8 + m
                    bank = self.ps[g % 2]
                    for sl in range(32):
                        if sl == 0:
                            k.w("tensor", ld, ldv)
                            k.w("tensor", wt, wb_done[e])
                            if g >= 2:
                                k.w("tensor", dve, yb_done[g - 2])
                        k.mm(bank[:, 0:256],
                             BF[:, WB + sl * 1024 + m * 128: WB + sl * 1024 + (m + 1) * 128],
                             BF[:, HCB[par] + sl * 256: HCB[par] + (sl + 1) * 256],
                             start=(sl == 0), stop=(sl == 31), sem=(pe if sl == 31 else None))
                    mmv = pe.v
                    k.w("vector", pe, mmv)
                    if m == 0:
                        k.w("vector", dve, gb_d[c // 2])
                    gk = e * 128 + g
                    if gk >= 2:
                        k.w("vector", st, ystore.get(gk - 2, 0))
                    ys = FA[:, YST[g % 2]: YST[g % 2] + 256]
                    k.vop("tensor_mul", ys, bank[:, 0:256],
                          FA[:, GB[(c // 2) % 2] + (c % 2) * 256: GB[(c // 2) % 2] + (c % 2) * 256 + 256],
                          sem=dve)
                    yb_done[g] = dve.v
                    gb_used[e * 8 + c // 2] = dve.v
                    k.w("gpsimd", dve, dve.v)
                    k.dma("gpsimd",
                          dd["ye"][e * 128:(e + 1) * 128,
                                   m * 4096 + c * 256: m * 4096 + (c + 1) * 256], ys, st)
                    ystore[gk] = st.v
                hread[c] = pe.v
            hB_ld[e] = ld.v
            phb_done[e] = pe.v
        self.fence(k, s)

        CH = 256
        YEC = [0, 16384]
        RES, ACC, XNF = 0, 2048, 3072
        X3C, OSG, SQT, RWS = 3584, 4096, 4608, 5120
        k.dma("gpsimd", FA[:, RWS + 768:RWS + 776], dd["gammac"][:, :], wt)
        k.dma("gpsimd", FA[:, RWS + 776:RWS + 784], dd["betac"][:, :], wt)
        wt_f = wt.v
        GAM = RWS + 768
        res_done = {}
        outst = {}
        sqd = {}
        gt_read3 = {}
        for c in range(16):
            par = c % 2
            if c >= 2:
                k.w("sync", dve, res_done.get(("acc", (c - 2) * 8 + 7), 0))
            if c >= 1:
                k.w("sync", pe, gt_read3.get(c - 1, 0))
            k.dma("sync", self.gt_sb[:, 0:256], dd["gtd"][:, c * 256:(c + 1) * 256], ld)
            gt3_ldv = ld.v
            for e in range(NE):
                ysrc = dd["ye"][e * 128:(e + 1) * 128, :].rearrange(
                    "p (n c2) -> p n c2", c2=4096)[:, :, c * 256:(c + 1) * 256]
                k.dma("sync", sb3(BF, YEC[par] + e * 2048, 8, 256), ysrc, ld)
            ye_ldv = ld.v
            for m in range(8):
                g = c * 8 + m
                if g >= 2:
                    k.w("sync", dve, res_done.get(("acc", g - 2), 0))
                xnb = FA[:, XNF + (m % 2) * 256: XNF + (m % 2) * 256 + 256]
                k.dma("sync", xnb,
                      dd["xn3f"][m * 128:(m + 1) * 128, c * 256:(c + 1) * 256], ld)
                ldv = ld.v
                if g >= 2:
                    k.w("tensor", dve, res_done.get(("bias", g - 2), 0))
                    if m < 2 and c >= 1:
                        k.w("tensor", dve, res_done.get(("x3", c * 8 - 1), 0))
                if m == 0:
                    k.w("tensor", ld, gt3_ldv)
                k.mm(self.ps[4 + g % 2][:, 0:256],
                     self.fc2b_sb[0:8, m * 128:(m + 1) * 128],
                     self.gt_sb[0:8, 0:256], start=True, stop=True, sem=pe)
                biasv = pe.v
                gt_read3[c] = pe.v
                k.w("vector", ld, ldv)
                if m == 0:
                    k.w("vector", ld, ye_ldv)
                ye_at = lambda e_: YEC[par] + e_ * 2048 + m * 256
                a_ = FA[:, ACC:ACC + 256]
                b_ = FA[:, ACC + 256:ACC + 512]
                c_ = FA[:, ACC + 512:ACC + 768]
                d_ = FA[:, ACC + 768:ACC + 1024]
                k.vop("tensor_add", a_, BF[:, ye_at(0):ye_at(0) + 256], BF[:, ye_at(1):ye_at(1) + 256])
                k.vop("tensor_add", b_, BF[:, ye_at(2):ye_at(2) + 256], BF[:, ye_at(3):ye_at(3) + 256])
                k.vop("tensor_add", c_, BF[:, ye_at(4):ye_at(4) + 256], BF[:, ye_at(5):ye_at(5) + 256])
                k.vop("tensor_add", d_, BF[:, ye_at(6):ye_at(6) + 256], BF[:, ye_at(7):ye_at(7) + 256])
                k.vop("tensor_add", a_, a_, b_)
                k.vop("tensor_add", c_, c_, d_)
                k.vop("tensor_add", a_, a_, c_)
                k.w("vector", pe, biasv)
                k.vop("tensor_add", a_, a_, self.ps[4 + g % 2][:, 0:256], sem=dve)
                res_done[("bias", g)] = dve.v
                rm = FA[:, RES + m * 256: RES + (m + 1) * 256]
                k.vop("tensor_add", rm, a_, xnb, sem=dve)
                res_done[("acc", g)] = dve.v
            res_v = dve.v
            bsum, bsq = self.ps[par], self.ps[2 + par]
            for m in range(8):
                if m == 0:
                    k.w("scalar", dve, res_v)
                sqb = FA[:, SQT + (m % 2) * 256: SQT + (m % 2) * 256 + 256]
                if (c, m - 2) in sqd:
                    k.w("scalar", pe, sqd[(c, m - 2)])
                elif c >= 1 and (c - 1, m + 6) in sqd:
                    k.w("scalar", pe, sqd[(c - 1, m + 6)])
                k.act(sqb, FA[:, RES + m * 256: RES + (m + 1) * 256], AF.Square, sem=acts)
                sqv = acts.v
                if m == 0:
                    k.w("tensor", dve, res_v)
                    if c >= 2:
                        k.w("tensor", dve, res_done[("stats", c - 2)])
                k.mm(bsum[0:1, 256:256 + CH], self.onesf[:, 0:1],
                     FA[:, RES + m * 256: RES + (m + 1) * 256], start=(m == 0), stop=(m == 7))
                k.w("tensor", acts, sqv)
                k.mm(bsq[0:1, 256:256 + CH], self.onesf[:, 0:1], sqb,
                     start=(m == 0), stop=(m == 7), sem=pe)
                sqd[(c, m)] = pe.v
            stat_pe = pe.v
            mu = FA[0:1, RWS:RWS + CH]
            v1 = FA[0:1, RWS + 256:RWS + 256 + CH]
            rs = FA[0:1, RWS + 512:RWS + 512 + CH]
            k.w("vector", pe, stat_pe)
            if c >= 2:
                k.w("vector", pe, res_done[("bcast", c - 2)])
            k.vop("tensor_scalar_mul", mu, bsum[0:1, 256:256 + CH], 1.0 / 1024.0, sem=dve)
            vsync()
            k.vop("tensor_mul", v1, mu, mu, sem=dve)
            vsync()
            k.vop("scalar_tensor_tensor", v1, bsq[0:1, 256:256 + CH], 1.0 / 1024.0, v1,
                  op0=ALU.mult, op1=ALU.subtract, sem=dve)
            k.w("scalar", dve, dve.v)
            k.act(v1, v1, AF.Sqrt, bias=self.smallf[0:1, 0:1], sem=acts)
            k.w("vector", acts, acts.v)
            k.vop("reciprocal", rs, v1, sem=dve)
            res_done[("stats", c)] = dve.v
            k.w("tensor", dve, res_done[("stats", c)])
            if c >= 2:
                k.w("tensor", dve, res_done[("x3", (c - 1) * 8 + 7)])
            k.mm(self.ps[4][:, 256:256 + CH], self.onesf[0:1, :], mu, start=True, stop=True)
            k.mm(self.ps[5][:, 256:256 + CH], self.onesf[0:1, :], rs, start=True, stop=True,
                 sem=pe)
            bc_pe = pe.v
            res_done[("bcast", c)] = bc_pe
            k.w("vector", pe, bc_pe)
            for m in range(8):
                g = c * 8 + m
                x3b = FA[:, X3C + (m % 2) * 256: X3C + (m % 2) * 256 + 256]
                if g >= 2:
                    k.w("sync", dve, res_done.get(("x3", g - 2), 0))
                k.dma("sync", x3b, dd[src][m * 128:(m + 1) * 128, c * 256:(c + 1) * 256], ld)
                ldx = ld.v
                rm = FA[:, RES + m * 256: RES + (m + 1) * 256]
                tm = FA[:, ACC:ACC + 256]
                t2 = FA[:, ACC + 256:ACC + 512]
                k.vop("tensor_sub", tm, rm, self.ps[4][:, 256:256 + CH])
                k.vop("tensor_mul", t2, tm, self.ps[5][:, 256:256 + CH])
                k.vop("tensor_scalar", tm, t2, FA[:, GAM + m:GAM + m + 1],
                      FA[:, GAM + 8 + m:GAM + 8 + m + 1], op0=ALU.mult, op1=ALU.add)
                k.w("vector", ld, ldx)
                k.w("vector", wt, wt_f)
                if g >= 2:
                    k.w("vector", st, outst.get(g - 2, 0))
                osg = FA[:, OSG + (m % 2) * 256: OSG + (m % 2) * 256 + 256]
                k.vop("tensor_add", osg, tm, x3b, sem=dve)
                res_done[("x3", g)] = dve.v
                k.w("gpsimd", dve, dve.v)
                k.dma("gpsimd", dd[dst][m * 128:(m + 1) * 128, c * 256:(c + 1) * 256], osg, st)
                outst[g] = st.v
        k.w("gpsimd", st, st.v)

    def build_in_maps(self, w, percore):
        maps = []
        for b in range(NCORES):
            m = {}
            for nm, sh, dty in WSPECS:
                if nm == "coremask":
                    m[nm] = percore[b]["coremask"]
                else:
                    m[nm] = w[nm]
            m["xT"] = percore[b]["xT"]
            maps.append(m)
        return maps


def run_stages(stages, inputs, out_name="outT", dbg_out=(), raw=False):
    w, percore = prep_host(inputs)
    p = Prog(stages, dbg_out=dbg_out)
    maps = p.build_in_maps(w, percore)
    res = run_bass_kernel_spmd(p.nc, maps, core_ids=list(range(NCORES)))
    if raw:
        return res
    return np.stack([np.asarray(r[out_name], dtype=np.float32).T for r in res.results])


def kernel(**inputs):
    stages = [
        ("ln", "xT", "xnS"),
        ("l0a", "xT", "x1"),
        ("ln", "x1", "xnS"),
        ("l0f", "x1", "x2"),
        ("ln", "x2", "xnS"),
        ("l1a", "x2", "x3"),
        ("ln", "x3", "xnS", "xn3f"),
        ("moe", "x3", "outT"),
    ]
    return run_stages(stages, inputs)


# version 06
# speedup vs baseline: 1.3406x; 1.0308x over previous
import numpy as np
import ml_dtypes
from contextlib import ExitStack
import concourse.bass as bass
import concourse.mybir as mybir
from concourse.bass_utils import run_bass_kernel_spmd

F32 = mybir.dt.float32
BF16 = mybir.dt.bfloat16
AF = mybir.ActivationFunctionType
ALU = mybir.AluOpType
AX = mybir.AxisListType
BFNP = ml_dtypes.bfloat16

D = 1024
T = 4096
NCORES = 8
HID = 4096
NH, HD = 16, 64
NE = 8
CAP = 10240
MOE_EPS = 1.1920929e-07

ENGS = ("sync", "tensor", "scalar", "vector", "gpsimd")


class Sem:
    __slots__ = ("h", "v")

    def __init__(self, h):
        self.h = h
        self.v = 0


class K:

    def __init__(self, nc, active, eng):
        self.nc = nc
        self.active = active
        self.eng = eng

    def w(self, engname, sem, val):
        if val > 0 and self.active == engname:
            self.eng.wait_ge(sem.h, val)

    def dma(self, engname, dst, src, sem, **kw):
        sem.v += 16
        if self.active == engname:
            self.eng.dma_start(dst, src, **kw).then_inc(sem.h, 16)
        return sem.v

    def mm(self, out, lhsT, rhs, start, stop, sem=None, tp=None):
        if sem is not None:
            sem.v += 1
        if self.active == "tensor":
            kw = {}
            if tp is not None:
                kw["tile_position"] = tp
            i = self.eng.matmul(out, lhsT, rhs, start=start, stop=stop, **kw)
            if sem is not None:
                i.then_inc(sem.h, 1)

    def tr(self, out, in_, ident, sem=None):
        if sem is not None:
            sem.v += 1
        if self.active == "tensor":
            i = self.eng.transpose(out, in_, ident)
            if sem is not None:
                i.then_inc(sem.h, 1)

    def act(self, out, in_, func, bias=0.0, scale=1.0, accum=None, sem=None):
        if sem is not None:
            sem.v += 1
        if self.active == "scalar":
            kw = {}
            if accum is not None:
                kw["accum_out"] = accum
            i = self.eng.activation(out, in_, func, bias=bias, scale=scale, **kw)
            if sem is not None:
                i.then_inc(sem.h, 1)

    def vop(self, meth, *args, sem=None, **kw):
        if sem is not None:
            sem.v += 1
        if self.active == "vector":
            i = getattr(self.eng, meth)(*args, **kw)
            if sem is not None:
                i.then_inc(sem.h, 1)

    def gop(self, meth, *args, sem=None, inc=1, **kw):
        if sem is not None:
            sem.v += inc
        if self.active == "gpsimd":
            i = getattr(self.eng, meth)(*args, **kw)
            if sem is not None:
                i.then_inc(sem.h, inc)

    def barrier(self, sbar):
        sbar.v += len(ENGS)
        tgt = sbar.v
        if self.eng is not None:
            self.eng.nop().then_inc(sbar.h, 1)
            self.eng.wait_ge(sbar.h, tgt)


def ap3(dram, r0, nslab, c0, ncols):
    return dram[r0:r0 + nslab * 128, c0:c0 + ncols].rearrange(
        "(n p) c -> p n c", p=128)


def sb3(sbt, col0, nslab, ncols):
    return sbt[:, col0:col0 + nslab * ncols].rearrange(
        "p (n c) -> p n c", c=ncols)



def col128(v):
    return np.ascontiguousarray(v.reshape(-1, 128).T)


def prep_host(inputs):
    f = np.ascontiguousarray
    w = {}
    w["Wq"] = f(inputs["L0_attn_Wq"])
    w["Wk"] = f(inputs["L0_attn_Wk"])
    w["Wv"] = f(inputs["L0_attn_Wv"])
    w["bq8"] = col128(inputs["L0_attn_bq"])
    w["bk8"] = col128(inputs["L0_attn_bk"])
    w["bv"] = f(inputs["L0_attn_bv"].reshape(1, D))
    w["w1T"] = f(inputs["L0_ffn_w1_w"].T)
    w["w2T"] = f(inputs["L0_ffn_w2_w"].T)
    w["h1T"] = f(inputs["L0_ffn_h1_w"].T)
    w["h2T"] = f(inputs["L0_ffn_h2_w"].T)
    w["w3T"] = f(inputs["L0_ffn_w3_w"].T)
    w["s1c"] = col128(inputs["L0_ffn_w1_s"])
    w["s2c"] = col128(inputs["L0_ffn_w2_s"])
    w["sh1c"] = col128(inputs["L0_ffn_h1_s"])
    w["sh2c"] = col128(inputs["L0_ffn_h2_s"])
    w["s3c"] = col128(inputs["L0_ffn_w3_s"])
    w["Wqkv"] = f(inputs["L1_attn_Wqkv"])
    w["Wproj"] = f(inputs["L1_attn_Wproj"])
    w["bprojc"] = col128(inputs["L1_attn_bproj"])
    w["qs_row"] = f(inputs["L1_attn_q_scale"].reshape(1, HD))
    w["ks_row"] = f(inputs["L1_attn_k_scale"].reshape(1, HD))
    w["wg"] = f(inputs["L1_ffn_wg"])
    w["fc1T"] = f(np.transpose(inputs["L1_ffn_fc1_w"], (0, 2, 1)).reshape(NE * D, HID))
    w["fc2"] = f(inputs["L1_ffn_fc2_w"].reshape(NE * HID, D))
    w["fc1b"] = f(np.concatenate([col128(inputs["L1_ffn_fc1_b"][e]) for e in range(NE)], axis=0))
    w["fc2b"] = f(inputs["L1_ffn_fc2_b"])
    oh = np.zeros((NE, NE * 128), dtype=BFNP)
    for e in range(NE):
        oh[e, e * 128:(e + 1) * 128] = 1.0
    w["ohot"] = oh
    w["gammac"] = col128(inputs["L1_ffn_ln_gamma"])
    w["betac"] = col128(inputs["L1_ffn_ln_beta"])

    w["identb"] = np.eye(128, dtype=BFNP)
    w["identf"] = np.eye(128, dtype=np.float32)
    p = np.arange(128)[:, None]
    c = np.arange(384)[None, :]
    w["cmask"] = (p <= c - 128).astype(BFNP)
    w["ltri"] = np.triu(np.ones((128, 128), dtype=np.float32))
    w["eidxm8"] = np.broadcast_to(
        (np.arange(8, dtype=np.float32) - 8.0)[None, :], (128, 8)).copy()

    percore = []
    x = inputs["L0_attn_x"]
    for b in range(NCORES):
        cm = np.zeros((8, 1), dtype=np.float32)
        cm[:b] = 1.0
        percore.append({"xT": f(x[b].T.astype(np.float32)), "coremask": cm})
    return w, percore


WSPECS = [
    ("Wq", [D, D], F32), ("Wk", [D, D], F32), ("Wv", [D, D], F32),
    ("bq8", [128, 8], F32), ("bk8", [128, 8], F32), ("bv", [1, D], F32),
    ("w1T", [D, HID], F32), ("w2T", [D, HID], F32),
    ("h1T", [HID, HID], F32), ("h2T", [HID, HID], F32), ("w3T", [HID, D], F32),
    ("s1c", [128, 32], F32), ("s2c", [128, 32], F32), ("sh1c", [128, 32], F32),
    ("sh2c", [128, 32], F32), ("s3c", [128, 8], F32),
    ("Wqkv", [D, 3 * D], F32), ("Wproj", [D, D], F32), ("bprojc", [128, 8], F32),
    ("qs_row", [1, HD], F32), ("ks_row", [1, HD], F32),
    ("wg", [D, NE], F32), ("fc1T", [NE * D, HID], F32), ("fc2", [NE * HID, D], F32),
    ("fc1b", [NE * 128, 32], F32), ("fc2b", [NE, D], F32),
    ("gammac", [128, 8], F32), ("betac", [128, 8], F32),
    ("identf", [128, 128], F32), ("ltri", [128, 128], F32),
    ("eidxm8", [128, 8], F32), ("coremask", [8, 1], F32),
    ("identb", [128, 128], BF16), ("cmask", [128, 384], BF16),
    ("ohot", [NE, NE * 128], BF16),
]

BF_COLS = 90112
FA_COLS = 6144


class Prog:
    def __init__(self, stages, ext_in="xT", ext_out="outT", dbg_out=()):
        self.stages = stages
        self.dbg_out = set(dbg_out)
        nc = bass.Bass("TRN2", target_bir_lowering=False)
        self.nc = nc
        self.d = {}

        def dram(name, shape, dtype, kind):
            self.d[name] = nc.dram_tensor(name, shape, dtype, kind=kind).ap()

        dram("xT", [D, T], F32, "ExternalInput")
        dram("outT", [D, T], F32, "ExternalOutput")
        for nm, sh, dty in WSPECS:
            dram(nm, sh, dty, "ExternalInput")
        for nm, sh, dty in [
                ("x1", [D, T], F32), ("x2", [D, T], F32), ("x3", [D, T], F32),
                ("xnS", [D, T], BF16), ("xn3f", [D, T], F32),
                ("l0qt", [D, T], BF16), ("l0kt", [D, T], BF16), ("l0ou", [D, T], F32),
                ("hbA", [HID, T], BF16), ("hbB", [HID, T], BF16),
                ("l1qt", [128, 8 * T], BF16), ("l1kt", [128, 8 * T], BF16),
                ("l1ou", [128, 8 * T], BF16),
                ("ccin", [1, 128], F32), ("ccout", [8, 128], F32),
                ("ye", [NE * 128, 8 * T], BF16), ("gtd", [8, T], BF16),
                ("resd", [D, T], F32)]:
            dram(nm, sh, dty, "ExternalOutput" if nm in self.dbg_out else "Internal")

        self.stack = ExitStack()
        e = self.stack.enter_context
        self.BF = e(nc.sbuf_tensor("BF", [128, BF_COLS], BF16))
        self.FA = e(nc.sbuf_tensor("FA", [128, FA_COLS], F32))
        self.identb = e(nc.sbuf_tensor("identb_sb", [128, 128], BF16))
        self.identf = e(nc.sbuf_tensor("identf_sb", [128, 128], F32))
        self.onesf = e(nc.sbuf_tensor("onesf_sb", [128, 128], F32))
        self.cmask = e(nc.sbuf_tensor("cmask_sb", [128, 384], BF16))
        self.ltri = e(nc.sbuf_tensor("ltri_sb", [128, 128], F32))
        self.eidxm8 = e(nc.sbuf_tensor("eidxm8_sb", [128, 8], F32))
        self.smallf = e(nc.sbuf_tensor("smallf_sb", [128, 8], F32))
        self.ohot_sb = e(nc.sbuf_tensor("ohot_sb", [8, NE * 128], BF16))
        self.gt_sb = e(nc.sbuf_tensor("gt_sb", [8, 512], BF16))
        self.fc2b_sb = e(nc.sbuf_tensor("fc2b_sb", [8, D], BF16))
        self.ps = [e(nc.psum_tensor(f"ps{j}", [128, 512], F32)) for j in range(6)]
        self.pb = [e(nc.psum_tensor(f"pb{j}", [128, 1024], BF16)) for j in range(2)]
        self.sems = {}
        for pos in range(len(stages) + 1):
            for nm in ("LD", "WT", "PE", "ACT", "DVE", "ST", "GP"):
                key = f"{nm}{pos}"
                self.sems[key] = Sem(e(nc.semaphore(key)))
        self.sems["BAR"] = Sem(e(nc.semaphore("BAR")))

        with nc.Block() as blk:
            for name, meth in (("sync", blk.sync), ("tensor", blk.tensor),
                               ("scalar", blk.scalar), ("vector", blk.vector),
                               ("gpsimd", blk.gpsimd)):
                def body(eng, name=name):
                    for s in self.sems.values():
                        s.v = 0
                    self.walk(K(nc, name, eng))
                meth(body)
        self.stack.close()

    def walk(self, k):
        sm = self.sems
        pre = sm["LD0"]
        k.dma("gpsimd", self.identb[:, :], self.d["identb"][:, :], pre)
        k.dma("gpsimd", self.identf[:, :], self.d["identf"][:, :], pre)
        k.dma("gpsimd", self.cmask[:, :], self.d["cmask"][:, :], pre)
        k.dma("gpsimd", self.ltri[:, :], self.d["ltri"][:, :], pre)
        k.dma("gpsimd", self.eidxm8[:, :], self.d["eidxm8"][:, :], pre)
        k.gop("memset", self.onesf[:, :], 1.0, sem=sm["GP0"])
        k.gop("memset", self.smallf[:, 0:1], 1e-5, sem=sm["GP0"])
        for e in ("vector", "tensor", "scalar"):
            k.w(e, pre, pre.v)
            k.w(e, sm["GP0"], sm["GP0"].v)
        k.barrier(sm["BAR"])
        for idx, st in enumerate(self.stages, start=1):
            s = {nm: sm[f"{nm}{idx}"] for nm in ("LD", "WT", "PE", "ACT", "DVE", "ST", "GP")}
            getattr(self, "st_" + st[0])(k, s, *st[1:])
            k.barrier(sm["BAR"])

    def fence(self, k, s):
        k.w("gpsimd", s["ST"], s["ST"].v)
        k.w("gpsimd", s["WT"], s["WT"].v)
        k.w("sync", s["LD"], s["LD"].v)
        k.barrier(self.sems["BAR"])

    def st_ln(self, k, s, src, dstb, dstf=None):
        CH = 128
        NCH = T // CH
        FA, BF = self.FA, self.BF
        src_d, dstb_d = self.d[src], self.d[dstb]
        dstf_d = self.d[dstf] if dstf else None
        XB = [0, 1024]
        SQ = [2048, 2176]
        TMP = 2304
        FST = [2432, 3456]
        ROWS = 4480
        ld, pe, acts, dve, st = s["LD"], s["PE"], s["ACT"], s["DVE"], s["ST"]

        mmsq = {}
        stats_done = [0] * NCH
        bcast_done = [0] * NCH
        apply_done = [0] * NCH
        store_done = [0] * NCH
        for c in range(NCH):
            par = c % 2
            mu = FA[0:1, ROWS + par * 512: ROWS + par * 512 + CH]
            v1 = FA[0:1, ROWS + par * 512 + 128: ROWS + par * 512 + 128 + CH]
            rs = FA[0:1, ROWS + par * 512 + 256: ROWS + par * 512 + 256 + CH]
            sd = FA[0:1, ROWS + par * 512 + 384: ROWS + par * 512 + 384 + CH]
            if c >= 2:
                k.w("sync", dve, apply_done[c - 2])
            k.dma("sync", sb3(FA, XB[par], 8, CH), ap3(src_d, 0, 8, c * CH, CH), ld)
            ldv = ld.v
            bsum, bsq = self.ps[par], self.ps[2 + par]
            bmu, brs = self.ps[4], self.ps[5]
            for sl in range(8):
                xsl = FA[:, XB[par] + sl * CH: XB[par] + (sl + 1) * CH]
                sqb = FA[:, SQ[sl % 2]: SQ[sl % 2] + CH]
                if sl == 0:
                    k.w("scalar", ld, ldv)
                    if c >= 1:
                        k.w("scalar", pe, mmsq[(c - 1, 7)])
                elif sl in (1,) and c >= 1:
                    k.w("scalar", pe, mmsq[(c - 1, 7)])
                elif sl >= 2:
                    k.w("scalar", pe, mmsq[(c, sl - 2)])
                k.act(sqb, xsl, AF.Square, sem=acts)
                sqv = acts.v
                if sl == 0:
                    k.w("tensor", ld, ldv)
                    if c >= 2:
                        k.w("tensor", dve, stats_done[c - 2])
                k.mm(bsum[0:1, 0:CH], self.onesf[:, 0:1], xsl,
                     start=(sl == 0), stop=(sl == 7))
                k.w("tensor", acts, sqv)
                k.mm(bsq[0:1, 0:CH], self.onesf[:, 0:1], sqb,
                     start=(sl == 0), stop=(sl == 7), sem=pe)
                mmsq[(c, sl)] = pe.v
            k.w("vector", pe, mmsq[(c, 7)])
            if c >= 2:
                k.w("vector", pe, bcast_done[c - 2])
            k.vop("tensor_scalar_mul", mu, bsum[0:1, 0:CH], 1.0 / 1024.0, sem=dve)
            k.w("vector", dve, dve.v)
            k.vop("tensor_mul", v1, mu, mu, sem=dve)
            k.w("vector", dve, dve.v)
            k.vop("scalar_tensor_tensor", v1, bsq[0:1, 0:CH], 1.0 / 1024.0, v1,
                  op0=ALU.mult, op1=ALU.subtract, sem=dve)
            k.w("scalar", dve, dve.v)
            k.act(sd, v1, AF.Sqrt, bias=self.smallf[0:1, 0:1], sem=acts)
            k.w("vector", acts, acts.v)
            k.vop("reciprocal", rs, sd, sem=dve)
            stats_done[c] = dve.v
            if c >= 1:
                k.w("tensor", dve, apply_done[c - 1])
            k.w("tensor", dve, stats_done[c])
            k.mm(bmu[:, 0:CH], self.onesf[0:1, :], mu, start=True, stop=True)
            k.mm(brs[:, 0:CH], self.onesf[0:1, :], rs, start=True, stop=True, sem=pe)
            bcast_done[c] = pe.v
            k.w("vector", pe, bcast_done[c])
            if c >= 2:
                k.w("vector", st, store_done[c - 2])
            for sl in range(8):
                xsl = FA[:, XB[par] + sl * CH: XB[par] + (sl + 1) * CH]
                tm = FA[:, TMP: TMP + CH]
                ob = BF[:, par * 1024 + sl * CH: par * 1024 + (sl + 1) * CH]
                k.vop("tensor_sub", tm, xsl, bmu[:, 0:CH])
                k.vop("tensor_mul", ob, tm, brs[:, 0:CH],
                      sem=(dve if (sl == 7 and not dstf) else None))
                if dstf:
                    fo = FA[:, FST[par] + sl * CH: FST[par] + (sl + 1) * CH]
                    k.vop("tensor_mul", fo, tm, brs[:, 0:CH],
                          sem=(dve if sl == 7 else None))
            apply_done[c] = dve.v
            k.w("gpsimd", dve, apply_done[c])
            k.dma("gpsimd", ap3(dstb_d, 0, 8, c * CH, CH), sb3(BF, par * 1024, 8, CH), st)
            if dstf:
                k.dma("gpsimd", ap3(dstf_d, 0, 8, c * CH, CH), sb3(FA, FST[par], 8, CH), st)
            store_done[c] = st.v
        k.w("gpsimd", st, store_done[NCH - 1])

    def st_l0a(self, k, s, src, dst):
        FA, BF = self.FA, self.BF
        dd = self.d
        ld, wt, pe, acts, dve, st = s["LD"], s["WT"], s["PE"], s["ACT"], s["DVE"], s["ST"]
        WQ, WK, WV = 0, 8192, 16384
        XN = [24576, 28672]
        VT = 32768
        QS = [65536, 66048]
        KS = [66560, 67072]
        BQ8, BK8 = 0, 8
        BVB = 16
        k.dma("gpsimd", sb3(BF, WQ, 8, 1024), ap3(dd["Wq"], 0, 8, 0, 1024), wt)
        k.dma("gpsimd", sb3(BF, WK, 8, 1024), ap3(dd["Wk"], 0, 8, 0, 1024), wt)
        k.dma("gpsimd", sb3(BF, WV, 8, 1024), ap3(dd["Wv"], 0, 8, 0, 1024), wt)
        k.dma("gpsimd", FA[:, BQ8:BQ8 + 8], dd["bq8"][:, :], wt)
        k.dma("gpsimd", FA[:, BK8:BK8 + 8], dd["bk8"][:, :], wt)
        k.dma("gpsimd", FA[0:1, BVB:BVB + D], dd["bv"][:, :], wt)
        wt_all = wt.v
        k.w("tensor", wt, wt_all)
        k.mm(self.ps[4][:, :], self.onesf[0:1, :], FA[0:1, BVB:BVB + 512],
             start=True, stop=True)
        k.mm(self.ps[5][:, :], self.onesf[0:1, :], FA[0:1, BVB + 512:BVB + 1024],
             start=True, stop=True, sem=pe)
        k.w("vector", pe, pe.v)
        k.vop("tensor_copy", FA[:, BVB:BVB + 512], self.ps[4][:, :])
        k.vop("tensor_copy", FA[:, BVB + 512:BVB + 1024], self.ps[5][:, :], sem=dve)
        bvb_done = dve.v

        q_evac = {}
        v_evac = {}
        qk_store = {}
        xn_read = {}
        for c in range(8):
            par = c % 2
            if c >= 2:
                k.w("sync", pe, xn_read[c - 2])
            k.dma("sync", sb3(BF, XN[par], 8, 512), ap3(dd["xnS"], 0, 8, c * 512, 512), ld)
            ldv = ld.v
            for mi in range(16):
                which = "q" if mi < 8 else "k"
                m = mi % 8
                woff = WQ if which == "q" else WK
                boff = BQ8 if which == "q" else BK8
                qdram = dd["l0qt"] if which == "q" else dd["l0kt"]
                stg = QS if which == "q" else KS
                g = c * 16 + mi
                bank = self.ps[g % 2]
                for sl in range(8):
                    if sl == 0:
                        k.w("tensor", ld, ldv)
                        if g == 0:
                            k.w("tensor", wt, wt_all)
                        if g >= 2:
                            k.w("tensor", dve, q_evac[g - 2])
                    k.mm(bank[:, :], BF[:, woff + sl * 1024 + m * 128: woff + sl * 1024 + (m + 1) * 128],
                         BF[:, XN[par] + sl * 512: XN[par] + (sl + 1) * 512],
                         start=(sl == 0), stop=(sl == 7), sem=(pe if sl == 7 else None))
                mmv = pe.v
                k.w("vector", pe, mmv)
                if g >= 2:
                    k.w("vector", st, qk_store[g - 2])
                k.vop("tensor_scalar_add", BF[:, stg[m % 2]: stg[m % 2] + 512],
                      bank[:, :], FA[:, boff + m: boff + m + 1], sem=dve)
                q_evac[g] = dve.v
                k.w("gpsimd", dve, dve.v)
                k.dma("gpsimd", qdram[m * 128:(m + 1) * 128, c * 512:(c + 1) * 512],
                      BF[:, stg[m % 2]: stg[m % 2] + 512], st)
                qk_store[g] = st.v
            for sub in range(4):
                tile = c * 4 + sub
                for half in range(2):
                    gv = c * 8 + sub * 2 + half
                    bank = self.ps[2 + gv % 2]
                    for sl in range(8):
                        if sl == 0:
                            k.w("tensor", ld, ldv)
                            if gv >= 2:
                                k.w("tensor", dve, v_evac[gv - 2])
                        k.mm(bank[:, :],
                             BF[:, XN[par] + sl * 512 + sub * 128: XN[par] + sl * 512 + (sub + 1) * 128],
                             BF[:, WV + sl * 1024 + half * 512: WV + sl * 1024 + (half + 1) * 512],
                             start=(sl == 0), stop=(sl == 7), sem=(pe if sl == 7 else None))
                    mmv = pe.v
                    k.w("vector", pe, mmv)
                    if gv == 0:
                        k.w("vector", dve, bvb_done)
                    k.vop("tensor_add", BF[:, VT + tile * 1024 + half * 512: VT + tile * 1024 + (half + 1) * 512],
                          bank[:, :], FA[:, BVB + half * 512: BVB + (half + 1) * 512], sem=dve)
                    v_evac[gv] = dve.v
            xn_read[c] = pe.v
        p0_pe = pe.v
        p0_dve = dve.v
        p0_st = st.v

        KT = 0
        EB = 65536
        QB = 74240
        ZC, OUS, ZCOL, RZC = 1152, 1760, 2304, 2320
        k.w("gpsimd", st, p0_st)
        k.w("gpsimd", dve, p0_dve)
        k.dma("gpsimd", sb3(BF, KT, 8, 4096), ap3(dd["l0kt"], 0, 8, 0, 4096), wt)
        kt_done = wt.v
        k.vop("memset", FA[:, ZC:ZC + 544], 0.0, sem=dve)
        zc_zero = dve.v
        zci = 0
        av_done = [0] * 16
        exp_done = [0] * 16
        s_done = [0] * 16
        ou_stored = {}
        ou_evac = {}
        for i in range(16):
            nj = 2 * i + 2
            if i >= 1:
                k.w("sync", pe, s_done[i - 1])
            k.dma("sync", sb3(BF, QB, 8, 256), ap3(dd["l0qt"], 0, 8, i * 256, 256), ld)
            qldv = ld.v
            if i >= 1:
                k.w("scalar", pe, av_done[i - 1])
            for j in range(nj):
                bank = self.ps[j % 2]
                for sl in range(8):
                    if sl == 0:
                        k.w("tensor", ld, qldv)
                        if i == 0 and j == 0:
                            k.w("tensor", wt, kt_done)
                        if j >= 2:
                            k.w("tensor", acts, exp_j[j - 2])
                        elif i >= 1:
                            k.w("tensor", acts, exp_done[i - 1])
                    k.mm(bank[:, 0:256],
                         BF[:, KT + sl * 4096 + j * 128: KT + sl * 4096 + (j + 1) * 128],
                         BF[:, QB + sl * 256: QB + (sl + 1) * 256],
                         start=(sl == 0), stop=(sl == 7), sem=(pe if sl == 7 else None))
                if j == 0:
                    exp_j = {}
                smm = pe.v
                k.w("scalar", pe, smm)
                diag = (j >= 2 * i)
                etile = BF[:, EB + j * 256: EB + (j + 1) * 256]
                if not diag:
                    if zci == 0:
                        k.w("scalar", dve, zc_zero)
                    k.act(etile, bank[:, 0:256], AF.Exp, scale=1.0 / 32.0,
                          accum=FA[:, ZC + zci: ZC + zci + 1], sem=acts)
                else:
                    k.act(etile, bank[:, 0:256], AF.Exp, scale=1.0 / 32.0, sem=acts)
                    k.w("vector", acts, acts.v)
                    moff = 128 if j == 2 * i else 0
                    k.vop("tensor_mul", etile, etile, self.cmask[:, moff:moff + 256])
                    k.w("vector", dve, zc_zero)
                    k.vop("reduce_sum", FA[:, ZC + zci: ZC + zci + 1], etile,
                          axis=AX.X, sem=dve)
                exp_j[j] = acts.v
                zci += 1
            s_done[i] = pe.v
            exp_done[i] = acts.v
            mask_done_i = dve.v
            for m in range(8):
                bank = self.ps[2 + m % 2]
                for j in range(nj):
                    if j == 0:
                        k.w("tensor", acts, exp_done[i])
                        k.w("tensor", dve, mask_done_i)
                        if m >= 2:
                            k.w("tensor", dve, ou_evac[(i, m - 2)])
                        elif i >= 1:
                            k.w("tensor", dve, ou_evac[(i - 1, m + 6)])
                    k.mm(bank[:, 0:256],
                         BF[:, VT + j * 1024 + m * 128: VT + j * 1024 + (m + 1) * 128],
                         BF[:, EB + j * 256: EB + (j + 1) * 256],
                         start=(j == 0), stop=(j == nj - 1), sem=(pe if j == nj - 1 else None))
                avv = pe.v
                k.w("vector", pe, avv)
                idx = i * 8 + m
                if idx >= 2:
                    k.w("vector", st, ou_stored.get(idx - 2, 0))
                ob = FA[:, OUS + (m % 2) * 256: OUS + (m % 2) * 256 + 256]
                k.vop("tensor_copy", ob, bank[:, 0:256], sem=dve)
                ou_evac[(i, m)] = dve.v
                k.w("gpsimd", dve, dve.v)
                k.dma("gpsimd", dd["l0ou"][m * 128:(m + 1) * 128, i * 256:(i + 1) * 256], ob, st)
                ou_stored[idx] = st.v
            av_done[i] = pe.v
        k.w("vector", acts, exp_done[15])
        k.vop("reduce_sum", FA[:, ZCOL:ZCOL + 1], FA[:, ZC:ZC + 544], axis=AX.X, sem=dve)
        k.w("tensor", dve, dve.v)
        k.tr(self.ps[0][0:1, 0:128], FA[:, ZCOL:ZCOL + 1], self.identf[:, :], sem=pe)
        k.w("vector", pe, pe.v)
        k.vop("reduce_sum", FA[0:1, ZCOL + 2:ZCOL + 3], self.ps[0][0:1, 0:128],
              axis=AX.X, sem=dve)
        k.w("vector", dve, dve.v)
        k.vop("reciprocal", FA[0:1, ZCOL + 4:ZCOL + 5], FA[0:1, ZCOL + 2:ZCOL + 3], sem=dve)
        k.w("tensor", dve, dve.v)
        k.mm(self.ps[1][:, 0:1], self.onesf[0:1, :], FA[0:1, ZCOL + 4:ZCOL + 5],
             start=True, stop=True, sem=pe)
        k.w("vector", pe, pe.v)
        k.vop("tensor_copy", FA[:, RZC:RZC + 1], self.ps[1][:, 0:1], sem=dve)
        rz_done = dve.v

        XC, OC, OO = 2560, 3584, 4608
        res_done = {}
        out_done = {}
        k.w("sync", st, ou_stored[127])
        for sl in range(8):
            for h in range(8):
                idx = sl * 8 + h
                par = idx % 2
                xc = FA[:, XC + par * 512: XC + par * 512 + 512]
                oc = FA[:, OC + par * 512: OC + par * 512 + 512]
                oo = FA[:, OO + par * 512: OO + par * 512 + 512]
                if idx >= 2:
                    k.w("sync", dve, res_done[idx - 2])
                k.dma("sync", xc, dd[src][sl * 128:(sl + 1) * 128, h * 512:(h + 1) * 512], ld)
                k.dma("sync", oc, dd["l0ou"][sl * 128:(sl + 1) * 128, h * 512:(h + 1) * 512], ld)
                ldv = ld.v
                k.w("vector", ld, ldv)
                if idx == 0:
                    k.w("vector", dve, rz_done)
                if idx >= 2:
                    k.w("vector", st, out_done[idx - 2])
                k.vop("scalar_tensor_tensor", oo, oc, FA[:, RZC:RZC + 1], xc,
                      op0=ALU.mult, op1=ALU.add, sem=dve)
                res_done[idx] = dve.v
                k.w("gpsimd", dve, dve.v)
                k.dma("gpsimd", dd[dst][sl * 128:(sl + 1) * 128, h * 512:(h + 1) * 512], oo, st)
                out_done[idx] = st.v
        k.w("gpsimd", st, st.v)

    def st_l0f(self, k, s, src, dst):
        FA, BF = self.FA, self.BF
        dd = self.d
        ld, wt, pe, acts, dve, st = s["LD"], s["WT"], s["PE"], s["ACT"], s["DVE"], s["ST"]

        W1, W2 = 0, 32768
        XNC = [65536, 69632]
        HST = [73728, 74240]
        TMPA = [0, 512]
        SC = 1024
        for sl in range(4):
            k.dma("gpsimd", sb3(BF, W1 + sl * 8192, 2, 4096),
                  ap3(dd["w1T"], sl * 256, 2, 0, 4096), wt)
        for sl in range(4):
            k.dma("gpsimd", sb3(BF, W2 + sl * 8192, 2, 4096),
                  ap3(dd["w2T"], sl * 256, 2, 0, 4096), wt)
        k.dma("gpsimd", FA[:, SC:SC + 32], dd["s1c"][:, :], wt)
        k.dma("gpsimd", FA[:, SC + 32:SC + 64], dd["s2c"][:, :], wt)
        wt_a = wt.v
        evac1 = {}
        evac2 = {}
        hstore = {}
        xn_read = {}
        for c in range(8):
            par = c % 2
            if c >= 2:
                k.w("sync", pe, xn_read[c - 2])
            k.dma("sync", sb3(BF, XNC[par], 8, 512), ap3(dd["xnS"], 0, 8, c * 512, 512), ld)
            ldv = ld.v
            for oc in range(32):
                g = c * 32 + oc
                b1, b2 = self.ps[g % 2], self.ps[2 + g % 2]
                for sl in range(8):
                    if sl == 0:
                        k.w("tensor", ld, ldv)
                        if g == 0:
                            k.w("tensor", wt, wt_a)
                        if g >= 2:
                            k.w("tensor", acts, evac1[g - 2])
                            k.w("tensor", dve, evac2[g - 2])
                    k.mm(b1[:, :], BF[:, W1 + sl * 4096 + oc * 128: W1 + sl * 4096 + (oc + 1) * 128],
                         BF[:, XNC[par] + sl * 512: XNC[par] + (sl + 1) * 512],
                         start=(sl == 0), stop=(sl == 7))
                    k.mm(b2[:, :], BF[:, W2 + sl * 4096 + oc * 128: W2 + sl * 4096 + (oc + 1) * 128],
                         BF[:, XNC[par] + sl * 512: XNC[par] + (sl + 1) * 512],
                         start=(sl == 0), stop=(sl == 7), sem=(pe if sl == 7 else None))
                mmv = pe.v
                k.w("scalar", pe, mmv)
                if g >= 2:
                    k.w("scalar", dve, evac2[g - 2])
                tmp = FA[:, TMPA[g % 2]: TMPA[g % 2] + 512]
                k.act(tmp, b1[:, :], AF.Silu, scale=FA[:, SC + oc: SC + oc + 1], sem=acts)
                evac1[g] = acts.v
                k.w("vector", acts, acts.v)
                if g >= 2:
                    k.w("vector", st, hstore[g - 2])
                hs = BF[:, HST[g % 2]: HST[g % 2] + 512]
                k.vop("scalar_tensor_tensor", hs, b2[:, :], FA[:, SC + 32 + oc: SC + 33 + oc],
                      tmp, op0=ALU.mult, op1=ALU.mult, sem=dve)
                evac2[g] = dve.v
                k.w("gpsimd", dve, dve.v)
                k.dma("gpsimd", dd["hbA"][oc * 128:(oc + 1) * 128, c * 512:(c + 1) * 512], hs, st)
                hstore[g] = st.v
            xn_read[c] = pe.v
        self.fence(k, s)

        for (wname, scoff, hin, hout) in (("h1T", 0, "hbA", "hbB"), ("h2T", 32, "hbB", "hbA")):
            WQ = 0
            HC = [32768, 49152]
            GST = [65536, 66048]
            k.dma("gpsimd", FA[:, SC + 64 + scoff: SC + 96 + scoff],
                  dd["sh1c" if wname == "h1T" else "sh2c"][:, :], wt)
            evg = {}
            gst_d = {}
            hread = {}
            for qt in range(4):
                if qt >= 1:
                    k.w("gpsimd", pe, hread[(qt - 1, 7)])
                for sl in range(4):
                    k.dma("gpsimd", sb3(BF, WQ + sl * 8192, 8, 1024),
                          ap3(dd[wname], sl * 1024, 8, qt * 1024, 1024), wt)
                wt_q = wt.v
                for c in range(8):
                    par = c % 2
                    if (qt * 8 + c) >= 2:
                        pc = qt * 8 + c - 2
                        k.w("sync", pe, hread[(pc // 8, pc % 8)])
                    for half in range(4):
                        k.dma("sync", sb3(BF, HC[par] + half * 4096, 8, 512),
                              ap3(dd[hin], half * 1024, 8, c * 512, 512), ld)
                    ldv = ld.v
                    for oc in range(8):
                        g = (qt * 8 + c) * 8 + oc
                        bank = self.ps[g % 2]
                        for sl in range(32):
                            if sl == 0:
                                k.w("tensor", ld, ldv)
                                k.w("tensor", wt, wt_q)
                                if g >= 2:
                                    k.w("tensor", acts, evg[g - 2])
                            k.mm(bank[:, :],
                                 BF[:, WQ + sl * 1024 + oc * 128: WQ + sl * 1024 + (oc + 1) * 128],
                                 BF[:, HC[par] + sl * 512: HC[par] + (sl + 1) * 512],
                                 start=(sl == 0), stop=(sl == 31), sem=(pe if sl == 31 else None))
                        mmv = pe.v
                        k.w("scalar", pe, mmv)
                        if g >= 2:
                            k.w("scalar", st, gst_d[g - 2])
                        gs = BF[:, GST[g % 2]: GST[g % 2] + 512]
                        k.act(gs, bank[:, :], AF.Gelu,
                              scale=FA[:, SC + 64 + scoff + qt * 8 + oc: SC + 65 + scoff + qt * 8 + oc],
                              sem=acts)
                        evg[g] = acts.v
                        k.w("gpsimd", acts, acts.v)
                        k.dma("gpsimd",
                              dd[hout][qt * 1024 + oc * 128: qt * 1024 + (oc + 1) * 128,
                                       c * 512:(c + 1) * 512], gs, st)
                        gst_d[g] = st.v
                    hread[(qt, c)] = pe.v
            self.fence(k, s)

        W3 = 0
        HC = [32768, 40960]
        XC = [0, 2048]
        OST = [4352, 4608]
        S3 = 4864
        for sl in range(4):
            k.dma("gpsimd", sb3(BF, W3 + sl * 8192, 8, 1024),
                  ap3(dd["w3T"], sl * 1024, 8, 0, 1024), wt)
        k.dma("gpsimd", FA[:, S3:S3 + 8], dd["s3c"][:, :], wt)
        wt_d = wt.v
        evd = {}
        ost_d = {}
        hread = {}
        for c in range(16):
            par = c % 2
            if c >= 2:
                k.w("sync", pe, hread[c - 2])
                k.w("sync", dve, evd[(c - 2) * 8 + 7])
            for half in range(4):
                k.dma("sync", sb3(BF, HC[par] + half * 2048, 8, 256),
                      ap3(dd["hbA"], half * 1024, 8, c * 256, 256), ld)
            k.dma("sync", sb3(FA, XC[par], 8, 256), ap3(dd[src], 0, 8, c * 256, 256), ld)
            ldv = ld.v
            for m in range(8):
                g = c * 8 + m
                bank = self.ps[g % 2]
                for sl in range(32):
                    if sl == 0:
                        k.w("tensor", ld, ldv)
                        if g == 0:
                            k.w("tensor", wt, wt_d)
                        if g >= 2:
                            k.w("tensor", dve, evd[g - 2])
                    k.mm(bank[:, 0:256],
                         BF[:, W3 + sl * 1024 + m * 128: W3 + sl * 1024 + (m + 1) * 128],
                         BF[:, HC[par] + sl * 256: HC[par] + (sl + 1) * 256],
                         start=(sl == 0), stop=(sl == 31), sem=(pe if sl == 31 else None))
                mmv = pe.v
                k.w("vector", pe, mmv)
                k.w("vector", ld, ldv)
                if g >= 2:
                    k.w("vector", st, ost_d[g - 2])
                os_ = FA[:, OST[g % 2]: OST[g % 2] + 256]
                k.vop("scalar_tensor_tensor", os_, bank[:, 0:256], FA[:, S3 + m: S3 + m + 1],
                      FA[:, XC[par] + m * 256: XC[par] + (m + 1) * 256],
                      op0=ALU.mult, op1=ALU.add, sem=dve)
                evd[g] = dve.v
                k.w("gpsimd", dve, dve.v)
                k.dma("gpsimd", dd[dst][m * 128:(m + 1) * 128, c * 256:(c + 1) * 256], os_, st)
                ost_d[g] = st.v
            hread[c] = pe.v
        k.w("gpsimd", st, st.v)

    def st_l1a(self, k, s, src, dst):
        FA, BF = self.FA, self.BF
        dd = self.d
        ld, wt, pe, acts, dve, st = s["LD"], s["WT"], s["PE"], s["ACT"], s["DVE"], s["ST"]

        WQKV = 0
        XNC = [24576, 28672]
        V2 = 32768
        QST = [66560, 70656]
        KST = [74752, 78848]
        QN0 = 82944
        QSB, KSB = 0, 64
        SSC, DEN = 128, 136
        JNK = 160
        k.gop("memset", BF[:, V2:V2 + 33280], 1.0, sem=s["GP"])
        v2_ones = s["GP"].v
        for sl in range(4):
            k.dma("gpsimd", sb3(BF, WQKV + sl * 6144, 2, 3072),
                  ap3(dd["Wqkv"], sl * 256, 2, 0, 3072), wt)
        k.dma("gpsimd", FA[0:1, QSB:QSB + 64], dd["qs_row"][:, :], wt)
        k.dma("gpsimd", FA[0:1, KSB:KSB + 64], dd["ks_row"][:, :], wt)
        wt_a = wt.v
        k.w("tensor", wt, wt_a)
        k.mm(self.ps[4][:, 0:64], self.onesf[0:1, :], FA[0:1, QSB:QSB + 64],
             start=True, stop=True)
        k.mm(self.ps[5][:, 0:64], self.onesf[0:1, :], FA[0:1, KSB:KSB + 64],
             start=True, stop=True, sem=pe)
        k.w("vector", pe, pe.v)
        k.vop("tensor_copy", FA[:, QSB:QSB + 64], self.ps[4][:, 0:64])
        k.vop("tensor_copy", FA[:, KSB:KSB + 64], self.ps[5][:, 0:64], sem=dve)
        scb_done = dve.v

        ev = {}
        tr_done = {}
        qn_done = {}
        cp_done = {}
        xn_read = {}
        stq = {}
        last_tr = {}
        last_pb = {}
        for c in range(8):
            par = c % 2
            if c >= 2:
                k.w("sync", pe, xn_read[c - 2])
                k.w("vector", st, stq[c - 2])
            k.dma("sync", sb3(BF, XNC[par], 8, 512), ap3(dd["xnS"], 0, 8, c * 512, 512), ld)
            ldv = ld.v
            for sub in range(4):
                tile = c * 4 + sub
                for n in range(6):
                    g = (c * 4 + sub) * 6 + n
                    bank = self.ps[g % 4]
                    for sl in range(8):
                        if sl == 0:
                            k.w("tensor", ld, ldv)
                            if g == 0:
                                k.w("tensor", wt, wt_a)
                            if g >= 4:
                                k.w("tensor", dve, ev.get(g - 4, 0))
                                k.w("tensor", acts, ev.get(("a", g - 4), 0))
                        k.mm(bank[:, :],
                             BF[:, XNC[par] + sl * 512 + sub * 128: XNC[par] + sl * 512 + (sub + 1) * 128],
                             BF[:, WQKV + sl * 3072 + n * 512: WQKV + sl * 3072 + (n + 1) * 512],
                             start=(sl == 0), stop=(sl == 7), sem=(pe if sl == 7 else None))
                    mmv = pe.v
                    if n >= 4:
                        k.w("vector", pe, mmv)
                        k.w("vector", s["GP"], v2_ones)
                        slot0 = tile * 16 + (n - 4) * 8
                        vdst = BF[:, V2 + slot0 * 65: V2 + (slot0 + 8) * 65].rearrange(
                            "p (n c) -> p n c", c=65)[:, :, 0:64]
                        vsrc = bank[:, :].rearrange("p (n c) -> p n c", c=64)
                        k.vop("tensor_copy", vdst, vsrc, sem=dve)
                        ev[g] = dve.v
                        ev[("a", g)] = ev.get(("a", g - 1), 0)
                        continue
                    isq = n < 2
                    scoff = QSB if isq else KSB
                    sstg = QST[par] if isq else KST[par]
                    k.w("scalar", pe, mmv)
                    for h8 in range(8):
                        k.act(FA[:, JNK:JNK + 64], bank[:, h8 * 64:(h8 + 1) * 64],
                              AF.Square, accum=FA[:, SSC + h8:SSC + h8 + 1],
                              sem=(acts if h8 == 7 else None))
                    k.w("scalar", acts, acts.v)
                    k.act(FA[:, SSC:SSC + 8], FA[:, SSC:SSC + 8], AF.Sqrt, sem=acts)
                    sq_done = acts.v
                    ev[("a", g)] = acts.v
                    k.w("vector", acts, sq_done)
                    k.vop("tensor_scalar", FA[:, DEN:DEN + 8], FA[:, SSC:SSC + 8],
                          0.125, 1e-8, op0=ALU.mult, op1=ALU.add, sem=dve)
                    k.w("vector", dve, dve.v)
                    k.vop("reciprocal", FA[:, DEN:DEN + 8], FA[:, DEN:DEN + 8], sem=dve)
                    k.w("vector", dve, dve.v)
                    if c == 0 and sub == 0 and n == 0:
                        k.w("vector", dve, scb_done)
                    k.w("vector", pe, last_tr.get(0, 0))
                    for h8 in range(8):
                        k.vop("scalar_tensor_tensor", BF[:, QN0 + h8 * 64:QN0 + (h8 + 1) * 64],
                              bank[:, h8 * 64:(h8 + 1) * 64], FA[:, DEN + h8:DEN + h8 + 1],
                              FA[:, scoff:scoff + 64], op0=ALU.mult, op1=ALU.mult,
                              sem=(dve if h8 == 7 else None))
                    qn_batch = dve.v
                    k.w("tensor", dve, qn_batch)
                    k.w("tensor", dve, last_pb.get(g % 2, 0))
                    pbk = self.pb[g % 2]
                    for h8 in range(8):
                        k.tr(pbk[0:64, h8 * 128:(h8 + 1) * 128],
                             BF[:, QN0 + h8 * 64:QN0 + (h8 + 1) * 64],
                             self.identb[:, :], sem=(pe if h8 == 7 else None))
                    last_tr[0] = pe.v
                    k.w("vector", pe, last_tr[0])
                    pb3 = pbk[0:64, :].rearrange("p (n c) -> p n c", c=256)
                    base = sstg + (n % 2) * 4 * 512 + sub * 128
                    dste = BF[0:64, base:base + 4 * 512].rearrange("p (n c) -> p n c", c=512)[:, :, 0:128]
                    dsto = BF[64:128, base:base + 4 * 512].rearrange("p (n c) -> p n c", c=512)[:, :, 0:128]
                    k.vop("tensor_copy", dste, pb3[:, :, 0:128])
                    k.vop("tensor_copy", dsto, pb3[:, :, 128:256], sem=dve)
                    last_pb[g % 2] = dve.v
                    ev[g] = dve.v
            xn_read[c] = pe.v
            k.w("gpsimd", dve, ev[(c * 4 + 3) * 6 + 3])
            for p_ in range(8):
                k.dma("gpsimd", dd["l1qt"][:, p_ * 4096 + c * 512: p_ * 4096 + (c + 1) * 512],
                      BF[:, QST[par] + p_ * 512: QST[par] + (p_ + 1) * 512], st)
                k.dma("gpsimd", dd["l1kt"][:, p_ * 4096 + c * 512: p_ * 4096 + (c + 1) * 512],
                      BF[:, KST[par] + p_ * 512: KST[par] + (p_ + 1) * 512], st)
            stq[c] = st.v
            if c >= 1:
                pass
        self.fence(k, s)

        KT = 0
        EA = [66048, 67072]
        EB = [68096, 69120]
        QB = 70144
        OUS = 74240
        ZR = 200
        ZBS = 1224
        k.dma("sync", BF[:, KT:KT + 32768], dd["l1kt"][:, :], ld)
        kt_done = ld.v
        sblk = {}
        expd = {}
        oud = {}
        oust = {}
        avd = {}
        for i in range(8):
            if i >= 1:
                k.w("sync", pe, sblk[i - 1])
                k.w("vector", st, oust[i - 1])
            qsrc = dd["l1qt"].rearrange("p (n c) -> p n c", c=4096)[:, :, i * 512:(i + 1) * 512]
            k.dma("sync", sb3(BF, QB, 8, 512), qsrc, ld)
            ldq = ld.v
            for p_ in range(8):
                avd = {}
                expd_l = {}

                def emit_S(j, ldq=None):
                    psA = self.ps[2 * (j % 2)]
                    psB = self.ps[1 + 2 * (j % 2)]
                    if j == 0:
                        k.w("tensor", ld, ldq)
                        if i == 0 and p_ == 0:
                            k.w("tensor", ld, kt_done)
                        k.w("tensor", dve, oud.get((i, p_ - 1) if p_ >= 1 else (i - 1, 7), 0))
                    if j >= 2:
                        k.w("tensor", acts, expd[(i, p_, j - 2)])
                    elif (i, p_) != (0, 0):
                        pp = (i, p_ - 1) if p_ >= 1 else (i - 1, 7)
                        k.w("tensor", acts, expd[(pp[0], pp[1], 31)])
                    k.mm(psA[:, :], BF[0:64, KT + p_ * 4096 + j * 128: KT + p_ * 4096 + (j + 1) * 128],
                         BF[0:64, QB + p_ * 512: QB + (p_ + 1) * 512], start=True, stop=True)
                    k.mm(psB[:, :], BF[64:128, KT + p_ * 4096 + j * 128: KT + p_ * 4096 + (j + 1) * 128],
                         BF[64:128, QB + p_ * 512: QB + (p_ + 1) * 512], start=True, stop=True,
                         sem=pe, tp=(64, 0))
                    smm = pe.v
                    k.w("scalar", pe, smm)
                    if j >= 2:
                        k.w("scalar", pe, avd.get((j - 2, 1), 0))
                    ea = BF[:, EA[j % 2]:EA[j % 2] + 512]
                    eb = BF[:, EB[j % 2]:EB[j % 2] + 512]
                    psA2 = self.ps[2 * (j % 2)]
                    psB2 = self.ps[1 + 2 * (j % 2)]
                    k.act(ea, psA2[:, :], AF.Exp, scale=0.125, sem=acts)
                    k.act(eb, psB2[:, :], AF.Exp, scale=0.125, sem=acts)
                    expd[(i, p_, j)] = acts.v

                def emit_AV(j):
                    ea = BF[:, EA[j % 2]:EA[j % 2] + 512]
                    eb = BF[:, EB[j % 2]:EB[j % 2] + 512]
                    k.w("tensor", acts, expd[(i, p_, j)])
                    k.mm(self.ps[4][0:65, :], BF[:, V2 + (j * 16 + 2 * p_) * 65: V2 + (j * 16 + 2 * p_) * 65 + 65],
                         ea, start=(j == 0), stop=(j == 31))
                    k.mm(self.ps[5][0:65, :], BF[:, V2 + (j * 16 + 2 * p_ + 1) * 65: V2 + (j * 16 + 2 * p_ + 1) * 65 + 65],
                         eb, start=(j == 0), stop=(j == 31), sem=pe)
                    avd[(j, 1)] = pe.v

                for j in range(32):
                    emit_S(j, ldq)
                    if j >= 1:
                        emit_AV(j - 1)
                emit_AV(31)
                av_last = pe.v
                k.w("vector", pe, av_last)
                zra = FA[0:1, ZR:ZR + 512]
                zrb = FA[0:1, ZR + 512:ZR + 1024]
                k.vop("reciprocal", zra, self.ps[4][64:65, 0:512])
                k.vop("reciprocal", zrb, self.ps[5][64:65, 0:512], sem=dve)
                k.w("vector", dve, dve.v)
                zb_done_prev = 0
                k.w("tensor", dve, dve.v)
                k.w("tensor", acts, expd[(i, p_, 31)])
                k.mm(self.ps[0][0:64, :], self.onesf[0:1, 0:64], zra, start=True, stop=True)
                k.mm(self.ps[1][0:64, :], self.onesf[0:1, 0:64], zrb, start=True, stop=True, sem=pe)
                zbv = pe.v
                k.w("vector", pe, zbv)
                k.vop("tensor_copy", FA[:, ZBS:ZBS + 512][0:64, :], self.ps[0][0:64, :])
                k.vop("tensor_copy", FA[:, ZBS + 512:ZBS + 1024][0:64, :], self.ps[1][0:64, :], sem=dve)
                k.w("vector", dve, dve.v)
                k.vop("tensor_mul", BF[0:64, OUS + p_ * 512: OUS + (p_ + 1) * 512],
                      self.ps[4][0:64, 0:512], FA[0:64, ZBS:ZBS + 512])
                k.vop("tensor_mul", BF[64:128, OUS + p_ * 512: OUS + (p_ + 1) * 512],
                      self.ps[5][0:64, 0:512], FA[0:64, ZBS + 512:ZBS + 1024], sem=dve)
                oud[(i, p_)] = dve.v
            sblk[i] = pe.v
            k.w("gpsimd", dve, oud[(i, 7)])
            if i >= 1:
                pass
            for p_ in range(8):
                k.dma("gpsimd", dd["l1ou"][:, p_ * 4096 + i * 512: p_ * 4096 + (i + 1) * 512],
                      BF[:, OUS + p_ * 512: OUS + (p_ + 1) * 512], st)
            oust[i] = st.v
        self.fence(k, s)

        WP = 0
        OUC = [8192, 10240]
        XC = [0, 2048]
        OST = [4352, 4608]
        BPJ = 4864
        for sl in range(4):
            k.dma("gpsimd", sb3(BF, WP + sl * 2048, 2, 1024),
                  ap3(dd["Wproj"], sl * 256, 2, 0, 1024), wt)
        k.dma("gpsimd", FA[:, BPJ:BPJ + 8], dd["bprojc"][:, :], wt)
        wt_p = wt.v
        evp = {}
        ostd = {}
        ourd = {}
        for c in range(16):
            par = c % 2
            if c >= 2:
                k.w("sync", pe, ourd[c - 2])
                k.w("sync", dve, evp[(c - 2) * 8 + 7])
            for p_ in range(8):
                k.dma("sync", BF[:, OUC[par] + p_ * 256: OUC[par] + (p_ + 1) * 256],
                      dd["l1ou"][:, p_ * 4096 + c * 256: p_ * 4096 + (c + 1) * 256], ld)
            k.dma("sync", sb3(FA, XC[par], 8, 256), ap3(dd[src], 0, 8, c * 256, 256), ld)
            ldv = ld.v
            for m in range(8):
                g = c * 8 + m
                bank = self.ps[g % 2]
                for sl in range(8):
                    if sl == 0:
                        k.w("tensor", ld, ldv)
                        if g == 0:
                            k.w("tensor", wt, wt_p)
                        if g >= 2:
                            k.w("tensor", dve, evp[g - 2])
                    k.mm(bank[:, 0:256],
                         BF[:, WP + sl * 1024 + m * 128: WP + sl * 1024 + (m + 1) * 128],
                         BF[:, OUC[par] + sl * 256: OUC[par] + (sl + 1) * 256],
                         start=(sl == 0), stop=(sl == 7), sem=(pe if sl == 7 else None))
                mmv = pe.v
                k.w("vector", pe, mmv)
                k.w("vector", ld, ldv)
                if g >= 2:
                    k.w("vector", st, ostd[g - 2])
                os_ = FA[:, OST[g % 2]: OST[g % 2] + 256]
                k.vop("scalar_tensor_tensor", os_, bank[:, 0:256], FA[:, BPJ + m: BPJ + m + 1],
                      FA[:, XC[par] + m * 256: XC[par] + (m + 1) * 256],
                      op0=ALU.add, op1=ALU.add, sem=dve)
                evp[g] = dve.v
                k.w("gpsimd", dve, dve.v)
                k.dma("gpsimd", dd[dst][m * 128:(m + 1) * 128, c * 256:(c + 1) * 256], os_, st)
                ostd[g] = st.v
            ourd[c] = pe.v
        k.w("gpsimd", st, st.v)

    def st_moe(self, k, s, src, dst):
        FA, BF = self.FA, self.BF
        dd = self.d
        ld, wt, pe, acts, dve, st = s["LD"], s["WT"], s["PE"], s["ACT"], s["DVE"], s["ST"]

        def vsync():
            k.w("vector", dve, dve.v)

        GTC = 0

        XNT = [0, 1024]
        WGC = 2048
        SEL1, SEL2 = 2112, 2368
        G1C, G2C = 2624, 2656
        RW = 2688
        NM, SE, M1, I1, M2, RD = RW, RW + 1, RW + 2, RW + 3, RW + 4, RW + 5
        PRB, T1, CAND = RW + 16, RW + 24, RW + 32
        SELT = RW + 40
        CC = RW + 64
        CCO = CC + 128
        OF1, OF2 = CCO + 140, CCO + 150
        OF1B, OF2B = CCO + 160, CCO + 168
        B1R, B2R = CCO + 180, CCO + 188
        RNK, KP = CCO + 196, CCO + 198
        k.dma("gpsimd", sb3(FA, WGC, 8, 8), ap3(dd["wg"], 0, 8, 0, 8), wt)
        k.dma("gpsimd", self.ohot_sb[:, :], dd["ohot"][:, :], wt)
        k.dma("gpsimd", self.fc2b_sb[:, :], dd["fc2b"][:, :], wt)
        wt_r = wt.v
        ev_t = {}
        for t in range(32):
            par = t % 2
            if t >= 2:
                k.w("sync", dve, ev_t[t - 2])
            k.dma("sync", sb3(FA, XNT[par], 8, 128), ap3(dd["xn3f"], 0, 8, t * 128, 128), ld)
            ldv = ld.v
            bank = self.ps[par]
            for sl in range(8):
                if sl == 0:
                    k.w("tensor", ld, ldv)
                    k.w("tensor", wt, wt_r)
                    if t >= 2:
                        k.w("tensor", dve, ev_t[t - 2])
                k.mm(bank[:, 0:8], FA[:, XNT[par] + sl * 128: XNT[par] + (sl + 1) * 128],
                     FA[:, WGC + sl * 8: WGC + (sl + 1) * 8],
                     start=(sl == 0), stop=(sl == 7), sem=(pe if sl == 7 else None))
            k.w("vector", pe, pe.v)
            nm = FA[:, NM:NM + 1]
            k.vop("reduce_max", nm, bank[:, 0:8], axis=AX.X, negate=True, sem=dve)
            k.w("scalar", dve, dve.v)
            k.act(FA[:, PRB:PRB + 8], bank[:, 0:8], AF.Exp, bias=nm,
                  accum=FA[:, SE:SE + 1], sem=acts)
            k.w("vector", acts, acts.v)
            k.vop("reciprocal", FA[:, SE:SE + 1], FA[:, SE:SE + 1], sem=dve)
            vsync()
            k.vop("tensor_scalar_mul", FA[:, PRB:PRB + 8], FA[:, PRB:PRB + 8],
                  FA[:, SE:SE + 1], sem=dve)
            vsync()
            sel1 = FA[:, SEL1 + t * 8: SEL1 + (t + 1) * 8]
            sel2 = FA[:, SEL2 + t * 8: SEL2 + (t + 1) * 8]
            for (selx, mx, gcol) in ((sel1, M1, G1C), (sel2, M2, G2C)):
                k.vop("reduce_max", FA[:, mx:mx + 1], FA[:, PRB:PRB + 8], axis=AX.X, sem=dve)
                vsync()
                k.vop("tensor_scalar", FA[:, T1:T1 + 8], FA[:, PRB:PRB + 8],
                      FA[:, mx:mx + 1], None, op0=ALU.is_equal, sem=dve)
                vsync()
                k.vop("tensor_mul", FA[:, CAND:CAND + 8], FA[:, T1:T1 + 8],
                      self.eidxm8[:, :], sem=dve)
                vsync()
                k.vop("tensor_reduce", FA[:, I1:I1 + 1], FA[:, CAND:CAND + 8],
                      axis=AX.X, op=ALU.min, sem=dve)
                vsync()
                k.vop("tensor_scalar", selx, self.eidxm8[:, :], FA[:, I1:I1 + 1],
                      None, op0=ALU.is_equal, sem=dve)
                vsync()
                if mx == M1:
                    k.vop("tensor_scalar", FA[:, SELT:SELT + 8], selx, -1.0, 1.0,
                          op0=ALU.mult, op1=ALU.add, sem=dve)
                    vsync()
                    k.vop("tensor_mul", FA[:, PRB:PRB + 8], FA[:, PRB:PRB + 8],
                          FA[:, SELT:SELT + 8], sem=dve)
                    vsync()
            k.vop("tensor_add", FA[:, RD:RD + 1], FA[:, M1:M1 + 1], FA[:, M2:M2 + 1], sem=dve)
            vsync()
            k.vop("tensor_scalar_max", FA[:, RD:RD + 1], FA[:, RD:RD + 1], MOE_EPS, sem=dve)
            vsync()
            k.vop("reciprocal", FA[:, RD:RD + 1], FA[:, RD:RD + 1], sem=dve)
            vsync()
            k.vop("tensor_mul", FA[:, G1C + t:G1C + t + 1], FA[:, M1:M1 + 1],
                  FA[:, RD:RD + 1], sem=dve)
            k.vop("tensor_mul", FA[:, G2C + t:G2C + t + 1], FA[:, M2:M2 + 1],
                  FA[:, RD:RD + 1], sem=dve)
            ev_t[t] = dve.v
            k.w("tensor", dve, ev_t[t])
            k.mm(self.ps[4][0:1, 0:8], self.onesf[:, 0:1], sel1,
                 start=(t == 0), stop=(t == 31))
            k.mm(self.ps[5][0:1, 0:8], self.onesf[:, 0:1], sel2,
                 start=(t == 0), stop=(t == 31), sem=(pe if t == 31 else None))
        cnt_pe = pe.v
        k.vop("memset", FA[0:1, CC:CC + 128], 0.0, sem=dve)
        vsync()
        k.w("vector", pe, cnt_pe)
        k.vop("tensor_copy", FA[0:1, CC:CC + 8], self.ps[4][0:1, 0:8])
        k.vop("tensor_copy", FA[0:1, CC + 8:CC + 16], self.ps[5][0:1, 0:8], sem=dve)
        k.w("gpsimd", dve, dve.v)
        k.dma("gpsimd", dd["ccin"][:, :], FA[0:1, CC:CC + 128], s["GP"])
        k.w("gpsimd", s["GP"], s["GP"].v)
        k.gop("collective_compute", "AllGather", ALU.bypass,
              replica_groups=[[0, 1, 2, 3, 4, 5, 6, 7]],
              ins=[dd["ccin"].tensor.ap().opt()],
              outs=[dd["ccout"].tensor.ap().opt()], sem=s["GP"], inc=1)
        k.w("gpsimd", s["GP"], s["GP"].v)
        k.dma("gpsimd", FA[0:8, CCO:CCO + 128], dd["ccout"][:, :], s["GP"])
        k.dma("gpsimd", FA[0:8, CCO + 128:CCO + 129], dd["coremask"][:, :], s["GP"])
        cm_done = s["GP"].v
        k.w("tensor", s["GP"], cm_done)
        k.mm(self.ps[2][0:1, 0:16], FA[0:8, CCO + 128:CCO + 129], FA[0:8, CCO:CCO + 16],
             start=True, stop=True)
        k.mm(self.ps[3][0:1, 0:16], self.onesf[0:8, 0:1], FA[0:8, CCO:CCO + 16],
             start=True, stop=True, sem=pe)
        pfx_pe = pe.v
        k.w("vector", pe, pfx_pe)
        k.vop("tensor_scalar_add", FA[0:1, OF1:OF1 + 8], self.ps[2][0:1, 0:8], -1.0)
        k.vop("tensor_copy", FA[0:1, OF2:OF2 + 8], self.ps[3][0:1, 0:8], sem=dve)
        vsync()
        k.vop("scalar_tensor_tensor", FA[0:1, OF2:OF2 + 8], self.ps[2][0:1, 8:16], -1.0,
              FA[0:1, OF2:OF2 + 8], op0=ALU.add, op1=ALU.add, sem=dve)
        vsync()
        k.w("tensor", dve, dve.v)
        k.mm(self.ps[2][:, 32:40], self.onesf[0:1, :], FA[0:1, OF1:OF1 + 8],
             start=True, stop=True)
        k.mm(self.ps[3][:, 32:40], self.onesf[0:1, :], FA[0:1, OF2:OF2 + 8],
             start=True, stop=True, sem=pe)
        k.w("vector", pe, pe.v)
        k.vop("tensor_copy", FA[:, OF1B:OF1B + 8], self.ps[2][:, 32:40])
        k.vop("tensor_copy", FA[:, OF2B:OF2B + 8], self.ps[3][:, 32:40], sem=dve)
        vsync()
        k.vop("memset", FA[0:1, B1R:B1R + 8], 0.0)
        k.vop("memset", FA[0:1, B2R:B2R + 8], 0.0, sem=dve)
        vsync()
        gt_done = {}
        base_upd = {0: 0, 1: 0}
        for t in range(32):
            gtile = FA[:, CAND:CAND + 8]
            for (j, SEL, BR, OFB) in ((0, SEL1, B1R, OF1B), (1, SEL2, B2R, OF2B)):
                selx = FA[:, SEL + t * 8: SEL + (t + 1) * 8]
                gcol = (G1C if j == 0 else G2C) + t
                bank = self.ps[j]
                bsum = self.ps[2 + j]
                k.w("tensor", dve, dve.v)
                k.mm(bank[:, 0:8], self.ltri[:, :], selx, start=True, stop=False)
                k.mm(bank[:, 0:8], self.onesf[0:1, :], FA[0:1, BR:BR + 8],
                     start=False, stop=True)
                k.mm(bsum[0:1, 0:8], self.onesf[:, 0:1], selx, start=True, stop=True,
                     sem=pe)
                cum_pe = pe.v
                k.w("vector", pe, cum_pe)
                k.vop("tensor_add", FA[0:1, BR:BR + 8], FA[0:1, BR:BR + 8],
                      bsum[0:1, 0:8], sem=dve)
                base_upd[j] = dve.v
                vsync()
                k.vop("tensor_add", FA[:, T1:T1 + 8], bank[:, 0:8], FA[:, OFB:OFB + 8], sem=dve)
                vsync()
                k.vop("tensor_mul", FA[:, T1:T1 + 8], FA[:, T1:T1 + 8], selx, sem=dve)
                vsync()
                k.vop("reduce_sum", FA[:, RNK:RNK + 1], FA[:, T1:T1 + 8], axis=AX.X, sem=dve)
                vsync()
                k.vop("tensor_single_scalar", FA[:, KP:KP + 1], FA[:, RNK:RNK + 1],
                      float(CAP), op=ALU.is_lt, sem=dve)
                vsync()
                k.vop("tensor_mul", FA[:, KP:KP + 1], FA[:, KP:KP + 1],
                      FA[:, gcol:gcol + 1], sem=dve)
                vsync()
                if j == 0:
                    k.vop("tensor_scalar_mul", gtile, selx, FA[:, KP:KP + 1], sem=dve)
                else:
                    k.vop("scalar_tensor_tensor", gtile, selx, FA[:, KP:KP + 1],
                          gtile, op0=ALU.mult, op1=ALU.add, sem=dve)
                vsync()
            k.w("tensor", dve, dve.v)
            if t >= 2:
                k.w("tensor", dve, gt_done[t - 2])
            k.tr(self.ps[4 + t % 2][0:8, 0:128], gtile, self.identf[:, :], sem=pe)
            k.w("vector", pe, pe.v)
            k.vop("tensor_copy", BF[0:8, GTC + t * 128: GTC + (t + 1) * 128],
                  self.ps[4 + t % 2][0:8, 0:128], sem=dve)
            gt_done[t] = dve.v
        k.w("gpsimd", dve, gt_done[31])
        k.dma("gpsimd", dd["gtd"][:, :], BF[0:8, GTC:GTC + T], st)
        self.fence(k, s)

        WA, WB = 0, 32768
        XNC = [65536, 69632]
        HCB = [73728, 81920]
        HST = [3072, 3584]
        YST = [4096, 4352]
        GB = [0, 512]
        F1B = 2048
        wa_done = [0] * NE
        wb_done = [0] * NE
        pha_done = [0] * NE
        phb_done = [0] * NE
        hA_st = [0] * NE
        hB_ld = [0] * NE
        pha_evac = [0] * NE
        phb_evac = [0] * NE
        gb_used = {}
        gt_read = {}
        hstore = {}
        ystore = {}
        for e in range(NE):
            hb = dd["hbA"] if e % 2 == 0 else dd["hbB"]
            if e >= 1:
                k.w("gpsimd", pe, pha_done[e - 1])
            for q in range(4):
                k.dma("gpsimd", sb3(BF, WA + q * 8192, 2, 4096),
                      ap3(dd["fc1T"], e * 1024 + q * 256, 2, 0, 4096), wt)
            k.dma("gpsimd", FA[:, F1B + (e % 2) * 32:F1B + (e % 2) * 32 + 32],
                  dd["fc1b"][e * 128:(e + 1) * 128, :], wt)
            wa_done[e] = wt.v
            ga_done = {}
            xn_read = {}
            for c in range(8):
                par = c % 2
                if c >= 2:
                    k.w("sync", pe, xn_read[c - 2])
                k.dma("sync", sb3(BF, XNC[par], 8, 512), ap3(dd["xnS"], 0, 8, c * 512, 512), ld)
                ldv = ld.v
                for oc in range(32):
                    g = c * 32 + oc
                    bank = self.ps[g % 2]
                    for sl in range(8):
                        if sl == 0:
                            k.w("tensor", ld, ldv)
                            if g == 0:
                                k.w("tensor", wt, wa_done[e])
                                if e >= 1:
                                    k.w("tensor", dve, phb_evac[e - 1])
                            if g >= 2:
                                k.w("tensor", acts, ga_done[g - 2])
                        k.mm(bank[:, :],
                             BF[:, WA + sl * 4096 + oc * 128: WA + sl * 4096 + (oc + 1) * 128],
                             BF[:, XNC[par] + sl * 512: XNC[par] + (sl + 1) * 512],
                             start=(sl == 0), stop=(sl == 7), sem=(pe if sl == 7 else None))
                    mmv = pe.v
                    k.w("scalar", pe, mmv)
                    gk = e * 256 + g
                    if gk >= 2:
                        k.w("scalar", st, hstore.get(gk - 2, 0))
                    hs = FA[:, HST[g % 2]: HST[g % 2] + 512]
                    k.act(hs, bank[:, :], AF.Silu,
                          bias=FA[:, F1B + (e % 2) * 32 + oc:F1B + (e % 2) * 32 + oc + 1],
                          sem=acts)
                    ga_done[g] = acts.v
                    k.w("gpsimd", acts, acts.v)
                    if g == 0 and e >= 2:
                        k.w("gpsimd", ld, hB_ld[e - 2])
                    k.dma("gpsimd", hb[oc * 128:(oc + 1) * 128, c * 512:(c + 1) * 512],
                          hs, st)
                    hstore[gk] = st.v
                xn_read[c] = pe.v
            pha_done[e] = pe.v
            pha_evac[e] = acts.v
            hA_st[e] = st.v
            if e >= 1:
                k.w("gpsimd", pe, phb_done[e - 1])
            for q in range(4):
                k.dma("gpsimd", sb3(BF, WB + q * 8192, 8, 1024),
                      ap3(dd["fc2"], e * 4096 + q * 1024, 8, 0, 1024), wt)
            wb_done[e] = wt.v
            yb_done = {}
            hread = {}
            gb_d = {}
            for c in range(16):
                par = c % 2
                if c >= 2:
                    k.w("sync", pe, hread[c - 2])
                k.w("sync", st, hA_st[e])
                for q in range(4):
                    k.dma("sync", sb3(BF, HCB[par] + q * 2048, 8, 256),
                          ap3(hb, q * 1024, 8, c * 256, 256), ld)
                ldv = ld.v
                if c % 2 == 0:
                    cc2 = c // 2
                    gidx = e * 8 + cc2
                    if gidx >= 1:
                        k.w("sync", pe, gt_read.get(gidx - 1, 0))
                    k.dma("sync", self.gt_sb[:, :], dd["gtd"][:, cc2 * 512:(cc2 + 1) * 512], ld)
                    gt_ldv = ld.v
                    k.w("tensor", wt, wb_done[e])
                    k.w("tensor", ld, gt_ldv)
                    if gidx >= 2:
                        k.w("tensor", dve, gb_used.get(gidx - 2, 0))
                    k.mm(self.ps[4 + cc2 % 2][:, :], self.ohot_sb[0:8, e * 128:(e + 1) * 128],
                         self.gt_sb[0:8, :], start=True, stop=True, sem=pe)
                    gt_read[gidx] = pe.v
                    gbv = pe.v
                    k.w("vector", pe, gbv)
                    gb = FA[:, GB[cc2 % 2]:GB[cc2 % 2] + 512]
                    k.vop("tensor_copy", gb, self.ps[4 + cc2 % 2][:, :], sem=dve)
                    gb_d[cc2] = dve.v
                for m in range(8):
                    g = c * 8 + m
                    bank = self.ps[g % 2]
                    for sl in range(32):
                        if sl == 0:
                            k.w("tensor", ld, ldv)
                            k.w("tensor", wt, wb_done[e])
                            if g < 2:
                                k.w("tensor", acts, pha_evac[e])
                            if g >= 2:
                                k.w("tensor", dve, yb_done[g - 2])
                        k.mm(bank[:, 0:256],
                             BF[:, WB + sl * 1024 + m * 128: WB + sl * 1024 + (m + 1) * 128],
                             BF[:, HCB[par] + sl * 256: HCB[par] + (sl + 1) * 256],
                             start=(sl == 0), stop=(sl == 31), sem=(pe if sl == 31 else None))
                    mmv = pe.v
                    k.w("vector", pe, mmv)
                    if m == 0:
                        k.w("vector", dve, gb_d[c // 2])
                    gk = e * 128 + g
                    if gk >= 2:
                        k.w("vector", st, ystore.get(gk - 2, 0))
                    ys = FA[:, YST[g % 2]: YST[g % 2] + 256]
                    k.vop("tensor_mul", ys, bank[:, 0:256],
                          FA[:, GB[(c // 2) % 2] + (c % 2) * 256: GB[(c // 2) % 2] + (c % 2) * 256 + 256],
                          sem=dve)
                    yb_done[g] = dve.v
                    gb_used[e * 8 + c // 2] = dve.v
                    k.w("gpsimd", dve, dve.v)
                    k.dma("gpsimd",
                          dd["ye"][e * 128:(e + 1) * 128,
                                   m * 4096 + c * 256: m * 4096 + (c + 1) * 256], ys, st)
                    ystore[gk] = st.v
                hread[c] = pe.v
            hB_ld[e] = ld.v
            phb_done[e] = pe.v
        self.fence(k, s)

        CH = 256
        YEC = [0, 16384]
        RES, ACC, XNF = 0, 2048, 3072
        X3C, OSG, SQT, RWS = 3584, 4096, 4608, 5120
        k.dma("gpsimd", FA[:, RWS + 768:RWS + 776], dd["gammac"][:, :], wt)
        k.dma("gpsimd", FA[:, RWS + 776:RWS + 784], dd["betac"][:, :], wt)
        wt_f = wt.v
        GAM = RWS + 768
        res_done = {}
        outst = {}
        sqd = {}
        gt_read3 = {}
        for c in range(16):
            par = c % 2
            if c >= 2:
                k.w("sync", dve, res_done.get(("acc", (c - 2) * 8 + 7), 0))
            if c >= 1:
                k.w("sync", pe, gt_read3.get(c - 1, 0))
            k.dma("sync", self.gt_sb[:, 0:256], dd["gtd"][:, c * 256:(c + 1) * 256], ld)
            gt3_ldv = ld.v
            for e in range(NE):
                ysrc = dd["ye"][e * 128:(e + 1) * 128, :].rearrange(
                    "p (n c2) -> p n c2", c2=4096)[:, :, c * 256:(c + 1) * 256]
                k.dma("sync", sb3(BF, YEC[par] + e * 2048, 8, 256), ysrc, ld)
            ye_ldv = ld.v
            for m in range(8):
                g = c * 8 + m
                if g >= 2:
                    k.w("sync", dve, res_done.get(("acc", g - 2), 0))
                xnb = FA[:, XNF + (m % 2) * 256: XNF + (m % 2) * 256 + 256]
                k.dma("sync", xnb,
                      dd["xn3f"][m * 128:(m + 1) * 128, c * 256:(c + 1) * 256], ld)
                ldv = ld.v
                if g >= 2:
                    k.w("tensor", dve, res_done.get(("bias", g - 2), 0))
                    if m < 2 and c >= 1:
                        k.w("tensor", dve, res_done.get(("x3", c * 8 - 1), 0))
                if m == 0:
                    k.w("tensor", ld, gt3_ldv)
                k.mm(self.ps[4 + g % 2][:, 0:256],
                     self.fc2b_sb[0:8, m * 128:(m + 1) * 128],
                     self.gt_sb[0:8, 0:256], start=True, stop=True, sem=pe)
                biasv = pe.v
                gt_read3[c] = pe.v
                k.w("vector", ld, ldv)
                if m == 0:
                    k.w("vector", ld, ye_ldv)
                ye_at = lambda e_: YEC[par] + e_ * 2048 + m * 256
                a_ = FA[:, ACC:ACC + 256]
                b_ = FA[:, ACC + 256:ACC + 512]
                c_ = FA[:, ACC + 512:ACC + 768]
                d_ = FA[:, ACC + 768:ACC + 1024]
                k.vop("tensor_add", a_, BF[:, ye_at(0):ye_at(0) + 256], BF[:, ye_at(1):ye_at(1) + 256])
                k.vop("tensor_add", b_, BF[:, ye_at(2):ye_at(2) + 256], BF[:, ye_at(3):ye_at(3) + 256])
                k.vop("tensor_add", c_, BF[:, ye_at(4):ye_at(4) + 256], BF[:, ye_at(5):ye_at(5) + 256])
                k.vop("tensor_add", d_, BF[:, ye_at(6):ye_at(6) + 256], BF[:, ye_at(7):ye_at(7) + 256])
                k.vop("tensor_add", a_, a_, b_)
                k.vop("tensor_add", c_, c_, d_)
                k.vop("tensor_add", a_, a_, c_)
                k.w("vector", pe, biasv)
                k.vop("tensor_add", a_, a_, self.ps[4 + g % 2][:, 0:256], sem=dve)
                res_done[("bias", g)] = dve.v
                rm = FA[:, RES + m * 256: RES + (m + 1) * 256]
                k.vop("tensor_add", rm, a_, xnb, sem=dve)
                res_done[("acc", g)] = dve.v
            res_v = dve.v
            bsum, bsq = self.ps[par], self.ps[2 + par]
            for m in range(8):
                if m == 0:
                    k.w("scalar", dve, res_v)
                sqb = FA[:, SQT + (m % 2) * 256: SQT + (m % 2) * 256 + 256]
                if (c, m - 2) in sqd:
                    k.w("scalar", pe, sqd[(c, m - 2)])
                elif c >= 1 and (c - 1, m + 6) in sqd:
                    k.w("scalar", pe, sqd[(c - 1, m + 6)])
                k.act(sqb, FA[:, RES + m * 256: RES + (m + 1) * 256], AF.Square, sem=acts)
                sqv = acts.v
                if m == 0:
                    k.w("tensor", dve, res_v)
                    if c >= 2:
                        k.w("tensor", dve, res_done[("stats", c - 2)])
                k.mm(bsum[0:1, 256:256 + CH], self.onesf[:, 0:1],
                     FA[:, RES + m * 256: RES + (m + 1) * 256], start=(m == 0), stop=(m == 7))
                k.w("tensor", acts, sqv)
                k.mm(bsq[0:1, 256:256 + CH], self.onesf[:, 0:1], sqb,
                     start=(m == 0), stop=(m == 7), sem=pe)
                sqd[(c, m)] = pe.v
            stat_pe = pe.v
            mu = FA[0:1, RWS:RWS + CH]
            v1 = FA[0:1, RWS + 256:RWS + 256 + CH]
            rs = FA[0:1, RWS + 512:RWS + 512 + CH]
            k.w("vector", pe, stat_pe)
            if c >= 2:
                k.w("vector", pe, res_done[("bcast", c - 2)])
            k.vop("tensor_scalar_mul", mu, bsum[0:1, 256:256 + CH], 1.0 / 1024.0, sem=dve)
            vsync()
            k.vop("tensor_mul", v1, mu, mu, sem=dve)
            vsync()
            k.vop("scalar_tensor_tensor", v1, bsq[0:1, 256:256 + CH], 1.0 / 1024.0, v1,
                  op0=ALU.mult, op1=ALU.subtract, sem=dve)
            k.w("scalar", dve, dve.v)
            k.act(v1, v1, AF.Sqrt, bias=self.smallf[0:1, 0:1], sem=acts)
            k.w("vector", acts, acts.v)
            k.vop("reciprocal", rs, v1, sem=dve)
            res_done[("stats", c)] = dve.v
            k.w("tensor", dve, res_done[("stats", c)])
            if c >= 2:
                k.w("tensor", dve, res_done[("x3", (c - 1) * 8 + 7)])
            k.mm(self.ps[4][:, 256:256 + CH], self.onesf[0:1, :], mu, start=True, stop=True)
            k.mm(self.ps[5][:, 256:256 + CH], self.onesf[0:1, :], rs, start=True, stop=True,
                 sem=pe)
            bc_pe = pe.v
            res_done[("bcast", c)] = bc_pe
            k.w("vector", pe, bc_pe)
            for m in range(8):
                g = c * 8 + m
                x3b = FA[:, X3C + (m % 2) * 256: X3C + (m % 2) * 256 + 256]
                if g >= 2:
                    k.w("sync", dve, res_done.get(("x3", g - 2), 0))
                k.dma("sync", x3b, dd[src][m * 128:(m + 1) * 128, c * 256:(c + 1) * 256], ld)
                ldx = ld.v
                rm = FA[:, RES + m * 256: RES + (m + 1) * 256]
                tm = FA[:, ACC:ACC + 256]
                t2 = FA[:, ACC + 256:ACC + 512]
                k.vop("tensor_sub", tm, rm, self.ps[4][:, 256:256 + CH])
                k.vop("tensor_mul", t2, tm, self.ps[5][:, 256:256 + CH])
                k.vop("tensor_scalar", tm, t2, FA[:, GAM + m:GAM + m + 1],
                      FA[:, GAM + 8 + m:GAM + 8 + m + 1], op0=ALU.mult, op1=ALU.add)
                k.w("vector", ld, ldx)
                k.w("vector", wt, wt_f)
                if g >= 2:
                    k.w("vector", st, outst.get(g - 2, 0))
                osg = FA[:, OSG + (m % 2) * 256: OSG + (m % 2) * 256 + 256]
                k.vop("tensor_add", osg, tm, x3b, sem=dve)
                res_done[("x3", g)] = dve.v
                k.w("gpsimd", dve, dve.v)
                k.dma("gpsimd", dd[dst][m * 128:(m + 1) * 128, c * 256:(c + 1) * 256], osg, st)
                outst[g] = st.v
        k.w("gpsimd", st, st.v)

    def build_in_maps(self, w, percore):
        maps = []
        for b in range(NCORES):
            m = {}
            for nm, sh, dty in WSPECS:
                if nm == "coremask":
                    m[nm] = percore[b]["coremask"]
                else:
                    m[nm] = w[nm]
            m["xT"] = percore[b]["xT"]
            maps.append(m)
        return maps


def run_stages(stages, inputs, out_name="outT", dbg_out=(), raw=False):
    w, percore = prep_host(inputs)
    p = Prog(stages, dbg_out=dbg_out)
    maps = p.build_in_maps(w, percore)
    res = run_bass_kernel_spmd(p.nc, maps, core_ids=list(range(NCORES)))
    if raw:
        return res
    return np.stack([np.asarray(r[out_name], dtype=np.float32).T for r in res.results])


def kernel(**inputs):
    stages = [
        ("ln", "xT", "xnS"),
        ("l0a", "xT", "x1"),
        ("ln", "x1", "xnS"),
        ("l0f", "x1", "x2"),
        ("ln", "x2", "xnS"),
        ("l1a", "x2", "x3"),
        ("ln", "x3", "xnS", "xn3f"),
        ("moe", "x3", "outT"),
    ]
    return run_stages(stages, inputs)
